# version 02 (final)
# baseline (speedup 1.0000x reference)
import math
import numpy as np
import concourse.bass as bass
import concourse.mybir as mybir
from concourse.bass_utils import run_bass_kernel_spmd

F32 = mybir.dt.float32
BF16 = mybir.dt.bfloat16
U8 = mybir.dt.uint8
I32 = mybir.dt.int32
AF = mybir.ActivationFunctionType
ALU = mybir.AluOpType
AX = mybir.AxisListType

NCORE = 8
B = 8
S = 2048
D = 1024
NT = S // 128
ND = D // 128
RG = [[0, 1, 2, 3, 4, 5, 6, 7]]

N_SUB_TOTAL = 16
DBG = False
ONLY = None


def _prep_consts():
    c = {}
    c["ident"] = np.eye(128, dtype=np.float32)
    jj = np.arange(128)[:, None]
    ii = np.arange(256)[None, :]
    p0 = (jj <= ii).astype(np.float32)
    p1 = (jj + 128 <= ii).astype(np.float32)
    c["mtri"] = np.concatenate([p0, p1], axis=1)
    jj2 = np.arange(128)[:, None]
    ii2 = np.arange(128)[None, :]
    c["ctri"] = (jj2 <= ii2).astype(np.float32)
    half = 32
    inv_freq = (10000.0 ** (-np.arange(half, dtype=np.float64) / half))
    pos = np.arange(S, dtype=np.float64)
    ang = pos[:, None] * inv_freq[None, :]
    c["rope_neox"] = np.concatenate([np.cos(ang), np.sin(ang)], axis=1).astype(np.float32)
    inv2 = 1.0 / (10000.0 ** (np.arange(0, 64, 2, dtype=np.float64) / 64.0))
    ang2 = pos[:, None] * inv2[None, :]
    c["rope_il"] = np.concatenate([np.cos(ang2), np.sin(ang2)], axis=1).astype(np.float32)
    jj3 = np.arange(128)[:, None]
    ii3 = np.arange(512)[None, :]
    pats = []
    for p in range(4):
        pats.append(np.where(jj3 + p * 128 <= ii3, 0.0, -1e30).astype(np.float32))
    for p in range(4):
        pats.append(np.zeros((128, 512), np.float32))
    c["cm_causal"] = np.concatenate(pats, axis=1)[:, 0:4096]
    wpats = []
    for delta in (-256, -128, 0, 128, 256, 384, 512, 640):
        m = np.where(np.abs(jj3 + delta - ii3) <= 256, 0.0, -1e30).astype(np.float32)
        wpats.append(m)
    c["cm_win"] = np.concatenate(wpats, axis=1)
    import math as _math
    posi = np.arange(S, dtype=np.float64)[:, None]
    divi = np.exp(np.arange(0, 32, 2, dtype=np.float64) * (-_math.log(10000.0) / 32))
    angp = posi * divi
    pe = np.zeros((S, 32), np.float64)
    pe[:, 0::2] = np.sin(angp)
    pe[:, 1::2] = np.cos(angp)
    c["pe_T"] = np.ascontiguousarray(pe.T.astype(np.float32))
    return c


def _prep_core_consts(core):
    return {}


def _prep_weights(inp):
    w = {}
    T = lambda a: np.ascontiguousarray(np.asarray(a).T)
    w["w_l0aa_q"] = T(inp["L0_attn_a_Wq"])
    w["w_l0aa_k"] = T(inp["L0_attn_a_Wk"])
    w["w_l0aa_v"] = T(inp["L0_attn_a_Wv"])
    w["w_l0f_w1"] = np.ascontiguousarray(np.asarray(inp["L0_ffn_W1"]).reshape(8 * D, D))
    w["w_l0f_w2"] = np.ascontiguousarray(np.asarray(inp["L0_ffn_W2"]).reshape(8 * D, D))
    w["w_l0f_wr"] = np.ascontiguousarray(np.asarray(inp["L0_ffn_Wr"]))
    w["b_l0f_br"] = np.ascontiguousarray(np.asarray(inp["L0_ffn_br"]).reshape(1, 8))
    w["b_l0f_b1T"] = T(inp["L0_ffn_b1"])
    w["b_l0f_b2"] = np.ascontiguousarray(np.asarray(inp["L0_ffn_b2"]))
    w["w_l0ab_q"] = T(inp["L0_attn_b_Wq"])
    w["w_l0ab_k"] = T(inp["L0_attn_b_Wk"])
    w["w_l0ab_v"] = T(inp["L0_attn_b_Wv"])
    w["w_l0ab_o"] = T(inp["L0_attn_b_Wo"])
    w["w_l1aa_q"] = np.ascontiguousarray(np.asarray(inp["L1_attn_a_wq"]))
    w["w_l1aa_k"] = np.ascontiguousarray(np.asarray(inp["L1_attn_a_wk"]))
    w["w_l1aa_v"] = np.ascontiguousarray(np.asarray(inp["L1_attn_a_wv"]))
    w["w_l1aa_o"] = np.ascontiguousarray(np.asarray(inp["L1_attn_a_wo"]))
    w["w_l1ab_q"] = np.ascontiguousarray(np.asarray(inp["L1_attn_b_Wq"]))
    w["w_l1ab_k"] = np.ascontiguousarray(np.asarray(inp["L1_attn_b_Wk"]))
    w["w_l1ab_v"] = np.ascontiguousarray(np.asarray(inp["L1_attn_b_Wv"]))
    w["w_l1ab_o"] = np.ascontiguousarray(np.asarray(inp["L1_attn_b_Wo"]))
    w["w_l1f_gate"] = np.ascontiguousarray(np.asarray(inp["L1_ffn_w_gate"]))
    w["w_l1f_up"] = np.ascontiguousarray(np.asarray(inp["L1_ffn_w_up"]))
    w["w_l1f_down"] = np.ascontiguousarray(np.asarray(inp["L1_ffn_w_down"]))
    Wqkv2 = np.asarray(inp["L2_attn_a_W_qkv"])
    w["w_l2aa_q"] = T(Wqkv2[0:1024])
    w["w_l2aa_k"] = T(Wqkv2[1024:2048])
    w["w_l2aa_v"] = T(Wqkv2[2048:3072])
    w["ident1k"] = np.eye(D, dtype=np.float32)
    w["w_l2f_wT"] = np.ascontiguousarray(
        np.asarray(inp["L2_ffn_W"]).transpose(0, 2, 1).reshape(8 * D, D))
    w["w_l2f_b"] = np.ascontiguousarray(np.asarray(inp["L2_ffn_b"]))
    w["l2sigT"] = T(inp["L2_ffn_tile_sigs"])
    w["w_l2ab_q"] = T(inp["L2_attn_b_Wq"])
    w["w_l2ab_k"] = T(inp["L2_attn_b_Wk"])
    w["w_l2ab_v"] = T(inp["L2_attn_b_Wv"])
    w["w_l2ab_o"] = T(inp["L2_attn_b_Wo"])
    w["b_l2ab_qcol"] = np.ascontiguousarray(np.asarray(inp["L2_attn_b_bq"]).reshape(8, 128).T)
    w["b_l2ab_kcol"] = np.ascontiguousarray(np.asarray(inp["L2_attn_b_bk"]).reshape(8, 128).T)
    w["b_l2ab_v"] = np.ascontiguousarray(np.asarray(inp["L2_attn_b_bv"]).reshape(1, D))
    w["bkbc"] = np.ascontiguousarray(np.broadcast_to(np.asarray(inp["L2_attn_b_bk"])[None, :], (128, D)))
    w["bobc"] = np.ascontiguousarray(np.broadcast_to(np.asarray(inp["L2_attn_b_bo"])[None, :], (128, D)))
    w["bvbc"] = np.ascontiguousarray(np.broadcast_to(np.asarray(inp["L2_attn_b_bv"])[None, :], (128, D)))
    w["b_l0f_b2bc"] = np.ascontiguousarray(
        np.broadcast_to(np.asarray(inp["L0_ffn_b2"])[:, None, :], (8, 128, D)).reshape(8 * 128, D))
    Wqkv3 = np.asarray(inp["L3_attn_a_W_qkv"])
    w["w_l3aa_q"] = T(Wqkv3[0:1024])
    w["w_l3aa_k"] = T(Wqkv3[1024:2048])
    w["w_l3aa_v"] = T(Wqkv3[2048:3072])
    w["w_l3aa_o"] = T(inp["L3_attn_a_W_o"])
    w["w_l3f_gT"] = T(inp["L3_ffn_gate_w"])
    w["w_l3f_segT"] = np.ascontiguousarray(np.asarray(inp["L3_ffn_shared_expert_gate_w"]).reshape(8, 128).T)
    w["w_l3f_guT"] = np.ascontiguousarray(
        np.asarray(inp["L3_ffn_experts_gate_up"]).transpose(0, 2, 1).reshape(8 * D, 2816))
    w["w_l3f_sgT"] = T(inp["L3_ffn_shared_gate_w"])
    w["w_l3f_suT"] = T(inp["L3_ffn_shared_up_w"])
    w["w_l3f_dT"] = np.ascontiguousarray(
        np.asarray(inp["L3_ffn_experts_down"]).transpose(0, 2, 1).reshape(8 * 1408, D))
    w["w_l3f_sdT"] = T(inp["L3_ffn_shared_down_w"])
    w["w_m_inT"] = T(inp["L3_attn_b_in_proj_w"])
    w["w_m_convw"] = np.ascontiguousarray(np.asarray(inp["L3_attn_b_conv_w"]).reshape(2048, 4))
    w["w_m_convb"] = np.ascontiguousarray(np.asarray(inp["L3_attn_b_conv_b"]).reshape(16, 128).T)
    w["w_m_xprojT"] = T(inp["L3_attn_b_x_proj_w"])
    w["w_m_dtT"] = T(inp["L3_attn_b_dt_w"])
    w["w_m_dtb"] = np.ascontiguousarray(np.asarray(inp["L3_attn_b_dt_b"]).reshape(16, 128).T)
    w["w_m_Alog"] = np.ascontiguousarray(np.asarray(inp["L3_attn_b_A_log"]))
    w["w_m_D"] = np.ascontiguousarray(np.asarray(inp["L3_attn_b_D"]).reshape(16, 128).T)
    w["w_m_outT"] = T(inp["L3_attn_b_out_w"])
    w["w_m_ffgT"] = T(inp["L3_attn_b_ffg_w"])
    w["w_m_ffpT"] = T(inp["L3_attn_b_ffp_w"])
    bc = lambda a: np.ascontiguousarray(np.broadcast_to(np.asarray(a)[None, :], (128, np.asarray(a).shape[0])))
    w["b_m_n1g"] = bc(inp["L3_attn_b_n1_g"])
    w["b_m_n1b"] = bc(inp["L3_attn_b_n1_b"])
    w["b_m_n2g"] = bc(inp["L3_attn_b_n2_g"])
    w["b_m_n2b"] = bc(inp["L3_attn_b_n2_b"])
    w["b_m_ffgb"] = bc(inp["L3_attn_b_ffg_b"])
    w["b_m_ffpb"] = bc(inp["L3_attn_b_ffp_b"])
    return w


class Ctx:
    def __init__(self, nc, blk, sems, t):
        self.nc = nc
        self.blk = blk
        self.s = sems
        self.c = {n: 0 for n in sems}
        self.t = t

    def inc(self, inst, name, n=1):
        self.c[name] += n
        inst.then_inc(self.s[name], n)
        return self.c[name]

    def w(self, eng, name, val=None):
        if val is None:
            val = self.c[name]
        if val > 0:
            eng.wait_ge(self.s[name], val)


def emit_ln_stats(k):
    nc = k.nc
    SM, A4, A5, A1 = k.t["SM"], k.t["A4"], k.t["A5"], k.t["A1"]
    xres = k.t["xres"]
    ptb = [k.t["pt0"], k.t["pt1"]]

    @k.blk.vector
    def _(v):
        v.wait_ge(k.s["dS"], k.c["dS"])
        k.w(v, "se", k.c["se"])
        last = None
        for t in range(NT):
            last = v.reduce_sum(SM[:, t:t + 1], xres[:, t * 1024:(t + 1) * 1024], axis=AX.X)
        k.inc(last, "ve")
    m_sums = k.c["ve"]

    @k.blk.scalar
    def _(s):
        s.wait_ge(k.s["dS"], k.c["dS"])
        k.w(s, "ve", k.c["ve"])
        last = None
        for t in range(NT):
            last = s.activation(A5[:, (t % 2) * 1024:(t % 2) * 1024 + 1024],
                                xres[:, t * 1024:(t + 1) * 1024], AF.Square,
                                accum_out=SM[:, 16 + t:17 + t])
        k.inc(last, "se")
    m_sq = k.c["se"]

    @k.blk.vector
    def _(v):
        k.w(v, "ve", m_sums)
        k.w(v, "se", m_sq)
        def step(inst):
            m = k.inc(inst, "ve")
            v.wait_ge(k.s["ve"], m)
        step(v.tensor_scalar_mul(SM[:, 32:48], SM[:, 0:16], 1.0 / 1024.0))
        step(v.tensor_scalar_mul(SM[:, 48:64], SM[:, 16:32], 1.0 / 1024.0))
        step(v.tensor_tensor(SM[:, 96:112], SM[:, 32:48], SM[:, 32:48], ALU.mult))
        step(v.tensor_tensor(SM[:, 48:64], SM[:, 48:64], SM[:, 96:112], ALU.subtract))
        k.inc(v.tensor_scalar_add(SM[:, 48:64], SM[:, 48:64], 1e-5), "ve")
    m_var = k.c["ve"]

    @k.blk.scalar
    def _(s):
        k.w(s, "ve", m_var)
        m = k.inc(s.activation(SM[:, 96:112], SM[:, 48:64], AF.Ln), "se")
        s.wait_ge(k.s["se"], m)
        k.inc(s.activation(SM[:, 64:80], SM[:, 96:112], AF.Exp, scale=-0.5), "se")
    m_rst = k.c["se"]

    @k.blk.vector
    def _(v):
        k.w(v, "se", m_rst)
        k.inc(v.scalar_tensor_tensor(SM[:, 80:96], SM[:, 32:48], -1.0, SM[:, 64:80],
                                     ALU.mult, ALU.mult), "ve")
    m_nmr = k.c["ve"]

    return m_nmr


def emit_ln_xnT(k, also_nat=None):
    nc = k.nc
    SM, A4, A5, A1 = k.t["SM"], k.t["A4"], k.t["A5"], k.t["A1"]
    xres = k.t["xres"]
    ptb = [k.t["pt0"], k.t["pt1"]]
    m_nmr = emit_ln_stats(k)
    m_tr = {}
    m_evac = {}
    for t in range(NT):
        xt = xres[:, t * 1024:(t + 1) * 1024]
        xn = A4[:, (t % 2) * 1024:(t % 2) * 1024 + 1024]

        @k.blk.scalar
        def _(s, xt=xt, xn=xn, t=t):
            if t == 0:
                k.w(s, "ve", m_nmr)
            if t >= 2:
                k.w(s, "te", m_tr[t - 2])
            k.inc(s.activation(xn, xt, AF.Identity,
                               bias=SM[:, 80 + t:81 + t], scale=SM[:, 64 + t:65 + t]), "se")
        m_xn = k.c["se"]

        if also_nat is not None:
            @k.blk.vector
            def _(v, xn=xn, t=t, m_xn=m_xn):
                k.w(v, "se", m_xn)
                k.inc(v.tensor_copy(also_nat(t), xn), "ve")

        @k.blk.tensor
        def _(te, xn=xn, t=t, m_xn=m_xn):
            k.w(te, "se", m_xn)
            k.w(te, "dG", k.c["dG"])
            if t >= 1:
                k.w(te, "ve", m_evac[t - 1])
            last = None
            for g in range(8):
                last = te.transpose(ptb[g // 4][:, (g % 4) * 128:(g % 4) * 128 + 128],
                                    xn[:, g * 128:g * 128 + 128], k.t["identb"][:])
            m_tr[t] = k.inc(last, "te")

        @k.blk.vector
        def _(v, t=t):
            k.w(v, "te", m_tr[t])
            last = None
            for g in range(8):
                last = v.tensor_copy(A1[:, g * 2048 + t * 128:g * 2048 + t * 128 + 128],
                                     ptb[g // 4][:, (g % 4) * 128:(g % 4) * 128 + 128])
            m_evac[t] = k.inc(last, "ve")
    return k.c["ve"]


def emit_l0_attn_a(k):
    nc = k.nc
    A1, A2, A3, A4, A5, A6 = (k.t[n] for n in ("A1", "A2", "A3", "A4", "A5", "A6"))
    pb = [k.t[f"pb{i}"] for i in range(6)]
    SM = k.t["SM"]
    m_xnT = emit_ln_xnT(k)
    if DBG:
        @k.blk.sync
        def _(sy):
            k.w(sy, "ve", m_xnT)
            k.inc(sy.dma_start(k.t["dbg_xnT"][:, :], k.t["A1"][:]), "dS", 16)

    @k.blk.gpsimd
    def _(g):
        k.inc(g.dma_start(A2[:, 0:8192].rearrange("p (g d) -> p g d", d=1024),
                    k.t["w_l0aa_q"].ap().rearrange("(g p) d -> p g d", p=128)), "dG", 16)
        k.inc(g.dma_start(A2[:, 8192:16384].rearrange("p (g d) -> p g d", d=1024),
                    k.t["w_l0aa_k"].ap().rearrange("(g p) d -> p g d", p=128)), "dG", 16)
        k.inc(g.dma_start(A3[:, 0:8192].rearrange("p (g d) -> p g d", d=1024),
                    k.t["w_l0aa_v"].ap().rearrange("(g p) d -> p g d", p=128)), "dG", 16)
    m_w = k.c["dG"]

    @k.blk.vector
    def _(v):
        v.memset(SM[:, 200:201], -20.0)

    for which, wofs, warena, ccname in (("k", 8192, A2, "loc_k"), ("v", 0, A3, "loc_v")):
        for t in range(NT):
            for ch in range(2):
                bank = pb[(t * 2 + ch) % 2]

                @k.blk.tensor
                def _(te, t=t, ch=ch, bank=bank, wofs=wofs, warena=warena):
                    if t == 0 and ch == 0:
                        k.w(te, "dG", m_w)
                        k.w(te, "ve", m_xnT)
                    k.w(te, "se", k.c["se"])
                    last = None
                    for gi in range(8):
                        last = te.matmul(bank[:],
                                         A1[:, gi * 2048 + t * 128: gi * 2048 + t * 128 + 128],
                                         warena[:, wofs + gi * 1024 + ch * 512: wofs + gi * 1024 + ch * 512 + 512],
                                         start=(gi == 0), stop=(gi == 7))
                    k.inc(last, "te")
                m_mm = k.c["te"]

                @k.blk.scalar
                def _(s, t=t, ch=ch, bank=bank, m_mm=m_mm):
                    k.w(s, "te", m_mm)
                    k.w(s, "dS", k.c["dS"])
                    dst = A4[:, ((t * 2 + ch) % 4) * 512: ((t * 2 + ch) % 4) * 512 + 512]
                    k.inc(s.activation(dst, bank[:], AF.Copy), "se")
                m_ev = k.c["se"]

                @k.blk.sync
                def _(sy, t=t, ch=ch, m_ev=m_ev, ccname=ccname):
                    k.w(sy, "se", m_ev)
                    src = A4[:, ((t * 2 + ch) % 4) * 512: ((t * 2 + ch) % 4) * 512 + 512]
                    k.inc(sy.dma_start(k.t[ccname][t * 128:(t + 1) * 128, ch * 512:(ch + 1) * 512], src), "dS", 16)

    for gout in range(8):
        for tc in range(4):
            bank = pb[(gout * 4 + tc) % 2]

            @k.blk.tensor
            def _(te, gout=gout, tc=tc, bank=bank):
                k.w(te, "se", k.c["se"])
                last = None
                for gi in range(8):
                    last = te.matmul(bank[:],
                                     A2[:, gi * 1024 + gout * 128: gi * 1024 + gout * 128 + 128],
                                     A1[:, gi * 2048 + tc * 512: gi * 2048 + tc * 512 + 512],
                                     start=(gi == 0), stop=(gi == 7))
                k.inc(last, "te")
            m_mm = k.c["te"]

            @k.blk.scalar
            def _(s, gout=gout, tc=tc, bank=bank, m_mm=m_mm):
                k.w(s, "te", m_mm)
                k.w(s, "dS", k.c["dS"])
                dst = A4[:, ((gout * 4 + tc) % 4) * 512: ((gout * 4 + tc) % 4) * 512 + 512]
                k.inc(s.activation(dst, bank[:], AF.Copy, scale=1.0 / 32.0), "se")
            m_ev = k.c["se"]

            @k.blk.sync
            def _(sy, gout=gout, tc=tc, m_ev=m_ev):
                k.w(sy, "se", m_ev)
                src = A4[:, ((gout * 4 + tc) % 4) * 512: ((gout * 4 + tc) % 4) * 512 + 512]
                k.inc(sy.dma_start(k.t["loc_qT"][gout * 128:(gout + 1) * 128, tc * 512:(tc + 1) * 512], src), "dS", 16)

    m_proj_done = k.c["dS"]

    @k.blk.sync
    def _(sy):
        k.w(sy, "dS", m_proj_done)
        k.w(sy, "te", k.c["te"])
        for g in range(8):
            k.inc(sy.dma_start(A1[:, g * 2048: g * 2048 + 1024],
                               k.t["loc_k"][g * 256: (g + 1) * 256: 2, :]), "dS", 16)
            k.inc(sy.dma_start(A1[:, g * 2048 + 1024: (g + 1) * 2048],
                               k.t["loc_k"][g * 256 + 1: (g + 1) * 256: 2, :]), "dS", 16)
    m_kr = k.c["dS"]

    for ic in range(8):
        @k.blk.sync
        def _(sy, ic=ic):
            if ic == 0:
                k.w(sy, "dS", m_kr)
            k.w(sy, "te", k.c["te"])
            for g in range(8):
                k.inc(sy.dma_start(A3[:, 8192 + g * 256: 8192 + (g + 1) * 256],
                                   k.t["loc_qT"][g * 128:(g + 1) * 128, ic * 256:(ic + 1) * 256]), "dS", 16)
        m_q = k.c["dS"]

        for jt in range(NT):
            bank = pb[jt % 2]

            @k.blk.tensor
            def _(te, ic=ic, jt=jt, bank=bank, m_q=m_q):
                k.w(te, "dS", m_q)
                k.w(te, "se", k.c["se"])
                last = None
                for g in range(8):
                    base = g * 2048 + (jt // 8) * 1024 + (jt % 8) * 128
                    last = te.matmul(bank[:, 0:256],
                                     A1[:, base: base + 128],
                                     A3[:, 8192 + g * 256: 8192 + (g + 1) * 256],
                                     start=(g == 0), stop=(g == 7))
                k.inc(last, "te")
            m_mm = k.c["te"]

            @k.blk.scalar
            def _(s, ic=ic, jt=jt, bank=bank, m_mm=m_mm):
                k.w(s, "te", m_mm)
                k.w(s, "dS", k.c["dS"])
                dst = A5[:, (jt % 4) * 512:(jt % 4) * 512 + 256]
                k.inc(s.activation(dst, bank[:, 0:256], AF.Exp, bias=SM[:, 200:201]), "se")
            m_ev = k.c["se"]

            @k.blk.sync
            def _(sy, ic=ic, jt=jt, m_ev=m_ev):
                k.w(sy, "se", m_ev)
                src = A5[:, (jt % 4) * 512:(jt % 4) * 512 + 256]
                k.inc(sy.dma_start(k.t["cc_e_in"][jt * 128:(jt + 1) * 128, ic * 256:(ic + 1) * 256], src), "dS", 16)

    m_e_done = k.c["dS"]

    @k.blk.gpsimd
    def _(g):
        k.w(g, "dS", m_e_done)
        k.inc(g.collective_compute("AllReduce", ALU.add, replica_groups=RG,
                                   ins=[k.t["cc_e_in"].ap().opt()],
                                   outs=[k.t["cc_den"].ap().opt()]), "cc")
    m_cc = k.c["cc"]

    for ic in range(8):
        for jt in range(NT):
            full_masked = (jt * 128) >= (ic * 256 + 256)
            off = jt * 128 - ic * 256
            partial = off in (0, 128)

            @k.blk.sync
            def _(sy, ic=ic, jt=jt):
                if ic == 0 and jt == 0:
                    k.w(sy, "cc", m_cc)
                k.w(sy, "ve", k.c["ve"])
                k.w(sy, "te", k.c["te"])
                if not full_masked:
                    k.inc(sy.dma_start(A5[:, 0:256],
                                       k.t["cc_e_in"][jt * 128:(jt + 1) * 128, ic * 256:(ic + 1) * 256]), "dS", 16)
                    k.inc(sy.dma_start(A5[:, 256:512],
                                       k.t["cc_den"][jt * 128:(jt + 1) * 128, ic * 256:(ic + 1) * 256]), "dS", 16)
                k.inc(sy.dma_start(A4[:, 1024 + (jt % 2) * 1024: 1024 + (jt % 2) * 1024 + 1024],
                                   k.t["loc_v"][jt * 128:(jt + 1) * 128, :]), "dS", 16)
            m_ld = k.c["dS"]

            @k.blk.vector
            def _(v, ic=ic, jt=jt, m_ld=m_ld, full_masked=full_masked, partial=partial, off=off):
                k.w(v, "dS", m_ld)
                k.w(v, "te", k.c["te"])
                at = A4[:, 0:256]
                if full_masked:
                    k.inc(v.memset(at, 1e-9), "ve")
                else:
                    v.reciprocal(A5[:, 256:512], A5[:, 256:512])
                    v.tensor_tensor(A5[:, 0:256], A5[:, 0:256], A5[:, 256:512], ALU.mult)
                    if partial:
                        mA = k.t["mtri"][:, (off // 128) * 256:(off // 128) * 256 + 256]
                        v.tensor_scalar_add(A5[:, 0:256], A5[:, 0:256], -1e-9)
                        v.tensor_tensor(A5[:, 0:256], A5[:, 0:256], mA, ALU.mult)
                        k.inc(v.tensor_scalar_add(at, A5[:, 0:256], 1e-9), "ve")
                    else:
                        k.inc(v.tensor_copy(at, A5[:, 0:256]), "ve")
            m_at = k.c["ve"]

            @k.blk.tensor
            def _(te, ic=ic, jt=jt, m_at=m_at):
                k.w(te, "ve", m_at)
                if jt == 0:
                    k.w(te, "ve", k.c["ve"])
                at = A4[:, 0:256]
                vv = A4[:, 1024 + (jt % 2) * 1024: 1024 + (jt % 2) * 1024 + 1024]
                last = None
                for isl in range(2):
                    for ch in range(2):
                        last = te.matmul(pb[2 + isl * 2 + ch][:],
                                         at[:, isl * 128: isl * 128 + 128],
                                         vv[:, ch * 512:(ch + 1) * 512],
                                         start=(jt == 0), stop=(jt == 15))
                k.inc(last, "te")
        m_mm = k.c["te"]

        @k.blk.vector
        def _(v, ic=ic, m_mm=m_mm):
            k.w(v, "te", m_mm)
            last = None
            for isl in range(2):
                for ch in range(2):
                    xs = k.t["xres"][:, (2 * ic + isl) * 1024 + ch * 512: (2 * ic + isl) * 1024 + ch * 512 + 512]
                    last = v.tensor_tensor(xs, xs, pb[2 + isl * 2 + ch][:], ALU.add)
            k.inc(last, "ve")


def emit_l0_ffn(k):
    nc = k.nc
    A1, A2, A3, A4, A5, A6 = (k.t[n] for n in ("A1", "A2", "A3", "A4", "A5", "A6"))
    pb = [k.t[f"pb{i}"] for i in range(6)]
    SM = k.t["SM"]
    m_xnT = emit_ln_xnT(k)

    @k.blk.scalar
    def _(s):
        k.inc(s.activation(A5[:, 0:1024], k.t["xres"][:, 0:1024], AF.Identity,
                           bias=SM[:, 80:81], scale=SM[:, 64:65]), "se")
    m_xn32 = k.c["se"]

    @k.blk.sync
    def _(sy):
        k.inc(sy.dma_start(SM[:, 120:184].rearrange("p (g e) -> p g e", e=8),
                           k.t["b_l0f_b1T"].ap().rearrange("(g p) e -> p g e", p=128)), "dS", 16)
        k.inc(sy.dma_start(SM[:, 190:198].rearrange("p (a e) -> p a e", a=1)[0:1],
                           k.t["b_l0f_br"][0:1, :]), "dS", 16)
        k.inc(sy.dma_start(SM[:, 200:264].rearrange("p (g e) -> p g e", e=8),
                           k.t["w_l0f_wr"].ap().rearrange("(g p) e -> p g e", p=128)), "dS", 16)
    m_rload = k.c["dS"]

    @k.blk.tensor
    def _(te):
        k.w(te, "se", m_xn32)
        k.w(te, "ve", k.c["ve"])
        last = None
        for g in range(8):
            bank = pb[g // 4]
            last = te.transpose(bank[:, (g % 4) * 128:(g % 4) * 128 + 128],
                                A5[:, g * 128:(g + 1) * 128], k.t["identf32"][:])
        k.inc(last, "te")
    m_tr0 = k.c["te"]

    @k.blk.vector
    def _(v):
        k.w(v, "te", m_tr0)
        last = None
        for g in range(8):
            bank = pb[g // 4]
            last = v.tensor_copy(A6[:, g:g + 1], bank[:, (g % 4) * 128:(g % 4) * 128 + 1])
        k.inc(last, "ve")
    m_x0col = k.c["ve"]

    @k.blk.tensor
    def _(te):
        k.w(te, "ve", m_x0col)
        k.w(te, "dS", m_rload)
        last = None
        for g in range(8):
            last = te.matmul(pb[2][0:1, 0:8], A6[:, g:g + 1], SM[:, 200 + g * 8:208 + g * 8],
                             start=(g == 0), stop=(g == 7))
        k.inc(last, "te")
    m_sc = k.c["te"]

    @k.blk.vector
    def _(v):
        k.w(v, "te", m_sc)
        def step(inst):
            m = k.inc(inst, "ve")
            v.wait_ge(k.s["ve"], m)
        sc = SM[0:1, 264:272]
        step(v.tensor_tensor(sc, pb[2][0:1, 0:8], SM[0:1, 190:198], ALU.add))
        m1 = SM[0:1, 272:273]
        step(v.reduce_max(m1, sc, axis=AX.X))
        is1 = SM[0:1, 274:282]
        step(v.tensor_single_scalar(is1, sc, m1, ALU.is_equal))
        ones8 = SM[0:1, 284:292]
        step(v.memset(ones8, 1.0))
        cum = SM[0:1, 294:302]
        step(v.tensor_tensor_scan(cum, ones8, is1, 0.0, ALU.mult, ALU.add))
        f1 = SM[0:1, 304:312]
        step(v.tensor_single_scalar(cum, cum, 1.0, ALU.is_equal))
        step(v.tensor_tensor(f1, is1, cum, ALU.mult))
        s2 = SM[0:1, 314:322]
        step(v.scalar_tensor_tensor(s2, f1, -1e30, sc, ALU.mult, ALU.add))
        m2 = SM[0:1, 324:325]
        step(v.reduce_max(m2, s2, axis=AX.X))
        ind = SM[0:1, 326:334]
        step(v.tensor_single_scalar(ind, sc, m2, ALU.is_ge))
        k.inc(v.tensor_scalar_mul(SM[0:1, 326:334], ind, 0.5), "ve")
    m_ind = k.c["ve"]

    @k.blk.vector
    def _(v):
        v.memset(SM[0:1, 336:464], 1.0)
        k.inc(v.memset(A6[:, 512:1024], 0.0), "ve")
    m_ones = k.c["ve"]

    @k.blk.tensor
    def _(te):
        k.w(te, "ve", max(m_ind, m_ones))
        k.inc(te.matmul(pb[3][:, 0:8], SM[0:1, 336:464], SM[0:1, 326:334],
                        start=True, stop=True), "te")
    m_wb = k.c["te"]

    @k.blk.vector
    def _(v):
        k.w(v, "te", m_wb)
        k.inc(v.tensor_copy(SM[:, 112:120], pb[3][:, 0:8]), "ve")
    m_wbc = k.c["ve"]

    if DBG:
        @k.blk.sync
        def _(sy):
            k.w(sy, "ve", m_wbc)
            k.inc(sy.dma_start(k.t["dbg_sm"][:, :], SM[:]), "dS", 16)

    @k.blk.vector
    def _(v):
        k.w(v, "ve", m_wbc)
        k.inc(v.memset(A6[:, 0:1024], 0.0), "ve")

    for e in range(8):
        @k.blk.sync
        def _(sy, e=e):
            k.w(sy, "ve", k.c["ve"])
            k.inc(sy.dma_start(A5[:, 1024:2048],
                               k.t["b_l0f_b2bc"][e * 128:(e + 1) * 128, :]), "dS", 16)
        m_b2 = k.c["dS"]

        @k.blk.vector
        def _(v, e=e, m_b2=m_b2):
            k.w(v, "dS", m_b2)
            k.inc(v.scalar_tensor_tensor(A6[:, 0:1024], A5[:, 1024:2048],
                                         SM[:, 112 + e:113 + e], A6[:, 0:1024],
                                         ALU.mult, ALU.add), "ve")
    m_bsel = k.c["ve"]

    for t in range(NT):
        @k.blk.vector
        def _(v, t=t):
            if t == 0:
                k.w(v, "ve", m_bsel)
            xt = k.t["xres"][:, t * 1024:(t + 1) * 1024]
            k.inc(v.tensor_tensor(xt, xt, A6[:, 0:1024], ALU.add), "ve")

    for e in range(8):
        @k.blk.gpsimd
        def _(g, e=e):
            k.w(g, "te", k.c["te"])
            k.inc(g.dma_start(A2[:, 0:8192].rearrange("p (g2 d) -> p g2 d", d=1024),
                              k.t["w_l0f_w1"][e * 1024:(e + 1) * 1024, :].rearrange("(g2 p) d -> p g2 d", p=128)), "dG", 16)
            k.inc(g.dma_start(A2[:, 8192:16384].rearrange("p (g2 d) -> p g2 d", d=1024),
                              k.t["w_l0f_w2"][e * 1024:(e + 1) * 1024, :].rearrange("(g2 p) d -> p g2 d", p=128)), "dG", 16)
        m_we = k.c["dG"]

        for gout in range(8):
            for tc in range(4):
                bank = pb[(gout * 4 + tc) % 2]

                @k.blk.tensor
                def _(te, e=e, gout=gout, tc=tc, bank=bank, m_we=m_we):
                    k.w(te, "dG", m_we)
                    if gout == 0 and tc == 0:
                        k.w(te, "ve", k.c["ve"])
                    k.w(te, "se", k.c["se"])
                    last = None
                    for gi in range(8):
                        last = te.matmul(bank[:],
                                         A2[:, gi * 1024 + gout * 128: gi * 1024 + gout * 128 + 128],
                                         A1[:, gi * 2048 + tc * 512: gi * 2048 + tc * 512 + 512],
                                         start=(gi == 0), stop=(gi == 7))
                    k.inc(last, "te")
                m_mm = k.c["te"]

                @k.blk.scalar
                def _(s, e=e, gout=gout, tc=tc, bank=bank, m_mm=m_mm):
                    k.w(s, "te", m_mm)
                    k.w(s, "ve", k.c["ve"])
                    dst = A3[:, gout * 2048 + tc * 512: gout * 2048 + (tc + 1) * 512]
                    k.inc(s.activation(dst, bank[:], AF.Gelu,
                                       bias=SM[:, 120 + gout * 8 + e:121 + gout * 8 + e]), "se")
        m_h = k.c["se"]

        @k.blk.vector
        def _(v, e=e, m_h=m_h):
            k.w(v, "se", m_h)
            last = None
            for gout in range(8):
                last = v.tensor_scalar_mul(A3[:, gout * 2048:(gout + 1) * 2048],
                                           A3[:, gout * 2048:(gout + 1) * 2048],
                                           SM[:, 112 + e:113 + e])
            k.inc(last, "ve")
        m_hw = k.c["ve"]

        for t in range(NT):
            for ch in range(2):
                bank = pb[2 + (t * 2 + ch) % 2]

                @k.blk.tensor
                def _(te, e=e, t=t, ch=ch, bank=bank, m_hw=m_hw):
                    k.w(te, "ve", k.c["ve"])
                    last = None
                    for gi in range(8):
                        last = te.matmul(bank[:],
                                         A3[:, gi * 2048 + t * 128: gi * 2048 + t * 128 + 128],
                                         A2[:, 8192 + gi * 1024 + ch * 512: 8192 + gi * 1024 + ch * 512 + 512],
                                         start=(gi == 0), stop=(gi == 7))
                    k.inc(last, "te")
                m_mm = k.c["te"]

                @k.blk.vector
                def _(v, t=t, ch=ch, bank=bank, m_mm=m_mm):
                    k.w(v, "te", m_mm)
                    xt = k.t["xres"][:, t * 1024 + ch * 512: t * 1024 + ch * 512 + 512]
                    k.inc(v.tensor_tensor(xt, xt, bank[:], ALU.add), "ve")


def emit_l0_attn_b(k):
    nc = k.nc
    A1, A2, A3, A4, A5, A6 = (k.t[n] for n in ("A1", "A2", "A3", "A4", "A5", "A6"))
    pb = [k.t[f"pb{i}"] for i in range(6)]
    SM = k.t["SM"]
    m_xnT = emit_ln_xnT(k)

    @k.blk.gpsimd
    def _(g):
        k.w(g, "te", k.c["te"])
        k.inc(g.dma_start(A2[:, 0:8192].rearrange("p (g2 d) -> p g2 d", d=1024),
                          k.t["w_l0ab_q"].ap().rearrange("(g2 p) d -> p g2 d", p=128)), "dG", 16)
        k.inc(g.dma_start(A2[:, 8192:16384].rearrange("p (g2 d) -> p g2 d", d=1024),
                          k.t["w_l0ab_k"].ap().rearrange("(g2 p) d -> p g2 d", p=128)), "dG", 16)
        k.inc(g.dma_start(A3[:, 0:8192].rearrange("p (g2 d) -> p g2 d", d=1024),
                          k.t["w_l0ab_v"].ap().rearrange("(g2 p) d -> p g2 d", p=128)), "dG", 16)
        k.inc(g.dma_start(A3[:, 8192:16384].rearrange("p (g2 d) -> p g2 d", d=1024),
                          k.t["w_l0ab_o"].ap().rearrange("(g2 p) d -> p g2 d", p=128)), "dG", 16)
    m_w = k.c["dG"]

    KVs = A4[:, 7360:7360 + 520]

    @k.blk.vector
    def _(v):
        k.inc(v.memset(KVs, 0.0), "ve")
    m_kvz = k.c["ve"]

    pkv = [pb[4], pb[5]]

    for c in range(NT):
        for which, wofs, warena, aofs in (("q", 0, A2, 0), ("k", 8192, A2, 1024), ("v", 0, A3, 2048)):
            for ch in range(2):
                bank = pb[ch % 2]

                @k.blk.tensor
                def _(te, c=c, ch=ch, bank=bank, wofs=wofs, warena=warena):
                    if c == 0 and wofs == 0 and ch == 0:
                        k.w(te, "dG", m_w)
                        k.w(te, "ve", m_xnT)
                    k.w(te, "se", k.c["se"])
                    k.w(te, "ve", k.c["ve"])
                    last = None
                    for gi in range(8):
                        last = te.matmul(bank[:],
                                         A1[:, gi * 2048 + c * 128: gi * 2048 + c * 128 + 128],
                                         warena[:, wofs + gi * 1024 + ch * 512: wofs + gi * 1024 + ch * 512 + 512],
                                         start=(gi == 0), stop=(gi == 7))
                    k.inc(last, "te")
                m_mm = k.c["te"]

                @k.blk.scalar
                def _(s, c=c, ch=ch, bank=bank, m_mm=m_mm, aofs=aofs):
                    k.w(s, "te", m_mm)
                    k.w(s, "ve", k.c["ve"])
                    k.inc(s.activation(A4[:, aofs + ch * 512: aofs + ch * 512 + 512], bank[:], AF.Copy), "se")
        m_qkv = k.c["se"]

        @k.blk.sync
        def _(sy, c=c):
            k.w(sy, "ve", k.c["ve"])
            k.inc(sy.dma_start(A6[:, 0:64], k.t["rope_neox"][c * 128:(c + 1) * 128, :]), "dS", 16)
        m_rt = k.c["dS"]

        @k.blk.vector
        def _(v, c=c, m_qkv=m_qkv, m_rt=m_rt):
            k.w(v, "se", m_qkv)
            k.w(v, "dS", m_rt)
            cosb = A6[:, 0:32].rearrange("p (a i) -> p a i", a=1).broadcast_to([128, 16, 32])
            sinb = A6[:, 32:64].rearrange("p (a i) -> p a i", a=1).broadcast_to([128, 16, 32])
            for base in (0, 1024):
                x3 = A4[:, base:base + 1024].rearrange("p (h i) -> p h i", i=64)
                xa = x3[:, :, 0:32]
                xb = x3[:, :, 32:64]
                t1 = A4[:, 3072:4096].rearrange("p (h i) -> p h i", i=64)[:, :, 0:32]
                t2 = A4[:, 3072:4096].rearrange("p (h i) -> p h i", i=64)[:, :, 32:64]
                v.tensor_tensor(t1, xa, cosb, ALU.mult)
                v.tensor_tensor(t2, xb, sinb, ALU.mult)
                v.tensor_tensor(t1, t1, t2, ALU.subtract)
                v.tensor_tensor(t2, xa, sinb, ALU.mult)
                v.tensor_tensor(xb, xb, cosb, ALU.mult)
                v.tensor_tensor(xb, t2, xb, ALU.add)
                v.tensor_copy(xa, t1)
            v.tensor_scalar_min(A4[:, 3072:4096], A4[:, 0:1024], 0.0)
            k.inc(v.tensor_scalar_min(A4[:, 4096:5120], A4[:, 1024:2048], 0.0), "ve")
        m_rope = k.c["ve"]

        @k.blk.scalar
        def _(s, c=c, m_rope=m_rope):
            k.w(s, "ve", m_rope)
            s.activation(A4[:, 3072:4096], A4[:, 3072:4096], AF.Exp)
            k.inc(s.activation(A4[:, 4096:5120], A4[:, 4096:5120], AF.Exp), "se")
        m_exp = k.c["se"]

        @k.blk.vector
        def _(v, c=c, m_exp=m_exp):
            k.w(v, "se", m_exp)
            v.tensor_scalar_max(A4[:, 0:1024], A4[:, 0:1024], 0.0)
            v.tensor_tensor(A4[:, 0:1024], A4[:, 0:1024], A4[:, 3072:4096], ALU.add)
            v.tensor_scalar_max(A4[:, 1024:2048], A4[:, 1024:2048], 0.0)
            v.tensor_tensor(A4[:, 1024:2048], A4[:, 1024:2048], A4[:, 4096:5120], ALU.add)
            v.memset(A4[:, 3072:4112], 1.0)
            k.inc(v.tensor_copy(A4[:, 3072:4112].rearrange("p (h i) -> p h i", i=65)[:, :, 0:64],
                                A4[:, 2048:3072].rearrange("p (h i) -> p h i", i=64)), "ve")
        m_phi = k.c["ve"]

        @k.blk.tensor
        def _(te, c=c, m_phi=m_phi):
            k.w(te, "ve", m_phi)
            k.w(te, "se", k.c["se"])
            last = None
            for g in range(8):
                last = te.transpose(k.t["pt0"][:, (g % 4) * 128:(g % 4) * 128 + 128] if g < 4
                                    else k.t["pt1"][:, (g % 4) * 128:(g % 4) * 128 + 128],
                                    A4[:, g * 128:(g + 1) * 128], k.t["identb"][:])
            k.inc(last, "te")
        m_trq = k.c["te"]

        @k.blk.scalar
        def _(s, c=c, m_trq=m_trq):
            k.w(s, "te", m_trq)
            last = None
            for g in range(8):
                src = k.t["pt0"] if g < 4 else k.t["pt1"]
                last = s.activation(A4[:, 4160 + g * 128: 4160 + (g + 1) * 128],
                                    src[:, (g % 4) * 128:(g % 4) * 128 + 128], AF.Copy)
            k.inc(last, "se")
        m_eq = k.c["se"]

        @k.blk.tensor
        def _(te, c=c, m_eq=m_eq):
            k.w(te, "se", m_eq)
            last = None
            for g in range(8):
                last = te.transpose(k.t["pt0"][:, (g % 4) * 128:(g % 4) * 128 + 128] if g < 4
                                    else k.t["pt1"][:, (g % 4) * 128:(g % 4) * 128 + 128],
                                    A4[:, 1024 + g * 128: 1024 + (g + 1) * 128], k.t["identb"][:])
            k.inc(last, "te")
        m_trk = k.c["te"]

        @k.blk.scalar
        def _(s, c=c, m_trk=m_trk):
            k.w(s, "te", m_trk)
            last = None
            for g in range(8):
                src = k.t["pt0"] if g < 4 else k.t["pt1"]
                last = s.activation(A4[:, 5184 + g * 128: 5184 + (g + 1) * 128],
                                    src[:, (g % 4) * 128:(g % 4) * 128 + 128], AF.Copy)
            k.inc(last, "se")
        m_ek = k.c["se"]

        for h in range(16):
            qT_h = A4[:, 4160 + (h // 2) * 128:4160 + (h // 2) * 128 + 128][(h % 2) * 64:(h % 2) * 64 + 64, :]
            kT_h = A4[:, 5184 + (h // 2) * 128:5184 + (h // 2) * 128 + 128][(h % 2) * 64:(h % 2) * 64 + 64, :]
            k_h = A4[:, 1024 + h * 64:1024 + (h + 1) * 64]
            vp_h = A4[:, 3072 + h * 65:3072 + (h + 1) * 65]
            kv_bank = pkv[h // 8]
            kv_sl = kv_bank[(h % 2) * 64:(h % 2) * 64 + 64, ((h % 8) // 2) * 65:((h % 8) // 2) * 65 + 65]
            kvs_sl = KVs[(h % 2) * 64:(h % 2) * 64 + 64, ((h // 2) % 8) * 65:((h // 2) % 8) * 65 + 65] \
                if False else KVs[(h % 2) * 64:(h % 2) * 64 + 64, (h // 2) * 65:(h // 2) * 65 + 65]

            @k.blk.tensor
            def _(te, c=c, h=h, m_ek=m_ek, kT_h=kT_h, qT_h=qT_h):
                k.w(te, "se", m_ek)
                k.w(te, "ve", k.c["ve"])
                k.inc(te.matmul(pb[2][:, 0:128], kT_h, qT_h, start=True, stop=True), "te")
            m_st = k.c["te"]

            @k.blk.vector
            def _(v, c=c, h=h, m_st=m_st):
                k.w(v, "te", m_st)
                k.inc(v.tensor_tensor(A4[:, 6208:6336], pb[2][:, 0:128], k.t["ctri"][:], ALU.mult), "ve")
            m_T = k.c["ve"]

            @k.blk.tensor
            def _(te, c=c, h=h, m_T=m_T, qT_h=qT_h, vp_h=vp_h, kvs_sl=kvs_sl, k_h=k_h, kv_sl=kv_sl):
                k.w(te, "ve", m_T if c > 0 else max(m_T, m_kvz))
                te.matmul(pb[3][:, 0:65], A4[:, 6208:6336], vp_h, start=True, stop=False)
                k.inc(te.matmul(pb[3][:, 0:65], qT_h, kvs_sl, start=False, stop=True), "te")
                k.inc(te.matmul(kv_sl, k_h, vp_h, start=(c == 0), stop=True), "te")
            m_num = k.c["te"]

            @k.blk.vector
            def _(v, c=c, h=h, m_num=m_num, kv_sl=kv_sl, kvs_sl=kvs_sl):
                k.w(v, "te", m_num)
                def step(inst):
                    m = k.inc(inst, "ve")
                    v.wait_ge(k.s["ve"], m)
                dd = SM[:, 300:301]
                step(v.tensor_scalar_max(dd, pb[3][:, 64:65], 1e-6))
                step(v.reciprocal(dd, dd))
                v.tensor_scalar(A4[:, 6336 + h * 64: 6336 + (h + 1) * 64], pb[3][:, 0:64],
                                dd, None, ALU.mult)
                k.inc(v.tensor_copy(kvs_sl, kv_sl), "ve")

        m_y = k.c["ve"]

        @k.blk.tensor
        def _(te, c=c, m_y=m_y):
            k.w(te, "ve", m_y)
            k.w(te, "se", k.c["se"])
            last = None
            for g in range(8):
                last = te.transpose(k.t["pt0"][:, (g % 4) * 128:(g % 4) * 128 + 128] if g < 4
                                    else k.t["pt1"][:, (g % 4) * 128:(g % 4) * 128 + 128],
                                    A4[:, 6336 + g * 128: 6336 + (g + 1) * 128], k.t["identb"][:])
            k.inc(last, "te")
        m_try = k.c["te"]

        @k.blk.scalar
        def _(s, c=c, m_try=m_try):
            k.w(s, "te", m_try)
            k.w(s, "dS", k.c["dS"])
            last = None
            for g in range(8):
                src = k.t["pt0"] if g < 4 else k.t["pt1"]
                last = s.activation(A4[:, 5184 + g * 128: 5184 + (g + 1) * 128],
                                    src[:, (g % 4) * 128:(g % 4) * 128 + 128], AF.Copy)
            k.inc(last, "se")
        m_ey = k.c["se"]

        @k.blk.sync
        def _(sy, c=c, m_ey=m_ey):
            k.w(sy, "se", m_ey)
            for g in range(8):
                k.inc(sy.dma_start(k.t["loc_yT"][g * 128:(g + 1) * 128, c * 128:(c + 1) * 128],
                                   A4[:, 5184 + g * 128: 5184 + (g + 1) * 128]), "dS", 16)

    m_spill = k.c["dS"]

    @k.blk.sync
    def _(sy):
        k.w(sy, "dS", m_spill)
        k.w(sy, "te", k.c["te"])
        k.inc(sy.dma_start(A1[:].rearrange("p (g t) -> p g t", t=2048),
                           k.t["loc_yT"].ap().rearrange("(g p) t -> p g t", p=128)), "dS", 16)
    m_yT = k.c["dS"]

    for t in range(NT):
        for ch in range(2):
            bank = pb[(t * 2 + ch) % 2]

            @k.blk.tensor
            def _(te, t=t, ch=ch, bank=bank, m_yT=m_yT):
                k.w(te, "dS", m_yT)
                k.w(te, "ve", k.c["ve"])
                last = None
                for gi in range(8):
                    last = te.matmul(bank[:],
                                     A1[:, gi * 2048 + t * 128: gi * 2048 + t * 128 + 128],
                                     A3[:, 8192 + gi * 1024 + ch * 512: 8192 + gi * 1024 + ch * 512 + 512],
                                     start=(gi == 0), stop=(gi == 7))
                k.inc(last, "te")
            m_mm = k.c["te"]

            @k.blk.vector
            def _(v, t=t, ch=ch, bank=bank, m_mm=m_mm):
                k.w(v, "te", m_mm)
                xt = k.t["xres"][:, t * 1024 + ch * 512: t * 1024 + ch * 512 + 512]
                k.inc(v.tensor_tensor(xt, xt, bank[:], ALU.add), "ve")


def emit_wo_phase(k, yT_name, w_name):
    nc = k.nc
    A1, A2 = k.t["A1"], k.t["A2"]
    pb = [k.t[f"pb{i}"] for i in range(6)]

    @k.blk.sync
    def _(sy):
        k.w(sy, "dS", k.c["dS"])
        k.w(sy, "dG", k.c["dG"])
        k.w(sy, "te", k.c["te"])
        k.inc(sy.dma_start(A1[:].rearrange("p (g t) -> p g t", t=2048),
                           k.t[yT_name].ap().rearrange("(g p) t -> p g t", p=128)), "dS", 16)

    @k.blk.gpsimd
    def _(g):
        k.w(g, "te", k.c["te"])
        k.inc(g.dma_start(A2[:, 0:8192].rearrange("p (g2 d) -> p g2 d", d=1024),
                          k.t[w_name].ap().rearrange("(g2 p) d -> p g2 d", p=128)), "dG", 16)
    m_ld_s, m_ld_g = k.c["dS"], k.c["dG"]

    for t in range(NT):
        for ch in range(2):
            bank = pb[(t * 2 + ch) % 2]

            @k.blk.tensor
            def _(te, t=t, ch=ch, bank=bank, m_ld_s=m_ld_s, m_ld_g=m_ld_g):
                if t == 0 and ch == 0:
                    k.w(te, "dS", m_ld_s)
                    k.w(te, "dG", m_ld_g)
                k.w(te, "ve", k.c["ve"])
                last = None
                for gi in range(8):
                    last = te.matmul(bank[:],
                                     A1[:, gi * 2048 + t * 128: gi * 2048 + t * 128 + 128],
                                     A2[:, gi * 1024 + ch * 512: gi * 1024 + ch * 512 + 512],
                                     start=(gi == 0), stop=(gi == 7))
                k.inc(last, "te")
            m_mm = k.c["te"]

            @k.blk.vector
            def _(v, t=t, ch=ch, bank=bank, m_mm=m_mm):
                k.w(v, "te", m_mm)
                xt = k.t["xres"][:, t * 1024 + ch * 512: t * 1024 + ch * 512 + 512]
                k.inc(v.tensor_tensor(xt, xt, bank[:], ALU.add), "ve")


def emit_mha_T(k, cfg):
    nc = k.nc
    A1, A2, A3, A4, A5, A6 = (k.t[n] for n in ("A1", "A2", "A3", "A4", "A5", "A6"))
    pb = [k.t[f"pb{i}"] for i in range(6)]
    SM = k.t["SM"]
    m_xnT = emit_ln_xnT(k)

    @k.blk.gpsimd
    def _(g):
        k.w(g, "te", k.c["te"])
        k.inc(g.dma_start(A2[:, 0:8192].rearrange("p (g2 d) -> p g2 d", d=1024),
                          k.t[cfg["wq"]].ap().rearrange("(g2 p) d -> p g2 d", p=128)), "dG", 16)
        k.inc(g.dma_start(A2[:, 8192:16384].rearrange("p (g2 d) -> p g2 d", d=1024),
                          k.t[cfg["wk"]].ap().rearrange("(g2 p) d -> p g2 d", p=128)), "dG", 16)
        k.inc(g.dma_start(A3[:, 0:8192].rearrange("p (g2 d) -> p g2 d", d=1024),
                          k.t[cfg["wv"]].ap().rearrange("(g2 p) d -> p g2 d", p=128)), "dG", 16)
    m_w = k.c["dG"]

    if cfg.get("cm"):
        @k.blk.gpsimd
        def _(g):
            k.w(g, "ve", k.c["ve"])
            k.inc(g.dma_start(k.t["cmask"][:], k.t[cfg["cm"]][:, :]), "dG", 16)

    @k.blk.vector
    def _(v):
        v.memset(SM[:, 300:301], -float(cfg["shiftC"]))
        k.inc(v.memset(SM[0:1, 336:400], 1.0), "ve")

    for t in range(NT):
        for which, wofs, warena, aofs in (("q", 0, A2, 0), ("k", 8192, A2, 1024), ("v", 0, A3, 2048)):
            for ch in range(2):
                bank = pb[ch % 2]

                @k.blk.tensor
                def _(te, t=t, ch=ch, bank=bank, wofs=wofs, warena=warena):
                    if t == 0 and wofs == 0 and ch == 0 and warena is A2:
                        k.w(te, "dG", m_w)
                        k.w(te, "ve", m_xnT)
                    k.w(te, "se", k.c["se"])
                    k.w(te, "ve", k.c["ve"])
                    last = None
                    for gi in range(8):
                        last = te.matmul(bank[:],
                                         A1[:, gi * 2048 + t * 128: gi * 2048 + t * 128 + 128],
                                         warena[:, wofs + gi * 1024 + ch * 512: wofs + gi * 1024 + ch * 512 + 512],
                                         start=(gi == 0), stop=(gi == 7))
                    k.inc(last, "te")
                m_mm = k.c["te"]

                @k.blk.scalar
                def _(s, t=t, ch=ch, bank=bank, m_mm=m_mm, aofs=aofs):
                    k.w(s, "te", m_mm)
                    k.w(s, "ve", k.c["ve"])
                    k.inc(s.activation(A4[:, aofs + ch * 512: aofs + ch * 512 + 512], bank[:], AF.Copy), "se")
        m_qkv = k.c["se"]

        if cfg.get("rope") == "il":
            @k.blk.sync
            def _(sy, t=t):
                k.w(sy, "ve", k.c["ve"])
                k.inc(sy.dma_start(A6[:, 0:64], k.t[cfg["rt"]][t * 128:(t + 1) * 128, :]), "dS", 16)
            m_rt = k.c["dS"]

            @k.blk.vector
            def _(v, t=t, m_qkv=m_qkv, m_rt=m_rt):
                k.w(v, "se", m_qkv)
                k.w(v, "dS", m_rt)
                last = None
                for base in (0, 1024):
                    x3 = A4[:, base:base + 1024].rearrange("p (h i) -> p h i", i=64)
                    xa = x3[:, :, 0:64:2]
                    xb = x3[:, :, 1:64:2]
                    t3 = A4[:, 3072:4096].rearrange("p (h i) -> p h i", i=64)
                    t1 = t3[:, :, 0:64:2]
                    t2 = t3[:, :, 1:64:2]
                    cb = A6[:, 0:32].rearrange("p (a r) -> p a r", a=1).broadcast_to([128, 16, 32])
                    sb = A6[:, 32:64].rearrange("p (a r) -> p a r", a=1).broadcast_to([128, 16, 32])
                    v.tensor_tensor(t1, xa, cb, ALU.mult)
                    v.tensor_tensor(t2, xb, sb, ALU.mult)
                    v.tensor_tensor(t1, t1, t2, ALU.subtract)
                    v.tensor_tensor(t2, xa, sb, ALU.mult)
                    v.tensor_tensor(xb, xb, cb, ALU.mult)
                    v.tensor_tensor(xb, t2, xb, ALU.add)
                    last = v.tensor_copy(xa, t1)
                k.inc(last, "ve")
            m_qkv = k.c["ve"]
            sem_qkv = "ve"
        else:
            sem_qkv = "se"

        for which2, aofs2, dst in (("q", 0, "loc_qT"), ("k", 1024, "loc_kT")):
            @k.blk.tensor
            def _(te, t=t, aofs2=aofs2, m_qkv=m_qkv, sem_qkv=sem_qkv):
                k.w(te, sem_qkv, m_qkv)
                k.w(te, "se", k.c["se"])
                last = None
                for g in range(8):
                    last = te.transpose(k.t["pt0"][:, (g % 4) * 128:(g % 4) * 128 + 128] if g < 4
                                        else k.t["pt1"][:, (g % 4) * 128:(g % 4) * 128 + 128],
                                        A4[:, aofs2 + g * 128: aofs2 + (g + 1) * 128], k.t["identb"][:])
                k.inc(last, "te")
            m_tr = k.c["te"]

            @k.blk.scalar
            def _(s, t=t, m_tr=m_tr):
                k.w(s, "te", m_tr)
                k.w(s, "dS", k.c["dS"])
                last = None
                for g in range(8):
                    src = k.t["pt0"] if g < 4 else k.t["pt1"]
                    last = s.activation(A4[:, 3072 + g * 128: 3072 + (g + 1) * 128],
                                        src[:, (g % 4) * 128:(g % 4) * 128 + 128], AF.Copy)
                k.inc(last, "se")
            m_ev = k.c["se"]

            @k.blk.sync
            def _(sy, t=t, m_ev=m_ev, dst=dst):
                k.w(sy, "se", m_ev)
                for g in range(8):
                    k.inc(sy.dma_start(k.t[dst][g * 128:(g + 1) * 128, t * 128:(t + 1) * 128],
                                       A4[:, 3072 + g * 128: 3072 + (g + 1) * 128]), "dS", 16)

        @k.blk.sync
        def _(sy, t=t, m_qkv=m_qkv, sem_qkv=sem_qkv):
            k.w(sy, sem_qkv, m_qkv)
            k.w(sy, "se", k.c["se"])
            k.inc(sy.dma_start(k.t["loc_v"][t * 128:(t + 1) * 128, :], A4[:, 2048:3072]), "dS", 16)

    m_spill = k.c["dS"]

    @k.blk.sync
    def _(sy):
        k.w(sy, "dS", m_spill)
        k.w(sy, "te", k.c["te"])
        k.inc(sy.dma_start(A1[:].rearrange("p (g t) -> p g t", t=2048),
                           k.t["loc_kT"].ap().rearrange("(g p) t -> p g t", p=128)), "dS", 16)
        k.inc(sy.dma_start(A2[:].rearrange("p (g t) -> p g t", t=2048),
                           k.t["loc_qT"].ap().rearrange("(g p) t -> p g t", p=128)), "dS", 16)
    m_kq = k.c["dS"]

    vp_t = lambda t: (A3[:, t * 1040:(t + 1) * 1040] if t < 15
                      else A5[:, 0:520].bitcast(BF16))
    for t in range(NT):
        @k.blk.sync
        def _(sy, t=t):
            if t == 0:
                k.w(sy, "te", k.c["te"])
            k.w(sy, "ve", k.c["ve"])
            k.inc(sy.dma_start(A4[:, 4096 + (t % 2) * 1024: 4096 + (t % 2) * 1024 + 1024],
                               k.t["loc_v"][t * 128:(t + 1) * 128, :]), "dS", 16)
        m_vl = k.c["dS"]

        @k.blk.vector
        def _(v, t=t, m_vl=m_vl):
            k.w(v, "dS", m_vl)
            vp = vp_t(t)
            v.memset(vp, 1.0)
            k.inc(v.tensor_copy(vp.rearrange("p (h i) -> p h i", i=65)[:, :, 0:64],
                                A4[:, 4096 + (t % 2) * 1024: 4096 + (t % 2) * 1024 + 1024]
                                .rearrange("p (h i) -> p h i", i=64)), "ve")
    m_vp = k.c["ve"]

    kind = cfg["kind"]
    for ic in range(4):
        if kind == "causal":
            jts = list(range(0, 4 * ic + 4))
        elif kind == "window":
            jts = [j for j in range(16) if (4 * ic - 2) <= j <= (4 * ic + 5)]
        else:
            jts = list(range(16))
        for h in range(16):
            g = h // 2
            hp = (h % 2) * 64
            kT_h = lambda jt, g=g, hp=hp: A1[:, g * 2048 + jt * 128: g * 2048 + (jt + 1) * 128][hp:hp + 64, :]
            qT_h = A2[:, g * 2048 + ic * 512: g * 2048 + (ic + 1) * 512][hp:hp + 64, :]

            for jj, jt in enumerate(jts):
                bank = pb[jj % 2]

                @k.blk.tensor
                def _(te, ic=ic, h=h, jt=jt, bank=bank, kT_h=kT_h, qT_h=qT_h):
                    if ic == 0 and h == 0 and jt == jts[0]:
                        k.w(te, "dS", m_kq)
                        k.w(te, "ve", m_vp)
                    k.w(te, "se", k.c["se"])
                    k.inc(te.matmul(bank[:], kT_h(jt), qT_h, start=True, stop=True), "te")
                m_mm = k.c["te"]

                delta = jt * 128 - ic * 512
                pat = cfg.get("cm_pats", {}).get(delta) if cfg.get("cm") else None
                if pat is not None:
                    @k.blk.vector
                    def _(v, bank=bank, m_mm=m_mm, pat=pat):
                        k.w(v, "te", m_mm)
                        k.w(v, "dG", k.c["dG"])
                        k.inc(v.tensor_tensor(bank[:], bank[:], k.t["cmask"][:, pat * 512:(pat + 1) * 512],
                                              ALU.add), "ve")
                    m_mm = ("ve", k.c["ve"])
                else:
                    m_mm = ("te", m_mm)

                @k.blk.scalar
                def _(s, ic=ic, h=h, jt=jt, bank=bank, m_mm=m_mm):
                    k.w(s, m_mm[0], m_mm[1])
                    k.inc(s.activation(A4[:, jt * 512:(jt + 1) * 512], bank[:], AF.Exp,
                                       bias=SM[:, 300:301], scale=float(cfg["scale"])), "se")
            m_e = k.c["se"]

            @k.blk.tensor
            def _(te, ic=ic, h=h, m_e=m_e):
                k.w(te, "se", m_e)
                k.w(te, "ve", k.c["ve"])
                last = None
                for jj, jt in enumerate(jts):
                    last = te.matmul(pb[2][0:65, :], vp_t(jt)[:, h * 65:(h + 1) * 65],
                                     A4[:, jt * 512:(jt + 1) * 512],
                                     start=(jj == 0), stop=(jj == len(jts) - 1))
                k.inc(last, "te")
            m_o = k.c["te"]

            @k.blk.vector
            def _(v, m_o=m_o):
                k.w(v, "te", m_o)
                m = k.inc(v.tensor_scalar_max(A6[0:1, 0:512], pb[2][64:65, :], 1e-30), "ve")
                v.wait_ge(k.s["ve"], m)
                k.inc(v.reciprocal(A6[0:1, 0:512], A6[0:1, 0:512]), "ve")
            m_rc = k.c["ve"]

            @k.blk.tensor
            def _(te, m_rc=m_rc):
                k.w(te, "ve", m_rc)
                k.w(te, "se", k.c["se"])
                k.inc(te.matmul(pb[3][0:64, :], SM[0:1, 336:400], A6[0:1, 0:512],
                                start=True, stop=True), "te")
            m_rb = k.c["te"]

            @k.blk.scalar
            def _(s, m_rb=m_rb):
                k.w(s, "te", m_rb)
                k.w(s, "ve", k.c["ve"])
                k.inc(s.activation(A6[0:64, 0:512], pb[3][0:64, :], AF.Copy), "se")
            m_rbe = k.c["se"]

            @k.blk.vector
            def _(v, m_rbe=m_rbe):
                k.w(v, "se", m_rbe)
                k.w(v, "dG", k.c["dG"])
                k.inc(v.tensor_tensor(A6[0:64, 512:1024], pb[2][0:64, :], A6[0:64, 0:512],
                                      ALU.mult), "ve")
            m_on = k.c["ve"]

            @k.blk.gpsimd
            def _(gp, ic=ic, h=h, m_on=m_on):
                k.w(gp, "ve", m_on)
                k.inc(gp.dma_start(k.t["loc_yT"][h * 64:(h + 1) * 64, ic * 512:(ic + 1) * 512],
                                   A6[0:64, 512:1024]), "dG", 16)

    emit_wo_phase(k, "loc_yT", cfg["wo"])


def emit_fp4qd(k, src, dst, W, n_ap, a1_ap, sg_ap, gsc, gsc2, m_src, src_sem):
    G = W // 16

    @k.blk.scalar
    def _(sc):
        k.w(sc, src_sem, m_src)
        k.inc(sc.activation(sg_ap, src, AF.Sign), "se")
    m_sg = k.c["se"]

    @k.blk.vector
    def _(v):
        k.w(v, src_sem, m_src)
        def ss(inst):
            m = k.inc(inst, "ve")
            v.wait_ge(k.s["ve"], m)
        src3 = src.rearrange("p (g q) -> p g q", q=16)
        ss(v.tensor_reduce(gsc.rearrange("p (g q) -> p g q", q=1), src3, axis=AX.X,
                           op=ALU.max, apply_absolute_value=True))
        ss(v.tensor_scalar(gsc, gsc, 1.0 / 6.0, 1e-12, ALU.mult, ALU.max))
        ss(v.reciprocal(gsc2, gsc))
        rb = gsc2.rearrange("p (g q) -> p g q", q=1).broadcast_to([128, G, 16])
        sb = gsc.rearrange("p (g q) -> p g q", q=1).broadcast_to([128, G, 16])
        n3 = n_ap.rearrange("p (g q) -> p g q", q=16)
        a13 = a1_ap.rearrange("p (g q) -> p g q", q=16)
        v.tensor_tensor(n3, src3, rb, ALU.mult)
        v.tensor_scalar_mul(a1_ap, n_ap, -1.0)
        v.tensor_tensor(n_ap, n_ap, a1_ap, ALU.max)
        v.tensor_single_scalar(a1_ap, n_ap, 0.25, ALU.is_gt)
        for mj in (0.75, 1.25, 1.75):
            v.scalar_tensor_tensor(a1_ap, n_ap, mj, a1_ap, ALU.is_gt, ALU.add)
        v.tensor_single_scalar(dst, n_ap, 2.5, ALU.is_gt)
        v.scalar_tensor_tensor(dst, n_ap, 3.5, dst, ALU.is_gt, ALU.add)
        v.scalar_tensor_tensor(a1_ap, a1_ap, 0.5, dst, ALU.mult, ALU.add)
        v.tensor_single_scalar(n_ap, n_ap, 5.0, ALU.is_gt)
        v.scalar_tensor_tensor(n_ap, n_ap, 2.0, a1_ap, ALU.mult, ALU.add)
        n3b = n_ap.rearrange("p (g q) -> p g q", q=16)
        v.tensor_tensor(n3b, n3b, sb, ALU.mult)
        k.w(v, "se", m_sg)
        k.inc(v.tensor_tensor(dst, n_ap, sg_ap, ALU.mult), "ve")
    return ("ve", k.c["ve"])


def emit_l1_ffn(k, first_call):
    nc = k.nc
    A1, A2, A3, A4, A5, A6 = (k.t[n] for n in ("A1", "A2", "A3", "A4", "A5", "A6"))
    pb = [k.t[f"pb{i}"] for i in range(6)]
    SM = k.t["SM"]
    ptb = [k.t["pt0"], k.t["pt1"]]

    def transpose_spill(src_bf16_off, n_tiles, dst_name, dst_row0, dst_col0, m_ready, ready_sem, tag):
        for i0 in range(0, n_tiles, 8):
            nn = min(8, n_tiles - i0)

            @k.blk.tensor
            def _(te, i0=i0, nn=nn, m_ready=m_ready, ready_sem=ready_sem):
                k.w(te, ready_sem, m_ready)
                k.w(te, "se", k.c["se"])
                last = None
                for j in range(nn):
                    g = j
                    last = te.transpose(ptb[g // 4][:, (g % 4) * 128:(g % 4) * 128 + 128],
                                        A4[:, src_bf16_off + (i0 + j) * 128: src_bf16_off + (i0 + j + 1) * 128],
                                        k.t["identb"][:])
                k.inc(last, "te")
            m_tr = k.c["te"]

            @k.blk.scalar
            def _(s, i0=i0, nn=nn, m_tr=m_tr):
                k.w(s, "te", m_tr)
                k.w(s, "dS", k.c["dS"])
                last = None
                for j in range(nn):
                    src = ptb[j // 4]
                    last = s.activation(A4[:, 6400 + j * 128: 6400 + (j + 1) * 128],
                                        src[:, (j % 4) * 128:(j % 4) * 128 + 128], AF.Copy)
                k.inc(last, "se")
            m_ev = k.c["se"]

            @k.blk.sync
            def _(sy, i0=i0, nn=nn, m_ev=m_ev):
                k.w(sy, "se", m_ev)
                for j in range(nn):
                    k.inc(sy.dma_start(
                        k.t[dst_name][dst_row0 + (i0 + j) * 128: dst_row0 + (i0 + j + 1) * 128,
                                      dst_col0:dst_col0 + 128],
                        A4[:, 6400 + j * 128: 6400 + (j + 1) * 128]), "dS", 16)

    if first_call:
        for wname, dq, nrt, wcols in (("w_l1f_gate", "dq_gateT", 22, 1024),
                                      ("w_l1f_up", "dq_upT", 22, 1024)):
            for rt in range(nrt):
                @k.blk.sync
                def _(sy, wname=wname, rt=rt):
                    k.w(sy, "ve", k.c["ve"])
                    k.w(sy, "te", k.c["te"])
                    k.inc(sy.dma_start(A5[:, 0:1024], k.t[wname][rt * 128:(rt + 1) * 128, :]), "dS", 16)
                m_ld = k.c["dS"]
                mq = emit_fp4qd(k, A5[:, 0:1024], A4[:, 0:1024], 1024,
                                A5[:, 1024:2048], A5[:, 2048:3072], A4[:, 1024:2048],
                                A6[:, 0:64], A6[:, 64:128], m_ld, "dS")
                transpose_spill(0, 8, dq, 0, rt * 128, mq[1], mq[0], f"{wname}{rt}")
        for rt in range(8):
            for chq in range(4):
                @k.blk.sync
                def _(sy, rt=rt, chq=chq):
                    k.w(sy, "ve", k.c["ve"])
                    k.w(sy, "te", k.c["te"])
                    k.inc(sy.dma_start(A5[:, 0:704],
                                       k.t["w_l1f_down"][rt * 128:(rt + 1) * 128, chq * 704:(chq + 1) * 704]), "dS", 16)
                m_ld = k.c["dS"]
                mq = emit_fp4qd(k, A5[:, 0:704], A4[:, chq * 704:(chq + 1) * 704], 704,
                                A5[:, 704:1408], A5[:, 1408:2112], A4[:, 2816 + chq * 704:2816 + (chq + 1) * 704],
                                A6[:, 0:44], A6[:, 64:108], m_ld, "dS")
            transpose_spill(0, 22, "dq_downT", 0, rt * 128, mq[1], mq[0], f"down{rt}")

    m_nmr = emit_ln_stats(k)
    for t in range(NT):
        @k.blk.scalar
        def _(s, t=t):
            if t == 0:
                k.w(s, "ve", m_nmr)
            k.w(s, "ve", k.c["ve"])
            k.w(s, "te", k.c["te"])
            k.inc(s.activation(A5[:, 0:1024], k.t["xres"][:, t * 1024:(t + 1) * 1024], AF.Identity,
                               bias=SM[:, 80 + t:81 + t], scale=SM[:, 64 + t:65 + t]), "se")
        m_xn = k.c["se"]
        mq = emit_fp4qd(k, A5[:, 0:1024], A4[:, 0:1024], 1024,
                        A5[:, 1024:2048], A5[:, 2048:3072], A4[:, 1024:2048],
                        A6[:, 0:64], A6[:, 64:128], m_xn, "se")
        transpose_spill(0, 8, "loc_xqT", 0, t * 128, mq[1], mq[0], f"xq{t}")

    for half in range(2):
        c0 = half * 1408

        @k.blk.gpsimd
        def _(g, half=half):
            k.w(g, "te", k.c["te"])
            k.w(g, "dS", k.c["dS"])
            k.inc(g.dma_start(A2[:, 0:11264].rearrange("p (g2 d) -> p g2 d", d=1408),
                              k.t["dq_gateT"][0:1024, c0:c0 + 1408].rearrange("(g2 p) d -> p g2 d", p=128)), "dG", 16)
            k.inc(g.dma_start(A3[:, 0:11264].rearrange("p (g2 d) -> p g2 d", d=1408),
                              k.t["dq_upT"][0:1024, c0:c0 + 1408].rearrange("(g2 p) d -> p g2 d", p=128)), "dG", 16)
        m_w = k.c["dG"]

        for t in range(NT):
            @k.blk.sync
            def _(sy, t=t):
                k.w(sy, "te", k.c["te"])
                k.inc(sy.dma_start(A4[:, 1024:2048].rearrange("p (g2 d) -> p g2 d", d=128),
                                   k.t["loc_xqT"].ap().rearrange("(g2 p) s -> p g2 s", p=128)
                                   [:, :, t * 128:(t + 1) * 128]), "dS", 16)
            m_xqT = k.c["dS"]

            for which, warena, pbase in (("g", A2, 0), ("u", A3, 3)):
                for ci, (o0, w0) in enumerate(((0, 512), (512, 512), (1024, 384))):
                    bank = pb[pbase + ci]

                    @k.blk.tensor
                    def _(te, t=t, warena=warena, bank=bank, o0=o0, w0=w0, m_xqT=m_xqT):
                        k.w(te, "dG", m_w)
                        k.w(te, "dS", m_xqT)
                        k.w(te, "ve", k.c["ve"])
                        k.w(te, "se", k.c["se"])
                        last = None
                        for gi in range(8):
                            last = te.matmul(bank[:, 0:w0],
                                             A4[:, 1024 + gi * 128: 1024 + (gi + 1) * 128],
                                             warena[:, gi * 1408 + o0: gi * 1408 + o0 + w0],
                                             start=(gi == 0), stop=(gi == 7))
                        k.inc(last, "te")
            m_gu = k.c["te"]

            @k.blk.scalar
            def _(s, t=t, m_gu=m_gu):
                k.w(s, "te", m_gu)
                k.w(s, "ve", k.c["ve"])
                s.activation(A5[:, 0:512], pb[0][:], AF.Silu)
                s.activation(A5[:, 512:1024], pb[1][:], AF.Silu)
                k.inc(s.activation(A5[:, 1024:1408], pb[2][:, 0:384], AF.Silu), "se")
            m_sg = k.c["se"]

            @k.blk.vector
            def _(v, t=t, m_sg=m_sg):
                k.w(v, "se", m_sg)
                v.tensor_tensor(A5[:, 1408:1920], A5[:, 0:512], pb[3][:], ALU.mult)
                v.tensor_tensor(A5[:, 1920:2432], A5[:, 512:1024], pb[4][:], ALU.mult)
                k.inc(v.tensor_tensor(A5[:, 2432:2816], A5[:, 1024:1408], pb[5][:, 0:384], ALU.mult), "ve")
            m_h = k.c["ve"]

            for chq in range(2):
                mq = emit_fp4qd(k, A5[:, 1408 + chq * 704:1408 + (chq + 1) * 704],
                                A4[:, 2048 + chq * 704:2048 + (chq + 1) * 704], 704,
                                A5[:, chq * 704:(chq + 1) * 704],
                                A5[:, (1 - chq) * 704:(2 - chq) * 704],
                                A4[:, 3456 + chq * 704:3456 + (chq + 1) * 704],
                                A6[:, 0:44], A6[:, 64:108], m_h, "ve")
            transpose_spill(2048, 11, "loc_hqT", c0, t * 128, mq[1], mq[0], f"hq{t}")

        @k.blk.gpsimd
        def _(g, half=half):
            k.w(g, "te", k.c["te"])
            k.w(g, "dS", k.c["dS"])
            k.inc(g.dma_start(A2[:, 0:11264].rearrange("p (g2 d) -> p g2 d", d=1024),
                              k.t["dq_downT"][c0:c0 + 1408, :].rearrange("(g2 p) d -> p g2 d", p=128)), "dG", 16)
        m_wd = k.c["dG"]

        @k.blk.sync
        def _(sy, half=half):
            k.w(sy, "dS", k.c["dS"])
            k.w(sy, "te", k.c["te"])
            k.inc(sy.dma_start(A1[:].rearrange("p (g2 s2) -> p g2 s2", s2=2048),
                               k.t["loc_hqT"][c0:c0 + 1024, :].rearrange("(g2 p) s2 -> p g2 s2", p=128)), "dS", 16)
            k.inc(sy.dma_start(A3[:, 0:6144].rearrange("p (g2 s2) -> p g2 s2", s2=2048),
                               k.t["loc_hqT"][c0 + 1024:c0 + 1408, :].rearrange("(g2 p) s2 -> p g2 s2", p=128)), "dS", 16)
        m_hT = k.c["dS"]

        for t in range(NT):
            for ch in range(2):
                bank = pb[(t * 2 + ch) % 2]

                @k.blk.tensor
                def _(te, t=t, ch=ch, bank=bank, m_hT=m_hT, m_wd=m_wd):
                    k.w(te, "dS", m_hT)
                    k.w(te, "dG", m_wd)
                    k.w(te, "ve", k.c["ve"])
                    last = None
                    for gi in range(11):
                        src = (A1[:, gi * 2048 + t * 128: gi * 2048 + t * 128 + 128] if gi < 8
                               else A3[:, (gi - 8) * 2048 + t * 128: (gi - 8) * 2048 + t * 128 + 128])
                        last = te.matmul(bank[:],
                                         src,
                                         A2[:, gi * 1024 + ch * 512: gi * 1024 + ch * 512 + 512],
                                         start=(gi == 0), stop=(gi == 10))
                    k.inc(last, "te")
                m_mm = k.c["te"]

                @k.blk.vector
                def _(v, t=t, ch=ch, bank=bank, m_mm=m_mm):
                    k.w(v, "te", m_mm)
                    xt = k.t["xres"][:, t * 1024 + ch * 512: t * 1024 + ch * 512 + 512]
                    k.inc(v.tensor_tensor(xt, xt, bank[:], ALU.add), "ve")


def emit_l1_attn_b(k):
    nc = k.nc
    A1, A2, A3, A4, A5, A6 = (k.t[n] for n in ("A1", "A2", "A3", "A4", "A5", "A6"))
    pb = [k.t[f"pb{i}"] for i in range(6)]
    SM = k.t["SM"]
    ptb = [k.t["pt0"], k.t["pt1"]]
    A6b = A6[:].bitcast(BF16)
    m_nmr = emit_ln_stats(k)

    for t in range(NT):
        @k.blk.scalar
        def _(s, t=t):
            if t == 0:
                k.w(s, "ve", m_nmr)
            k.w(s, "ve", k.c["ve"])
            k.w(s, "te", k.c["te"])
            m = k.inc(s.activation(A5[:, 0:1024], k.t["xres"][:, t * 1024:(t + 1) * 1024], AF.Identity,
                                   bias=SM[:, 80 + t:81 + t], scale=SM[:, 64 + t:65 + t]), "se")
            s.wait_ge(k.s["se"], m)
            k.inc(s.activation(A4[:, 0:1024], A5[:, 0:1024], AF.Copy), "se")
        m_h = k.c["se"]

        @k.blk.vector
        def _(v, t=t, m_h=m_h):
            k.w(v, "se", m_h)
            k.inc(v.tensor_tensor(A4[:, 1024:2048], A5[:, 0:1024], A4[:, 0:1024], ALU.subtract), "ve")
        m_l = k.c["ve"]

        for src_off, dstname, dst_arena in ((0, "h", A1), (1024, "l", A3)):
            @k.blk.tensor
            def _(te, t=t, src_off=src_off, m_l=m_l):
                k.w(te, "ve", m_l)
                k.w(te, "se", k.c["se"])
                k.w(te, "ve", k.c["ve"])
                last = None
                for g in range(8):
                    last = te.transpose(ptb[g // 4][:, (g % 4) * 128:(g % 4) * 128 + 128],
                                        A4[:, src_off + g * 128: src_off + (g + 1) * 128], k.t["identb"][:])
                k.inc(last, "te")
            m_tr = k.c["te"]

            @k.blk.vector
            def _(v, t=t, m_tr=m_tr, dst_arena=dst_arena):
                k.w(v, "te", m_tr)
                last = None
                for g in range(8):
                    last = v.tensor_copy(dst_arena[:, g * 2048 + t * 128: g * 2048 + t * 128 + 128],
                                         ptb[g // 4][:, (g % 4) * 128:(g % 4) * 128 + 128])
                k.inc(last, "ve")
    m_xsplit = k.c["ve"]

    for wname, dh, dl in (("w_l1ab_q", "loc_qT", "loc_qTl"), ("w_l1ab_k", "loc_kT", "loc_kTl")):
        for gi in range(8):
            @k.blk.sync
            def _(sy, wname=wname, gi=gi):
                k.w(sy, "te", k.c["te"])
                k.w(sy, "se", k.c["se"])
                k.inc(sy.dma_start(A5[:, 0:1024], k.t[wname][gi * 128:(gi + 1) * 128, :]), "dS", 16)
            m_wl = k.c["dS"]

            @k.blk.scalar
            def _(s, gi=gi, m_wl=m_wl):
                k.w(s, "dS", m_wl)
                k.inc(s.activation(A2[:, gi * 1024:(gi + 1) * 1024], A5[:, 0:1024], AF.Copy), "se")
            m_wh = k.c["se"]

            @k.blk.vector
            def _(v, gi=gi, m_wh=m_wh):
                k.w(v, "se", m_wh)
                k.inc(v.tensor_tensor(A2[:, 8192 + gi * 1024: 8192 + (gi + 1) * 1024],
                                      A5[:, 0:1024], A2[:, gi * 1024:(gi + 1) * 1024], ALU.subtract), "ve")
        m_wpair = k.c["ve"]

        for gout in range(8):
            for tc in range(4):
                bank = pb[(gout * 4 + tc) % 2]

                @k.blk.tensor
                def _(te, gout=gout, tc=tc, bank=bank, m_wpair=m_wpair):
                    k.w(te, "ve", m_wpair)
                    k.w(te, "se", k.c["se"])
                    last = None
                    for gi in range(8):
                        wh = A2[:, gi * 1024 + gout * 128: gi * 1024 + gout * 128 + 128]
                        wl = A2[:, 8192 + gi * 1024 + gout * 128: 8192 + gi * 1024 + gout * 128 + 128]
                        xh = A1[:, gi * 2048 + tc * 512: gi * 2048 + tc * 512 + 512]
                        xl = A3[:, gi * 2048 + tc * 512: gi * 2048 + tc * 512 + 512]
                        te.matmul(bank[:], wh, xh, start=(gi == 0), stop=False)
                        te.matmul(bank[:], wl, xh, start=False, stop=False)
                        last = te.matmul(bank[:], wh, xl, start=False, stop=(gi == 7))
                    k.inc(last, "te")
                m_mm = k.c["te"]

                @k.blk.scalar
                def _(s, gout=gout, tc=tc, bank=bank, m_mm=m_mm):
                    k.w(s, "te", m_mm)
                    k.w(s, "dS", k.c["dS"])
                    k.w(s, "ve", k.c["ve"])
                    k.inc(s.activation(A4[:, 2048:2560], bank[:], AF.Copy), "se")
                m_evh = k.c["se"]

                @k.blk.vector
                def _(v, gout=gout, tc=tc, bank=bank, m_evh=m_evh):
                    k.w(v, "se", m_evh)
                    k.inc(v.tensor_tensor(A4[:, 2560:3072], bank[:], A4[:, 2048:2560], ALU.subtract), "ve")
                m_evl = k.c["ve"]

                @k.blk.sync
                def _(sy, gout=gout, tc=tc, m_evl=m_evl, dh=dh, dl=dl):
                    k.w(sy, "ve", m_evl)
                    k.inc(sy.dma_start(k.t[dh][gout * 128:(gout + 1) * 128, tc * 512:(tc + 1) * 512],
                                       A4[:, 2048:2560]), "dS", 16)
                    k.inc(sy.dma_start(k.t[dl][gout * 128:(gout + 1) * 128, tc * 512:(tc + 1) * 512],
                                       A4[:, 2560:3072]), "dS", 16)

    @k.blk.gpsimd
    def _(g):
        k.w(g, "te", k.c["te"])
        k.inc(g.dma_start(A2[:, 0:8192].rearrange("p (g2 d) -> p g2 d", d=1024),
                          k.t["w_l1ab_v"].ap().rearrange("(g2 p) d -> p g2 d", p=128)), "dG", 16)
    m_wv = k.c["dG"]

    for t in range(NT):
        for ch in range(2):
            bank = pb[2 + (t * 2 + ch) % 2]

            @k.blk.tensor
            def _(te, t=t, ch=ch, bank=bank, m_wv=m_wv):
                k.w(te, "dG", m_wv)
                k.w(te, "se", k.c["se"])
                last = None
                for gi in range(8):
                    last = te.matmul(bank[:],
                                     A1[:, gi * 2048 + t * 128: gi * 2048 + t * 128 + 128],
                                     A2[:, gi * 1024 + ch * 512: gi * 1024 + ch * 512 + 512],
                                     start=(gi == 0), stop=(gi == 7))
                k.inc(last, "te")
            m_mm = k.c["te"]

            @k.blk.scalar
            def _(s, t=t, ch=ch, bank=bank, m_mm=m_mm):
                k.w(s, "te", m_mm)
                k.w(s, "dS", k.c["dS"])
                dst2 = A4[:, 2048 + ((t * 2 + ch) % 2) * 512: 2048 + ((t * 2 + ch) % 2) * 512 + 512]
                k.inc(s.activation(dst2, bank[:], AF.Copy), "se")
            m_ev = k.c["se"]

            @k.blk.sync
            def _(sy, t=t, ch=ch, m_ev=m_ev):
                k.w(sy, "se", m_ev)
                src = A4[:, 2048 + ((t * 2 + ch) % 2) * 512: 2048 + ((t * 2 + ch) % 2) * 512 + 512]
                k.inc(sy.dma_start(k.t["loc_v"][t * 128:(t + 1) * 128, ch * 512:(ch + 1) * 512], src), "dS", 16)
    m_spill = k.c["dS"]

    @k.blk.sync
    def _(sy):
        k.w(sy, "dS", m_spill)
        k.w(sy, "te", k.c["te"])
        k.inc(sy.dma_start(A1[:].rearrange("p (g t2) -> p g t2", t2=2048),
                           k.t["loc_kT"].ap().rearrange("(g p) t2 -> p g t2", p=128)), "dS", 16)
        k.inc(sy.dma_start(A3[:].rearrange("p (g t2) -> p g t2", t2=2048),
                           k.t["loc_kTl"].ap().rearrange("(g p) t2 -> p g t2", p=128)), "dS", 16)
    m_kq = k.c["dS"]

    for qt in range(NT):
        @k.blk.sync
        def _(sy, qt=qt):
            if qt == 0:
                k.w(sy, "dS", m_kq)
            k.w(sy, "te", k.c["te"])
            for g in range(8):
                k.inc(sy.dma_start(A6b[:, g * 128:(g + 1) * 128],
                                   k.t["loc_qT"][g * 128:(g + 1) * 128, qt * 128:(qt + 1) * 128]), "dS", 16)
                k.inc(sy.dma_start(A6b[:, 1024 + g * 128: 1024 + (g + 1) * 128],
                                   k.t["loc_qTl"][g * 128:(g + 1) * 128, qt * 128:(qt + 1) * 128]), "dS", 16)
        m_q = k.c["dS"]

        for jc in range(4):
            bank = pb[jc]

            @k.blk.tensor
            def _(te, qt=qt, jc=jc, bank=bank, m_q=m_q):
                k.w(te, "dS", m_q)
                k.w(te, "se", k.c["se"])
                k.w(te, "ve", k.c["ve"])
                last = None
                for gi in range(8):
                    qh = A6b[:, gi * 128:(gi + 1) * 128]
                    ql = A6b[:, 1024 + gi * 128: 1024 + (gi + 1) * 128]
                    kh = A1[:, gi * 2048 + jc * 512: gi * 2048 + (jc + 1) * 512]
                    kl = A3[:, gi * 2048 + jc * 512: gi * 2048 + (jc + 1) * 512]
                    te.matmul(bank[:], qh, kh, start=(gi == 0), stop=False)
                    te.matmul(bank[:], qh, kl, start=False, stop=False)
                    last = te.matmul(bank[:], ql, kh, start=False, stop=(gi == 7))
                k.inc(last, "te")
        m_sc = k.c["te"]

        @k.blk.vector
        def _(v, qt=qt, m_sc=m_sc):
            k.w(v, "te", m_sc)
            def ss(inst):
                m = k.inc(inst, "ve")
                v.wait_ge(k.s["ve"], m)
            for jc in range(4):
                ss(v.reduce_max(SM[:, 304 + jc:305 + jc], pb[jc][:], axis=AX.X))
            ss(v.tensor_tensor(SM[:, 308:309], SM[:, 304:305], SM[:, 305:306], ALU.max))
            ss(v.tensor_tensor(SM[:, 309:310], SM[:, 306:307], SM[:, 307:308], ALU.max))
            ss(v.tensor_tensor(SM[:, 308:309], SM[:, 308:309], SM[:, 309:310], ALU.max))
            k.inc(v.tensor_scalar_mul(SM[:, 310:311], SM[:, 308:309], -8.0), "ve")
        m_max = k.c["ve"]

        @k.blk.scalar
        def _(s, qt=qt, m_max=m_max):
            k.w(s, "ve", m_max)
            last = None
            for jc in range(4):
                last = s.activation(A4[:, jc * 512:(jc + 1) * 512], pb[jc][:], AF.Exp,
                                    bias=SM[:, 310:311], scale=8.0,
                                    accum_out=SM[:, 312 + jc:313 + jc])
            k.inc(last, "se")
        m_e = k.c["se"]

        @k.blk.vector
        def _(v, qt=qt, m_e=m_e):
            k.w(v, "se", m_e)
            def ss(inst):
                m = k.inc(inst, "ve")
                v.wait_ge(k.s["ve"], m)
            ss(v.tensor_tensor(SM[:, 316:317], SM[:, 312:313], SM[:, 313:314], ALU.add))
            ss(v.tensor_tensor(SM[:, 317:318], SM[:, 314:315], SM[:, 315:316], ALU.add))
            ss(v.tensor_tensor(SM[:, 316:317], SM[:, 316:317], SM[:, 317:318], ALU.add))
            k.inc(v.reciprocal(SM[:, 318:319], SM[:, 316:317]), "ve")
        m_den = k.c["ve"]

        for r in range(2):
            @k.blk.tensor
            def _(te, qt=qt, r=r, m_e=m_e):
                k.w(te, "se", k.c["se"])
                k.w(te, "ve", k.c["ve"])
                last = None
                for j in range(8):
                    jt = r * 8 + j
                    last = te.transpose(ptb[j // 4][:, (j % 4) * 128:(j % 4) * 128 + 128],
                                        A4[:, jt * 128:(jt + 1) * 128], k.t["identb"][:])
                k.inc(last, "te")
            m_tr = k.c["te"]

            @k.blk.vector
            def _(v, qt=qt, r=r, m_tr=m_tr):
                k.w(v, "te", m_tr)
                last = None
                for j in range(8):
                    jt = r * 8 + j
                    last = v.tensor_copy(A4[:, 4096 + jt * 128: 4096 + (jt + 1) * 128],
                                         ptb[j // 4][:, (j % 4) * 128:(j % 4) * 128 + 128])
                k.inc(last, "ve")
        m_at = k.c["ve"]

        for jt in range(NT):
            @k.blk.sync
            def _(sy, qt=qt, jt=jt):
                k.w(sy, "te", k.c["te"])
                k.inc(sy.dma_start(A4[:, 6144 + (jt % 2) * 1024: 6144 + (jt % 2) * 1024 + 1024],
                                   k.t["loc_v"][jt * 128:(jt + 1) * 128, :]), "dS", 16)
            m_vl = k.c["dS"]

            @k.blk.tensor
            def _(te, qt=qt, jt=jt, m_vl=m_vl, m_at=m_at):
                k.w(te, "dS", m_vl)
                k.w(te, "ve", m_at)
                if jt == 0:
                    k.w(te, "ve", k.c["ve"])
                last = None
                for ch in range(2):
                    last = te.matmul(pb[4 + ch][:],
                                     A4[:, 4096 + jt * 128: 4096 + (jt + 1) * 128],
                                     A4[:, 6144 + (jt % 2) * 1024 + ch * 512: 6144 + (jt % 2) * 1024 + ch * 512 + 512],
                                     start=(jt == 0), stop=(jt == 15))
                k.inc(last, "te")
        m_out = k.c["te"]

        @k.blk.vector
        def _(v, qt=qt, m_out=m_out):
            k.w(v, "te", m_out)
            last = None
            for ch in range(2):
                last = v.tensor_scalar(A4[:, 2048 + ch * 512: 2048 + ch * 512 + 512],
                                       pb[4 + ch][:], SM[:, 318:319], None, ALU.mult)
            k.inc(last, "ve")
        m_on = k.c["ve"]

        @k.blk.tensor
        def _(te, qt=qt, m_on=m_on):
            k.w(te, "ve", m_on)
            k.w(te, "se", k.c["se"])
            last = None
            for g in range(8):
                last = te.transpose(ptb[g // 4][:, (g % 4) * 128:(g % 4) * 128 + 128],
                                    A4[:, 2048 + g * 128: 2048 + (g + 1) * 128], k.t["identb"][:])
            k.inc(last, "te")
        m_oT = k.c["te"]

        @k.blk.scalar
        def _(s, qt=qt, m_oT=m_oT):
            k.w(s, "te", m_oT)
            k.w(s, "dS", k.c["dS"])
            last = None
            for g in range(8):
                src = ptb[g // 4]
                last = s.activation(A4[:, 3072 + g * 128: 3072 + (g + 1) * 128],
                                    src[:, (g % 4) * 128:(g % 4) * 128 + 128], AF.Copy)
            k.inc(last, "se")
        m_oe = k.c["se"]

        @k.blk.sync
        def _(sy, qt=qt, m_oe=m_oe):
            k.w(sy, "se", m_oe)
            for g in range(8):
                k.inc(sy.dma_start(k.t["loc_yT"][g * 128:(g + 1) * 128, qt * 128:(qt + 1) * 128],
                                   A4[:, 3072 + g * 128: 3072 + (g + 1) * 128]), "dS", 16)

    emit_wo_phase(k, "loc_yT", "w_l1ab_o")


def emit_l2_ffn(k, first_call):
    nc = k.nc
    A1, A2, A3, A4, A5, A6 = (k.t[n] for n in ("A1", "A2", "A3", "A4", "A5", "A6"))
    pb = [k.t[f"pb{i}"] for i in range(6)]
    SM = k.t["SM"]
    m_xnT = emit_ln_xnT(k)

    if first_call:
        @k.blk.sync
        def _(sy):
            k.w(sy, "ve", k.c["ve"])
            k.w(sy, "se", k.c["se"])
            k.inc(sy.dma_start(A5[0:32, 0:2048], k.t["pe_T"][:, :]), "dS", 16)
            k.inc(sy.dma_start(SM[0:32, 480:488], k.t["l2sigT"][:, :]), "dS", 16)
        m_pe = k.c["dS"]

        @k.blk.scalar
        def _(s):
            k.w(s, "dS", m_pe)
            k.inc(s.activation(SM[0:32, 488:496], SM[0:32, 480:488], AF.Sign), "se")
        m_sig = k.c["se"]

        for t in range(NT):
            @k.blk.tensor
            def _(te, t=t, m_sig=m_sig):
                k.w(te, "se", m_sig)
                k.w(te, "ve", k.c["ve"])
                k.inc(te.matmul(pb[5][:, 0:8], A5[0:32, t * 128:(t + 1) * 128],
                                SM[0:32, 488:496], start=True, stop=True), "te")
            m_sc = k.c["te"]

            @k.blk.vector
            def _(v, t=t, m_sc=m_sc):
                k.w(v, "te", m_sc)
                def ss(inst):
                    m = k.inc(inst, "ve")
                    v.wait_ge(k.s["ve"], m)
                sc = SM[:, 440:448]
                ss(v.tensor_copy(sc, pb[5][:, 0:8]))
                m1 = SM[:, 448:449]
                ss(v.reduce_max(m1, sc, axis=AX.X))
                is1 = SM[:, 450:458]
                ss(v.tensor_single_scalar(is1, sc, m1, ALU.is_equal))
                ones8 = SM[:, 458:466]
                ss(v.memset(ones8, 1.0))
                cum = SM[:, 466:474]
                ss(v.tensor_tensor_scan(cum, ones8, is1, 0.0, ALU.mult, ALU.add))
                ss(v.tensor_single_scalar(cum, cum, 1.0, ALU.is_equal))
                k.inc(v.tensor_tensor(k.t["l2sel"][:, t * 8:(t + 1) * 8], is1, cum, ALU.mult), "ve")
            m_ind = k.c["ve"]

            @k.blk.tensor
            def _(te, t=t, m_ind=m_ind):
                k.w(te, "ve", m_ind)
                k.w(te, "se", k.c["se"])
                k.inc(te.transpose(k.t["pt0"][0:8, 0:128], k.t["l2sel"][:, t * 8:(t + 1) * 8],
                                   k.t["identb"][:]), "te")
            m_tr = k.c["te"]

            @k.blk.scalar
            def _(s, t=t, m_tr=m_tr):
                k.w(s, "te", m_tr)
                k.w(s, "dS", k.c["dS"])
                k.inc(s.activation(A4[:, 0:128][0:8, :], k.t["pt0"][0:8, 0:128], AF.Copy), "se")
            m_se = k.c["se"]

            @k.blk.sync
            def _(sy, t=t, m_se=m_se):
                k.w(sy, "se", m_se)
                k.inc(sy.dma_start(k.t["loc_l2selT"][0:8, t * 128:(t + 1) * 128],
                                   A4[:, 0:128][0:8, :]), "dS", 16)

    for tt in range(8):
        @k.blk.gpsimd
        def _(g, tt=tt):
            k.w(g, "te", k.c["te"])
            k.inc(g.dma_start(A2[:, 0:8192].rearrange("p (g2 d) -> p g2 d", d=1024),
                              k.t["w_l2f_wT"][tt * 1024:(tt + 1) * 1024, :]
                              .rearrange("(g2 p) d -> p g2 d", p=128)), "dG", 16)
        m_w = k.c["dG"]

        for t in range(NT):
            for ch in range(2):
                bank = pb[(t * 2 + ch) % 2]

                @k.blk.tensor
                def _(te, tt=tt, t=t, ch=ch, bank=bank, m_w=m_w):
                    k.w(te, "dG", m_w)
                    if tt == 0 and t == 0 and ch == 0:
                        k.w(te, "ve", m_xnT)
                        k.w(te, "se", k.c["se"])
                    k.w(te, "ve", k.c["ve"])
                    last = None
                    for gi in range(8):
                        last = te.matmul(bank[:],
                                         A1[:, gi * 2048 + t * 128: gi * 2048 + t * 128 + 128],
                                         A2[:, gi * 1024 + ch * 512: gi * 1024 + ch * 512 + 512],
                                         start=(gi == 0), stop=(gi == 7))
                    k.inc(last, "te")
                m_mm = k.c["te"]

                @k.blk.vector
                def _(v, tt=tt, t=t, ch=ch, bank=bank, m_mm=m_mm):
                    k.w(v, "te", m_mm)
                    xt = k.t["xres"][:, t * 1024 + ch * 512: t * 1024 + ch * 512 + 512]
                    k.inc(v.scalar_tensor_tensor(xt, bank[:], k.t["l2sel"][:, t * 8 + tt:t * 8 + tt + 1],
                                                 xt, ALU.mult, ALU.add), "ve")

    A6b2 = A6[:].bitcast(BF16)

    @k.blk.sync
    def _(sy):
        k.w(sy, "ve", k.c["ve"])
        k.w(sy, "te", k.c["te"])
        k.inc(sy.dma_start(A6b2[0:8, 0:2048], k.t["loc_l2selT"][0:8, :]), "dS", 16)

    @k.blk.gpsimd
    def _(g):
        k.w(g, "te", k.c["te"])
        k.w(g, "ve", k.c["ve"])
        k.inc(g.dma_start(A4[:, 0:1024][0:8, :], k.t["w_l2f_b"][:, :]), "dG", 16)
    m_btab = (k.c["dS"], k.c["dG"])

    for t in range(NT):
        for ch in range(2):
            bank = pb[2 + (t * 2 + ch) % 2]

            @k.blk.tensor
            def _(te, t=t, ch=ch, bank=bank):
                if t == 0 and ch == 0:
                    k.w(te, "dS", m_btab[0])
                    k.w(te, "dG", m_btab[1])
                k.w(te, "se", k.c["se"])
                k.w(te, "ve", k.c["ve"])
                k.inc(te.matmul(bank[:], A6b2[0:8, t * 128:(t + 1) * 128],
                                A4[:, 0:1024][0:8, ch * 512:(ch + 1) * 512],
                                start=True, stop=True), "te")
            m_mm = k.c["te"]

            @k.blk.vector
            def _(v, t=t, ch=ch, bank=bank, m_mm=m_mm):
                k.w(v, "te", m_mm)
                xt = k.t["xres"][:, t * 1024 + ch * 512: t * 1024 + ch * 512 + 512]
                k.inc(v.tensor_tensor(xt, xt, bank[:], ALU.add), "ve")


def emit_l2_attn_b(k):
    nc = k.nc
    A1, A2, A3, A4, A5, A6 = (k.t[n] for n in ("A1", "A2", "A3", "A4", "A5", "A6"))
    pb = [k.t[f"pb{i}"] for i in range(6)]
    SM = k.t["SM"]
    m_xnT = emit_ln_xnT(k)

    @k.blk.gpsimd
    def _(g):
        k.w(g, "te", k.c["te"])
        k.inc(g.dma_start(A2[:, 0:8192].rearrange("p (g2 d) -> p g2 d", d=1024),
                          k.t["w_l2ab_q"].ap().rearrange("(g2 p) d -> p g2 d", p=128)), "dG", 16)
        k.inc(g.dma_start(A2[:, 8192:16384].rearrange("p (g2 d) -> p g2 d", d=1024),
                          k.t["w_l2ab_k"].ap().rearrange("(g2 p) d -> p g2 d", p=128)), "dG", 16)
        k.inc(g.dma_start(A3[:, 0:8192].rearrange("p (g2 d) -> p g2 d", d=1024),
                          k.t["w_l2ab_v"].ap().rearrange("(g2 p) d -> p g2 d", p=128)), "dG", 16)
        k.inc(g.dma_start(A3[:, 8192:16384].rearrange("p (g2 d) -> p g2 d", d=1024),
                          k.t["w_l2ab_o"].ap().rearrange("(g2 p) d -> p g2 d", p=128)), "dG", 16)
    m_w = k.c["dG"]

    @k.blk.sync
    def _(sy):
        k.w(sy, "ve", k.c["ve"])
        k.inc(sy.dma_start(SM[:, 480:488], k.t["b_l2ab_qcol"][:, :]), "dS", 16)
        k.inc(sy.dma_start(A6[:, 0:1024], k.t["bvbc"][:, :]), "dS", 16)
        k.inc(sy.dma_start(A5[:, 1024:2048], k.t["bkbc"][:, :]), "dS", 16)
    m_b = k.c["dS"]

    for t in range(NT):
        base = (t % 2) * 2048
        m_kv_staged = None
        for ch in range(2):
            bank = pb[ch]
            bankv = pb[2 + ch]

            @k.blk.tensor
            def _(te, t=t, ch=ch, bank=bank, bankv=bankv):
                if t == 0 and ch == 0:
                    k.w(te, "dG", m_w)
                    k.w(te, "ve", m_xnT)
                k.w(te, "se", k.c["se"])
                k.w(te, "ve", k.c["ve"])
                last = None
                for gi in range(8):
                    te.matmul(bank[:],
                              A1[:, gi * 2048 + t * 128: gi * 2048 + t * 128 + 128],
                              A2[:, 8192 + gi * 1024 + ch * 512: 8192 + gi * 1024 + ch * 512 + 512],
                              start=(gi == 0), stop=(gi == 7))
                    last = te.matmul(bankv[:],
                                     A1[:, gi * 2048 + t * 128: gi * 2048 + t * 128 + 128],
                                     A3[:, gi * 1024 + ch * 512: gi * 1024 + ch * 512 + 512],
                                     start=(gi == 0), stop=(gi == 7))
                k.inc(last, "te")
            m_mm = k.c["te"]

            @k.blk.vector
            def _(v, t=t, ch=ch, bank=bank, m_mm=m_mm):
                k.w(v, "te", m_mm)
                k.w(v, "dS", m_b)
                k.w(v, "se", k.c["se"])
                k.w(v, "ve", k.c["ve"])
                m = k.inc(v.tensor_tensor(A5[:, 0:512], bank[:], A5[:, 1024 + ch * 512:1024 + (ch + 1) * 512], ALU.add), "ve")
                v.wait_ge(k.s["ve"], m)
                k.inc(v.tensor_scalar_min(A5[:, 512:1024], A5[:, 0:512], 0.0), "ve")
            m_pre = k.c["ve"]

            @k.blk.scalar
            def _(s, t=t, ch=ch, m_pre=m_pre):
                k.w(s, "ve", m_pre)
                k.inc(s.activation(A5[:, 512:1024], A5[:, 512:1024], AF.Exp), "se")
            m_exp = k.c["se"]

            @k.blk.vector
            def _(v, t=t, ch=ch, bankv=bankv, m_exp=m_exp, base=base):
                k.w(v, "se", m_exp)
                k.w(v, "ve", k.c["ve"])
                ksl = A4[:, base + ch * 512: base + (ch + 1) * 512]
                vsl = A4[:, base + 1024 + ch * 512: base + 1024 + (ch + 1) * 512]
                m = k.inc(v.tensor_scalar_max(A5[:, 0:512], A5[:, 0:512], 0.0), "ve")
                v.wait_ge(k.s["ve"], m)
                m = k.inc(v.tensor_tensor(ksl, A5[:, 0:512], A5[:, 512:1024], ALU.add), "ve")
                v.wait_ge(k.s["ve"], m)
                k.inc(v.tensor_tensor(vsl, bankv[:], A6[:, ch * 512:(ch + 1) * 512], ALU.add), "ve")
            m_kv_staged = k.c["ve"]

            if DBG:
                @k.blk.sync
                def _(sy, t=t, ch=ch, m_kv_staged=m_kv_staged, base=base):
                    k.w(sy, "ve", m_kv_staged)
                    k.inc(sy.dma_start(k.t["loc_k"][t * 128:(t + 1) * 128, ch * 512:(ch + 1) * 512],
                                       A4[:, base + ch * 512: base + (ch + 1) * 512]), "dS", 16)
                    k.inc(sy.dma_start(k.t["loc_v"][t * 128:(t + 1) * 128, ch * 512:(ch + 1) * 512],
                                       A4[:, base + 1024 + ch * 512: base + 1024 + (ch + 1) * 512]), "dS", 16)

        @k.blk.tensor
        def _(te, t=t, base=base, m_kv_staged=m_kv_staged):
            k.w(te, "ve", m_kv_staged)
            last = None
            for h in range(16):
                off = (h // 8) * 512 + (h % 8) * 64
                kk = A4[:, base + off: base + off + 64]
                vv = A4[:, base + 1024 + off: base + 1024 + off + 64]
                last = te.matmul(pb[4][(h % 2) * 64:(h % 2) * 64 + 64, (h // 2) * 64:(h // 2) * 64 + 64],
                                 kk, vv, start=True, stop=True)
            k.inc(last, "te")
        m_kvmm = k.c["te"]

        @k.blk.vector
        def _(v, t=t, m_kvmm=m_kvmm):
            k.w(v, "te", m_kvmm)
            KV = A5[:, 2048:2560]
            if t == 0:
                k.inc(v.tensor_copy(KV, pb[4][:]), "ve")
            else:
                k.inc(v.tensor_tensor(KV, pb[4][:], KV, ALU.add), "ve")
    m_kv = k.c["ve"]

    @k.blk.vector
    def _(v):
        k.w(v, "ve", m_kv)
        k.inc(v.tensor_copy(A4[:, 0:512], A5[:, 2048:2560]), "ve")
    m_kvs = k.c["ve"]

    for tc in range(4):
        for gout in range(8):
            bank = pb[2 + (tc * 8 + gout) % 2]

            @k.blk.tensor
            def _(te, tc=tc, gout=gout, bank=bank):
                k.w(te, "se", k.c["se"])
                k.w(te, "ve", k.c["ve"])
                last = None
                for gi in range(8):
                    last = te.matmul(bank[:],
                                     A2[:, gi * 1024 + gout * 128: gi * 1024 + gout * 128 + 128],
                                     A1[:, gi * 2048 + tc * 512: gi * 2048 + tc * 512 + 512],
                                     start=(gi == 0), stop=(gi == 7))
                k.inc(last, "te")
            m_mm = k.c["te"]

            @k.blk.vector
            def _(v, tc=tc, gout=gout, bank=bank, m_mm=m_mm):
                k.w(v, "te", m_mm)
                k.w(v, "se", k.c["se"])
                v.tensor_scalar(A5[:, 0:512], bank[:], SM[:, 480 + gout:481 + gout], 0.0, ALU.add, ALU.min)
                k.inc(v.tensor_scalar(A5[:, 512:1024], bank[:], SM[:, 480 + gout:481 + gout], 0.0, ALU.add, ALU.max), "ve")
            m_pre = k.c["ve"]

            @k.blk.scalar
            def _(s, tc=tc, gout=gout, m_pre=m_pre):
                k.w(s, "ve", m_pre)
                k.inc(s.activation(A5[:, 0:512], A5[:, 0:512], AF.Exp), "se")
            m_exp = k.c["se"]

            @k.blk.vector
            def _(v, tc=tc, gout=gout, m_exp=m_exp):
                k.w(v, "se", m_exp)
                k.w(v, "ve", k.c["ve"])
                k.inc(v.tensor_tensor(A4[:, 4096 + gout * 512: 4096 + (gout + 1) * 512],
                                      A5[:, 0:512], A5[:, 512:1024], ALU.add), "ve")
        m_qstage = k.c["ve"]

        @k.blk.vector
        def _(v, tc=tc, m_qstage=m_qstage):
            k.w(v, "te", k.c["te"])
            last = None
            for g in range(8):
                last = v.tensor_copy(A1[:, g * 2048 + tc * 512: g * 2048 + (tc + 1) * 512],
                                     A4[:, 4096 + g * 512: 4096 + (g + 1) * 512])
            k.inc(last, "ve")
        if DBG:
            @k.blk.sync
            def _(sy, tc=tc):
                k.w(sy, "ve", k.c["ve"])
                for g in range(8):
                    k.inc(sy.dma_start(k.t["loc_qT"][g * 128:(g + 1) * 128, tc * 512:(tc + 1) * 512],
                                       A1[:, g * 2048 + tc * 512: g * 2048 + (tc + 1) * 512]), "dS", 16)
    m_qT = k.c["ve"]

    @k.blk.sync
    def _(sy):
        k.w(sy, "ve", k.c["ve"])
        k.inc(sy.dma_start(A6[:, 0:1024], k.t["bobc"][:, :]), "dS", 16)
    m_bo = k.c["dS"]

    for ic in range(4):
        for hp in range(8):
            @k.blk.tensor
            def _(te, ic=ic, hp=hp, m_qT=m_qT, m_kvs=m_kvs):
                if ic == 0 and hp == 0:
                    k.w(te, "ve", max(m_qT, m_kvs))
                k.w(te, "se", k.c["se"])
                last = None
                for sub in range(2):
                    h = hp * 2 + sub
                    kv_sl = A4[:, 0:512][(h % 2) * 64:(h % 2) * 64 + 64, (h // 2) * 64:(h // 2) * 64 + 64]
                    qT_h = A1[:, (h // 2) * 2048 + ic * 512: (h // 2) * 2048 + (ic + 1) * 512][(h % 2) * 64:(h % 2) * 64 + 64, :]
                    last = te.matmul(pb[5][sub * 64:sub * 64 + 64, :], kv_sl, qT_h, start=True, stop=True,
                                     tile_position=(sub * 64, sub * 64))
                k.inc(last, "te")
            m_ot = k.c["te"]

            @k.blk.scalar
            def _(s, ic=ic, hp=hp, m_ot=m_ot):
                k.w(s, "te", m_ot)
                k.inc(s.activation(A4[:, 1024 + hp * 512: 1024 + (hp + 1) * 512], pb[5][:], AF.Copy), "se")
        m_allo = k.c["se"]

        for isl in range(4):
            for ch in range(2):
                bank = pb[(isl * 2 + ch) % 2]

                @k.blk.tensor
                def _(te, ic=ic, isl=isl, ch=ch, bank=bank, m_allo=m_allo):
                    k.w(te, "se", m_allo)
                    k.w(te, "ve", k.c["ve"])
                    last = None
                    for hp in range(8):
                        last = te.matmul(bank[:],
                                         A4[:, 1024 + hp * 512 + isl * 128: 1024 + hp * 512 + isl * 128 + 128],
                                         A3[:, 8192 + hp * 1024 + ch * 512: 8192 + hp * 1024 + ch * 512 + 512],
                                         start=(hp == 0), stop=(hp == 7))
                    k.inc(last, "te")
                m_mm = k.c["te"]

                @k.blk.vector
                def _(v, ic=ic, isl=isl, ch=ch, bank=bank, m_mm=m_mm):
                    k.w(v, "te", m_mm)
                    t = ic * 4 + isl
                    xt = k.t["xres"][:, t * 1024 + ch * 512: t * 1024 + ch * 512 + 512]
                    k.w(v, "dS", m_bo)
                    k.w(v, "ve", k.c["ve"])
                    m = k.inc(v.tensor_tensor(A5[:, 0:512], bank[:], A6[:, ch * 512:(ch + 1) * 512], ALU.add), "ve")
                    v.wait_ge(k.s["ve"], m)
                    k.inc(v.tensor_tensor(xt, xt, A5[:, 0:512], ALU.add), "ve")




def emit_l3_ffn(k):
    nc = k.nc
    A1, A2, A3, A4, A5, A6 = (k.t[n] for n in ("A1", "A2", "A3", "A4", "A5", "A6"))
    pb = [k.t[f"pb{i}"] for i in range(6)]
    SM = k.t["SM"]
    m_xnT = emit_ln_xnT(k)

    @k.blk.sync
    def _(sy):
        k.w(sy, "ve", k.c["ve"])
        k.inc(sy.dma_start(SM[:, 200:264].rearrange("p (g e) -> p g e", e=8),
                           k.t["w_l3f_gT"].ap().rearrange("(g p) e -> p g e", p=128)), "dS", 16)
        k.inc(sy.dma_start(SM[:, 264:272], k.t["w_l3f_segT"][:, :]), "dS", 16)
    m_gw = k.c["dS"]

    for t in range(NT):
        @k.blk.scalar
        def _(s, t=t):
            k.w(s, "ve", k.c["ve"])
            k.w(s, "te", k.c["te"])
            k.inc(s.activation(A5[:, 0:1024], k.t["xres"][:, t * 1024:(t + 1) * 1024], AF.Identity,
                               bias=SM[:, 80 + t:81 + t], scale=SM[:, 64 + t:65 + t]), "se")
        m_xn32 = k.c["se"]

        @k.blk.tensor
        def _(te, t=t, m_xn32=m_xn32):
            k.w(te, "se", m_xn32)
            k.w(te, "ve", k.c["ve"])
            last = None
            for g in range(8):
                last = te.transpose(pb[g // 4][:, (g % 4) * 128:(g % 4) * 128 + 128],
                                    A5[:, g * 128:(g + 1) * 128], k.t["identf32"][:])
            k.inc(last, "te")
        m_tr = k.c["te"]

        @k.blk.vector
        def _(v, t=t, m_tr=m_tr):
            k.w(v, "te", m_tr)
            last = None
            for g in range(8):
                last = v.tensor_copy(A6[:, g * 128:(g + 1) * 128],
                                     pb[g // 4][:, (g % 4) * 128:(g % 4) * 128 + 128])
            k.inc(last, "ve")
        m_x32 = k.c["ve"]

        @k.blk.tensor
        def _(te, t=t, m_x32=m_x32):
            k.w(te, "ve", m_x32)
            k.w(te, "dS", m_gw)
            k.w(te, "se", k.c["se"])
            last = None
            for g in range(8):
                te.matmul(pb[4][:, 0:8], A6[:, g * 128:(g + 1) * 128], SM[:, 200 + g * 8:208 + g * 8],
                          start=(g == 0), stop=(g == 7))
                last = te.matmul(pb[5][:, 0:1], A6[:, g * 128:(g + 1) * 128], SM[:, 264 + g:265 + g],
                                 start=(g == 0), stop=(g == 7))
            k.inc(last, "te")
        m_lg = k.c["te"]

        @k.blk.vector
        def _(v, t=t, m_lg=m_lg):
            k.w(v, "te", m_lg)
            def ss(inst):
                m = k.inc(inst, "ve")
                v.wait_ge(k.s["ve"], m)
            lg = SM[:, 430:438]
            ss(v.tensor_copy(lg, pb[4][:, 0:8]))
            m1 = SM[:, 438:439]
            ss(v.reduce_max(m1, lg, axis=AX.X))
            ss(v.tensor_scalar_mul(SM[:, 439:440], m1, -1.0))
            k.inc(v.engine_nop(), "ve")
        m_pre = k.c["ve"]

        @k.blk.scalar
        def _(s, t=t, m_pre=m_pre):
            k.w(s, "ve", m_pre)
            s.activation(SM[:, 440:448], SM[:, 430:438], AF.Exp, bias=SM[:, 439:440],
                         accum_out=SM[:, 448:449])
            k.inc(s.activation(SM[:, 429:430], pb[5][:, 0:1], AF.Sigmoid), "se")
        m_exp = k.c["se"]

        @k.blk.vector
        def _(v, t=t, m_exp=m_exp):
            k.w(v, "se", m_exp)
            def ss(inst):
                m = k.inc(inst, "ve")
                v.wait_ge(k.s["ve"], m)
            sc = SM[:, 450:458]
            ss(v.reciprocal(SM[:, 449:450], SM[:, 448:449]))
            ss(v.tensor_scalar(sc, SM[:, 440:448], SM[:, 449:450], None, ALU.mult))
            m1b = SM[:, 458:459]
            ss(v.reduce_max(m1b, sc, axis=AX.X))
            is1 = SM[:, 459:467]
            ss(v.tensor_single_scalar(is1, sc, m1b, ALU.is_equal))
            ones8 = SM[:, 467:475]
            ss(v.memset(ones8, 1.0))
            cum = SM[:, 475:483]
            ss(v.tensor_tensor_scan(cum, ones8, is1, 0.0, ALU.mult, ALU.add))
            ss(v.tensor_single_scalar(cum, cum, 1.0, ALU.is_equal))
            f1 = SM[:, 483:491]
            ss(v.tensor_tensor(f1, is1, cum, ALU.mult))
            s2 = SM[:, 491:499]
            ss(v.scalar_tensor_tensor(s2, f1, -1e30, sc, ALU.mult, ALU.add))
            m2 = SM[:, 499:500]
            ss(v.reduce_max(m2, s2, axis=AX.X))
            ind = SM[:, 500:508]
            ss(v.tensor_single_scalar(ind, sc, m2, ALU.is_ge))
            ss(v.tensor_tensor(k.t["l3w"][:, t * 9:t * 9 + 8], sc, ind, ALU.mult))
            k.inc(v.tensor_copy(k.t["l3w"][:, t * 9 + 8:t * 9 + 9], SM[:, 429:430]), "ve")
    m_route = k.c["ve"]

    for e in range(10):
        @k.blk.gpsimd
        def _(g, e=e):
            k.w(g, "te", k.c["te"])
            k.w(g, "dS", k.c["dS"])
            if e < 8:
                gsrc = k.t["w_l3f_guT"][e * 1024:(e + 1) * 1024, 0:1408]
                usrc = k.t["w_l3f_guT"][e * 1024:(e + 1) * 1024, 1408:2816]
            else:
                gsrc = k.t["w_l3f_sgT"][0:1024, (e - 8) * 1408:(e - 7) * 1408]
                usrc = k.t["w_l3f_suT"][0:1024, (e - 8) * 1408:(e - 7) * 1408]
            k.inc(g.dma_start(A2[:, 0:11264].rearrange("p (g2 d) -> p g2 d", d=1408),
                              gsrc.rearrange("(g2 p) d -> p g2 d", p=128)), "dG", 16)
            k.inc(g.dma_start(A3[:, 0:11264].rearrange("p (g2 d) -> p g2 d", d=1408),
                              usrc.rearrange("(g2 p) d -> p g2 d", p=128)), "dG", 16)
        m_w = k.c["dG"]

        for it in range(11):
            for tc in range(4):
                for arena, bk in ((A2, pb[0]), (A3, pb[1])):
                    @k.blk.tensor
                    def _(te, e=e, it=it, tc=tc, arena=arena, bk=bk, m_w=m_w):
                        k.w(te, "dG", m_w)
                        if e == 0 and it == 0 and tc == 0 and arena is A2:
                            k.w(te, "ve", m_route)
                        k.w(te, "se", k.c["se"])
                        k.w(te, "ve", k.c["ve"])
                        last = None
                        for gi in range(8):
                            last = te.matmul(bk[:],
                                             arena[:, gi * 1408 + it * 128: gi * 1408 + it * 128 + 128],
                                             A1[:, gi * 2048 + tc * 512: gi * 2048 + (tc + 1) * 512],
                                             start=(gi == 0), stop=(gi == 7))
                        k.inc(last, "te")
                m_gu = k.c["te"]

                @k.blk.scalar
                def _(s, e=e, it=it, tc=tc, m_gu=m_gu):
                    k.w(s, "te", m_gu)
                    k.w(s, "ve", k.c["ve"])
                    k.inc(s.activation(A5[:, 0:512], pb[0][:], AF.Silu), "se")
                m_sl = k.c["se"]

                @k.blk.vector
                def _(v, e=e, it=it, tc=tc, m_sl=m_sl):
                    k.w(v, "se", m_sl)
                    k.w(v, "dS", k.c["dS"])
                    k.inc(v.tensor_tensor(A4[:, 0:512], A5[:, 0:512], pb[1][:], ALU.mult), "ve")
                m_h = k.c["ve"]

                @k.blk.sync
                def _(sy, e=e, it=it, tc=tc, m_h=m_h):
                    k.w(sy, "ve", m_h)
                    k.inc(sy.dma_start(k.t["loc_hqT"][it * 128:(it + 1) * 128, tc * 512:(tc + 1) * 512],
                                       A4[:, 0:512]), "dS", 16)
        m_hspill = k.c["dS"]

        @k.blk.gpsimd
        def _(g, e=e, m_hspill=m_hspill):
            k.w(g, "te", k.c["te"])
            if e < 8:
                src = k.t["w_l3f_dT"][e * 1408:(e + 1) * 1408, :]
            else:
                src = k.t["w_l3f_sdT"][(e - 8) * 1408:(e - 7) * 1408, :]
            k.inc(g.dma_start(A2[:, 0:11264].rearrange("p (g2 d) -> p g2 d", d=1024),
                              src.rearrange("(g2 p) d -> p g2 d", p=128)), "dG", 16)
        m_wd = k.c["dG"]

        for t in range(NT):
            @k.blk.sync
            def _(sy, e=e, t=t, m_hspill=m_hspill):
                if t == 0:
                    k.w(sy, "dS", m_hspill)
                k.w(sy, "te", k.c["te"])
                for gi in range(11):
                    k.inc(sy.dma_start(A4[:, 1024 + gi * 128: 1024 + (gi + 1) * 128],
                                       k.t["loc_hqT"][gi * 128:(gi + 1) * 128, t * 128:(t + 1) * 128]),
                          "dS", 16)
            m_hT = k.c["dS"]

            for ch in range(2):
                bank = pb[2 + (t * 2 + ch) % 2]

                @k.blk.tensor
                def _(te, e=e, t=t, ch=ch, bank=bank, m_hT=m_hT, m_wd=m_wd):
                    k.w(te, "dS", m_hT)
                    if t == 0 and ch == 0:
                        k.w(te, "dG", m_wd)
                    k.w(te, "ve", k.c["ve"])
                    last = None
                    for gi in range(11):
                        src = A4[:, 1024 + gi * 128: 1024 + (gi + 1) * 128]
                        last = te.matmul(bank[:], src,
                                         A2[:, gi * 1024 + ch * 512: gi * 1024 + ch * 512 + 512],
                                         start=(gi == 0), stop=(gi == 10))
                    k.inc(last, "te")
                m_mm = k.c["te"]

                @k.blk.vector
                def _(v, e=e, t=t, ch=ch, bank=bank, m_mm=m_mm):
                    k.w(v, "te", m_mm)
                    xt = k.t["xres"][:, t * 1024 + ch * 512: t * 1024 + ch * 512 + 512]
                    wcol = k.t["l3w"][:, t * 9 + min(e, 8): t * 9 + min(e, 8) + 1]
                    k.inc(v.scalar_tensor_tensor(xt, bank[:], wcol, xt, ALU.mult, ALU.add), "ve")


def emit_l3_attn_b(k):
    nc = k.nc
    A1, A2, A3, A4, A5, A6 = (k.t[n] for n in ("A1", "A2", "A3", "A4", "A5", "A6"))
    pb = [k.t[f"pb{i}"] for i in range(6)]
    SM = k.t["SM"]
    A6b = A6[:].bitcast(BF16)
    ptb = [k.t["pt0"], k.t["pt1"]]

    def ln_from_hbm(src_name, gname, bname, sumofs, sqofs, meanofs, varofs, rstofs, nmrofs):
        for t in range(NT):
            @k.blk.sync
            def _(sy, t=t):
                k.w(sy, "se", k.c["se"])
                k.w(sy, "ve", k.c["ve"])
                k.inc(sy.dma_start(A5[:, 0:1024], k.t[src_name][t * 128:(t + 1) * 128, :]), "dS", 16)
            m_ld = k.c["dS"]

            @k.blk.scalar
            def _(s, t=t, m_ld=m_ld):
                k.w(s, "dS", m_ld)
                k.inc(s.activation(A5[:, 1024:2048], A5[:, 0:1024], AF.Square,
                                   accum_out=SM[:, sqofs + t:sqofs + t + 1]), "se")
            m_sq = k.c["se"]

            @k.blk.vector
            def _(v, t=t, m_sq=m_sq):
                k.w(v, "se", m_sq)
                k.inc(v.reduce_sum(SM[:, sumofs + t:sumofs + t + 1], A5[:, 0:1024], axis=AX.X), "ve")
        m_stats = k.c["ve"]

        @k.blk.vector
        def _(v):
            k.w(v, "ve", m_stats)
            k.w(v, "se", k.c["se"])
            def ss(inst):
                m = k.inc(inst, "ve")
                v.wait_ge(k.s["ve"], m)
            ss(v.tensor_scalar_mul(SM[:, meanofs:meanofs + 16], SM[:, sumofs:sumofs + 16], 1.0 / 1024.0))
            ss(v.tensor_scalar_mul(SM[:, varofs:varofs + 16], SM[:, sqofs:sqofs + 16], 1.0 / 1024.0))
            ss(v.tensor_tensor(SM[:, rstofs:rstofs + 16], SM[:, meanofs:meanofs + 16],
                               SM[:, meanofs:meanofs + 16], ALU.mult))
            ss(v.tensor_tensor(SM[:, varofs:varofs + 16], SM[:, varofs:varofs + 16],
                               SM[:, rstofs:rstofs + 16], ALU.subtract))
            k.inc(v.tensor_scalar_add(SM[:, varofs:varofs + 16], SM[:, varofs:varofs + 16], 1e-5), "ve")
        m_var = k.c["ve"]

        @k.blk.scalar
        def _(s):
            k.w(s, "ve", m_var)
            m = k.inc(s.activation(SM[:, nmrofs:nmrofs + 16], SM[:, varofs:varofs + 16], AF.Ln), "se")
            s.wait_ge(k.s["se"], m)
            k.inc(s.activation(SM[:, rstofs:rstofs + 16], SM[:, nmrofs:nmrofs + 16], AF.Exp, scale=-0.5), "se")
        m_rst = k.c["se"]

        @k.blk.vector
        def _(v):
            k.w(v, "se", m_rst)
            k.inc(v.scalar_tensor_tensor(SM[:, nmrofs:nmrofs + 16], SM[:, meanofs:meanofs + 16], -1.0,
                                         SM[:, rstofs:rstofs + 16], ALU.mult, ALU.mult), "ve")
        m_nmr = k.c["ve"]

        @k.blk.sync
        def _(sy):
            k.w(sy, "ve", k.c["ve"])
            k.w(sy, "se", k.c["se"])
            k.inc(sy.dma_start(A6[:, 0:1024], k.t[gname][:, :]), "dS", 16)
            k.inc(sy.dma_start(A5[:, 2048:3072], k.t[bname][:, :]), "dS", 16)
        m_gb = k.c["dS"]

        m_tr = {}
        m_ev = {}
        for t in range(NT):
            @k.blk.sync
            def _(sy, t=t):
                k.w(sy, "se", k.c["se"])
                k.inc(sy.dma_start(A5[:, 0:1024], k.t[src_name][t * 128:(t + 1) * 128, :]), "dS", 16)
            m_ld = k.c["dS"]

            @k.blk.scalar
            def _(s, t=t, m_ld=m_ld):
                k.w(s, "dS", m_ld)
                if t == 0:
                    k.w(s, "ve", m_nmr)
                    k.w(s, "dS", m_gb)
                k.inc(s.activation(A5[:, 1024:2048], A5[:, 0:1024], AF.Identity,
                                   bias=SM[:, nmrofs + t:nmrofs + t + 1],
                                   scale=SM[:, rstofs + t:rstofs + t + 1]), "se")
            m_xn = k.c["se"]

            @k.blk.vector
            def _(v, t=t, m_xn=m_xn):
                k.w(v, "se", m_xn)
                if t >= 2:
                    k.w(v, "te", m_tr.get(t - 2, 0))
                v.tensor_tensor(A5[:, 1024:2048], A5[:, 1024:2048], A6[:, 0:1024], ALU.mult)
                k.inc(v.tensor_tensor(A4[:, (t % 2) * 1024:(t % 2) * 1024 + 1024],
                                      A5[:, 1024:2048], A5[:, 2048:3072], ALU.add), "ve")
            m_aff = k.c["ve"]

            @k.blk.tensor
            def _(te, t=t, m_aff=m_aff):
                k.w(te, "ve", m_aff)
                if t >= 1:
                    k.w(te, "ve", m_ev.get(t - 1, 0))
                k.w(te, "se", k.c["se"])
                last = None
                for g in range(8):
                    last = te.transpose(ptb[g // 4][:, (g % 4) * 128:(g % 4) * 128 + 128],
                                        A4[:, (t % 2) * 1024 + g * 128: (t % 2) * 1024 + (g + 1) * 128],
                                        k.t["identb"][:])
                m_tr[t] = k.inc(last, "te")

            @k.blk.vector
            def _(v, t=t):
                k.w(v, "te", m_tr[t])
                last = None
                for g in range(8):
                    src = ptb[g // 4]
                    last = v.tensor_copy(A1[:, g * 2048 + t * 128: g * 2048 + t * 128 + 128],
                                        src[:, (g % 4) * 128:(g % 4) * 128 + 128])
                m_ev[t] = k.inc(last, "ve")
        return k.c["ve"]

    m_nmr0 = emit_ln_stats(k)
    for t in range(NT):
        @k.blk.scalar
        def _(s, t=t):
            if t == 0:
                k.w(s, "ve", m_nmr0)
            k.w(s, "dS", k.c["dS"])
            k.w(s, "te", k.c["te"])
            k.inc(s.activation(A5[:, 0:1024], k.t["xres"][:, t * 1024:(t + 1) * 1024], AF.Identity,
                               bias=SM[:, 80 + t:81 + t], scale=SM[:, 64 + t:65 + t]), "se")
        m_u = k.c["se"]

        @k.blk.sync
        def _(sy, t=t, m_u=m_u):
            k.w(sy, "se", m_u)
            k.inc(sy.dma_start(k.t["loc_u"][t * 128:(t + 1) * 128, :], A5[:, 0:1024]), "dS", 16)

    m_xnT = ln_from_hbm("loc_u", "b_m_n1g", "b_m_n1b", 294, 310, 326, 342, 358, 390)

    @k.blk.sync
    def _(sy):
        k.w(sy, "se", k.c["se"])
        k.w(sy, "ve", k.c["ve"])
        k.inc(sy.dma_start(SM[:, 200:264].rearrange("p (g c) -> p g c", c=4),
                           k.t["w_m_convw"].ap().rearrange("(g p) c -> p g c", p=128)), "dS", 16)
        k.inc(sy.dma_start(SM[:, 264:280], k.t["w_m_convb"][:, :]), "dS", 16)
        k.inc(sy.dma_start(SM[:, 280:296], k.t["w_m_dtb"][:, :]), "dS", 16)
        k.inc(sy.dma_start(SM[:, 406:422], k.t["w_m_D"][:, :]), "dS", 16)
    m_small = k.c["dS"]

    @k.blk.gpsimd
    def _(g):
        k.w(g, "te", k.c["te"])
        k.inc(g.dma_start(A2[:, 0:16384].rearrange("p (g2 d) -> p g2 d", d=2048),
                          k.t["w_m_inT"][0:1024, 0:2048].rearrange("(g2 p) d -> p g2 d", p=128)), "dG", 16)
        k.inc(g.dma_start(A3[:, 0:16384].rearrange("p (g2 d) -> p g2 d", d=2048),
                          k.t["w_m_inT"][0:1024, 2048:4096].rearrange("(g2 p) d -> p g2 d", p=128)), "dG", 16)
        k.inc(g.dma_start(A4[:, 4096:4864].rearrange("p (g2 d) -> p g2 d", d=96),
                          k.t["w_m_xprojT"][0:1024, :].rearrange("(g2 p) d -> p g2 d", p=128)), "dG", 16)
        k.inc(g.dma_start(A4[:, 4864:5632].rearrange("p (g2 d) -> p g2 d", d=96),
                          k.t["w_m_xprojT"][1024:2048, :].rearrange("(g2 p) d -> p g2 d", p=128)), "dG", 16)
    m_w1 = k.c["dG"]

    for gout in range(16):
        for tc in range(4):
            bank = pb[tc % 2]

            @k.blk.tensor
            def _(te, gout=gout, tc=tc, bank=bank, m_w1=m_w1):
                if gout == 0 and tc == 0:
                    k.w(te, "dG", m_w1)
                    k.w(te, "ve", m_xnT)
                    k.w(te, "dS", m_small)
                k.w(te, "se", k.c["se"])
                last = None
                for gi in range(8):
                    last = te.matmul(bank[:],
                                     A2[:, gi * 2048 + gout * 128: gi * 2048 + gout * 128 + 128],
                                     A1[:, gi * 2048 + tc * 512: gi * 2048 + (tc + 1) * 512],
                                     start=(gi == 0), stop=(gi == 7))
                k.inc(last, "te")
            m_mm = k.c["te"]

            @k.blk.scalar
            def _(s, gout=gout, tc=tc, bank=bank, m_mm=m_mm):
                k.w(s, "te", m_mm)
                k.w(s, "ve", k.c["ve"])
                k.inc(s.activation(A5[:, tc * 512:(tc + 1) * 512], bank[:], AF.Copy), "se")
        m_row = k.c["se"]

        for lc in range(2):
            @k.blk.vector
            def _(v, gout=gout, lc=lc, m_row=m_row):
                k.w(v, "se", m_row)
                acc = A5[:, 2048 + 0:3072]
                src0 = 1024 * lc
                wc = lambda tau: SM[:, 200 + gout * 4 + tau:200 + gout * 4 + tau + 1]
                v.tensor_scalar(acc, A5[:, src0:src0 + 1024], wc(3), None, ALU.mult)
                if lc == 0:
                    v.scalar_tensor_tensor(acc[:, 1:1024], A5[:, 0:1023], wc(2), acc[:, 1:1024], ALU.mult, ALU.add)
                    v.scalar_tensor_tensor(acc[:, 2:1024], A5[:, 0:1022], wc(1), acc[:, 2:1024], ALU.mult, ALU.add)
                    k.inc(v.scalar_tensor_tensor(acc[:, 3:1024], A5[:, 0:1021], wc(0), acc[:, 3:1024], ALU.mult, ALU.add), "ve")
                else:
                    v.scalar_tensor_tensor(acc, A5[:, src0 - 1:src0 + 1023], wc(2), acc, ALU.mult, ALU.add)
                    v.scalar_tensor_tensor(acc, A5[:, src0 - 2:src0 + 1022], wc(1), acc, ALU.mult, ALU.add)
                    k.inc(v.scalar_tensor_tensor(acc, A5[:, src0 - 3:src0 + 1021], wc(0), acc, ALU.mult, ALU.add), "ve")
            m_cv = k.c["ve"]

            @k.blk.scalar
            def _(s, gout=gout, lc=lc, m_cv=m_cv):
                k.w(s, "ve", m_cv)
                k.w(s, "te", k.c["te"])
                k.inc(s.activation(A4[:, lc * 1024:(lc + 1) * 1024], A5[:, 2048:3072], AF.Silu,
                                   bias=SM[:, 264 + gout:265 + gout]), "se")
        m_xc = k.c["se"]

        @k.blk.sync
        def _(sy, gout=gout, m_xc=m_xc):
            k.w(sy, "se", m_xc)
            k.inc(sy.dma_start(k.t["loc_xcT"][gout * 128:(gout + 1) * 128, :], A4[:, 0:2048]), "dS", 16)

        @k.blk.tensor
        def _(te, gout=gout, m_xc=m_xc):
            k.w(te, "se", m_xc)
            xp = A4[:, 4096 + (gout // 8) * 768 + (gout % 8) * 96: 4096 + (gout // 8) * 768 + (gout % 8) * 96 + 96]
            last = None
            for tc in range(4):
                last = te.matmul(pb[2 + tc][0:96, :], xp, A4[:, tc * 512:(tc + 1) * 512],
                                 start=(gout == 0), stop=(gout == 15))
            k.inc(last, "te")
    m_xdbl = k.c["te"]

    for gout in range(16):
        for tc in range(4):
            bank = pb[tc % 2]

            @k.blk.tensor
            def _(te, gout=gout, tc=tc, bank=bank):
                k.w(te, "se", k.c["se"])
                last = None
                for gi in range(8):
                    last = te.matmul(bank[:],
                                     A3[:, gi * 2048 + gout * 128: gi * 2048 + gout * 128 + 128],
                                     A1[:, gi * 2048 + tc * 512: gi * 2048 + (tc + 1) * 512],
                                     start=(gi == 0), stop=(gi == 7))
                k.inc(last, "te")
            m_mm = k.c["te"]

            @k.blk.scalar
            def _(s, gout=gout, tc=tc, bank=bank, m_mm=m_mm):
                k.w(s, "te", m_mm)
                k.w(s, "dS", k.c["dS"])
                k.inc(s.activation(A4[:, 6144 + (tc % 2) * 512: 6144 + (tc % 2) * 512 + 512], bank[:], AF.Copy), "se")
            m_ev2 = k.c["se"]

            @k.blk.sync
            def _(sy, gout=gout, tc=tc, m_ev2=m_ev2):
                k.w(sy, "se", m_ev2)
                k.inc(sy.dma_start(k.t["loc_zT"][gout * 128:(gout + 1) * 128, tc * 512:(tc + 1) * 512],
                                   A4[:, 6144 + (tc % 2) * 512: 6144 + (tc % 2) * 512 + 512]), "dS", 16)

    @k.blk.vector
    def _(v):
        k.w(v, "te", m_xdbl)
        k.w(v, "se", k.c["se"])
        last = None
        for tc in range(4):
            last = v.tensor_copy(A5[0:96, tc * 512:(tc + 1) * 512], pb[2 + tc][0:96, :])
        k.inc(last, "ve")
    m_xd = k.c["ve"]

    @k.blk.sync
    def _(sy):
        k.w(sy, "ve", m_xd)
        k.inc(sy.dma_start(k.t["loc_bc"][0:32, :], A5[64:96, 0:2048]), "dS", 16)
    m_bc = k.c["dS"]

    @k.blk.scalar
    def _(s):
        k.w(s, "ve", m_xd)
        k.w(s, "dS", k.c["dS"])
        k.inc(s.activation(A4[0:64, 6144:8192], A5[0:64, 0:2048], AF.Copy), "se")
    m_dtT = k.c["se"]

    @k.blk.gpsimd
    def _(g):
        k.w(g, "te", k.c["te"])
        k.inc(g.dma_start(A4[0:64, 0:2048], k.t["w_m_dtT"][:, :]), "dG", 16)
    m_dtw = k.c["dG"]

    for dt_tile in range(16):
        for tc in range(4):
            bank = pb[(dt_tile * 4 + tc) % 2]

            @k.blk.tensor
            def _(te, dt_tile=dt_tile, tc=tc, bank=bank, m_dtT=m_dtT, m_dtw=m_dtw):
                if dt_tile == 0 and tc == 0:
                    k.w(te, "se", m_dtT)
                    k.w(te, "dG", m_dtw)
                k.w(te, "ve", k.c["ve"])
                k.inc(te.matmul(bank[:], A4[0:64, dt_tile * 128:(dt_tile + 1) * 128],
                                A4[0:64, 6144 + tc * 512: 6144 + (tc + 1) * 512],
                                start=True, stop=True), "te")
            m_mm = k.c["te"]

            @k.blk.vector
            def _(v, dt_tile=dt_tile, tc=tc, bank=bank, m_mm=m_mm):
                k.w(v, "te", m_mm)
                k.w(v, "se", k.c["se"])
                t1 = A5[:, 2048:2560]
                t2 = A5[:, 2560:3072]
                v.tensor_scalar(t1, bank[:], SM[:, 280 + dt_tile:281 + dt_tile], None, ALU.add)
                v.tensor_scalar_max(t2, t1, 0.0)
                k.inc(v.scalar_tensor_tensor(t1, t2, -2.0, t1, ALU.mult, ALU.add), "ve")
            m_sp1 = k.c["ve"]

            @k.blk.scalar
            def _(s, dt_tile=dt_tile, tc=tc, m_sp1=m_sp1):
                k.w(s, "ve", m_sp1)
                s.activation(A5[:, 2048:2560], A5[:, 2048:2560], AF.Exp)
                k.inc(s.activation(A5[:, 2048:2560], A5[:, 2048:2560], AF.Ln, bias=1.0), "se")
            m_sp2 = k.c["se"]

            @k.blk.vector
            def _(v, dt_tile=dt_tile, tc=tc, m_sp2=m_sp2):
                k.w(v, "se", m_sp2)
                k.inc(v.tensor_tensor(A4[:, 2048 + (tc % 2) * 512: 2048 + (tc % 2) * 512 + 512],
                                      A5[:, 2048:2560], A5[:, 2560:3072], ALU.add), "ve")
            m_sp3 = k.c["ve"]

            @k.blk.sync
            def _(sy, dt_tile=dt_tile, tc=tc, m_sp3=m_sp3):
                k.w(sy, "ve", m_sp3)
                k.inc(sy.dma_start(k.t["loc_deltaT"][dt_tile * 128:(dt_tile + 1) * 128, tc * 512:(tc + 1) * 512],
                                   A4[:, 2048 + (tc % 2) * 512: 2048 + (tc % 2) * 512 + 512]), "dS", 16)
    m_delta = k.c["dS"]

    for g in range(16):
        @k.blk.sync
        def _(sy, g=g):
            k.w(sy, "dS", m_delta)
            k.w(sy, "ve", k.c["ve"])
            k.w(sy, "te", k.c["te"])
            k.inc(sy.dma_start(A4[:, 0:2048], k.t["loc_deltaT"][g * 128:(g + 1) * 128, :]), "dS", 16)
            k.inc(sy.dma_start(A4[:, 4096:6144], k.t["loc_xcT"][g * 128:(g + 1) * 128, :]), "dS", 16)
            k.inc(sy.dma_start(SM[:, 374:390], k.t["w_m_Alog"][g * 128:(g + 1) * 128, :]), "dS", 16)
        m_ld = k.c["dS"]

        @k.blk.scalar
        def _(s, g=g, m_ld=m_ld):
            k.w(s, "dS", m_ld)
            k.inc(s.activation(SM[:, 390:406], SM[:, 374:390], AF.Exp), "se")
        m_A1 = k.c["se"]

        @k.blk.vector
        def _(v, g=g, m_A1=m_A1):
            k.w(v, "se", m_A1)
            v.tensor_scalar_mul(SM[:, 390:406], SM[:, 390:406], -1.0)
            v.tensor_tensor(A4[:, 2048:4096], A4[:, 0:2048], A4[:, 4096:6144], ALU.mult)
            k.inc(v.memset(A5[:, 0:2048], 0.0), "ve")
        m_ug = k.c["ve"]

        for n in range(16):
            for lc in range(2):
                @k.blk.gpsimd
                def _(gp, g=g, n=n, lc=lc):
                    k.w(gp, "ve", k.c["ve"])
                    k.inc(gp.dma_start(A6b[:, 0:1024],
                                       k.t["loc_bc"][n:n + 1, lc * 1024:(lc + 1) * 1024]
                                       .broadcast_to([128, 1024])), "dG", 16)
                    k.inc(gp.dma_start(A6b[:, 1024:2048],
                                       k.t["loc_bc"][16 + n:17 + n, lc * 1024:(lc + 1) * 1024]
                                       .broadcast_to([128, 1024])), "dG", 16)
                m_bcl = k.c["dG"]

                @k.blk.scalar
                def _(s, g=g, n=n, lc=lc, m_ug=m_ug):
                    k.w(s, "ve", k.c["ve"])
                    k.inc(s.activation(A4[:, 4096 + lc * 0:4096 + 1024] if False else A4[:, 4096:5120],
                                       A4[:, lc * 1024:(lc + 1) * 1024], AF.Exp,
                                       scale=SM[:, 390 + n:391 + n]), "se")
                m_dA = k.c["se"]

                @k.blk.vector
                def _(v, g=g, n=n, lc=lc, m_dA=m_dA, m_bcl=m_bcl):
                    k.w(v, "se", m_dA)
                    k.w(v, "dG", m_bcl)
                    def ss(inst):
                        m = k.inc(inst, "ve")
                        v.wait_ge(k.s["ve"], m)
                    v.tensor_tensor(A4[:, 5120:6144], A4[:, 2048 + lc * 1024:2048 + (lc + 1) * 1024],
                                    A6b[:, 0:1024], ALU.mult)
                    init = 0.0 if lc == 0 else SM[:, 422:423]
                    ss(v.tensor_tensor_scan(A4[:, 6144:7168], A4[:, 4096:5120], A4[:, 5120:6144],
                                            init, ALU.mult, ALU.add))
                    ss(v.tensor_copy(SM[:, 422:423], A4[:, 7167:7168]))
                    k.inc(v.scalar_tensor_tensor(A5[:, lc * 1024:(lc + 1) * 1024], A4[:, 6144:7168], 1.0,
                                                 A5[:, lc * 1024:(lc + 1) * 1024], ALU.mult, ALU.add)
                          if False else
                          v.tensor_tensor(A4[:, 7168:8192], A4[:, 6144:7168], A6b[:, 1024:2048], ALU.mult), "ve")
                m_hC = k.c["ve"]

                @k.blk.vector
                def _(v, g=g, n=n, lc=lc, m_hC=m_hC):
                    k.w(v, "ve", m_hC)
                    k.inc(v.tensor_tensor(A5[:, lc * 1024:(lc + 1) * 1024],
                                          A5[:, lc * 1024:(lc + 1) * 1024], A4[:, 7168:8192], ALU.add), "ve")
        m_y = k.c["ve"]

        @k.blk.sync
        def _(sy, g=g, m_y=m_y):
            k.w(sy, "ve", m_y)
            k.inc(sy.dma_start(A4[:, 0:2048], k.t["loc_zT"][g * 128:(g + 1) * 128, :]), "dS", 16)
        m_z = k.c["dS"]

        @k.blk.sync
        def _(sy, g=g, m_z=m_z):
            k.w(sy, "dS", m_z)
            k.w(sy, "ve", k.c["ve"])
            k.inc(sy.dma_start(A4[:, 2048:4096], k.t["loc_xcT"][g * 128:(g + 1) * 128, :]), "dS", 16)
        m_xc2 = k.c["dS"]

        @k.blk.vector
        def _(v, g=g, m_xc2=m_xc2):
            k.w(v, "dS", m_xc2)
            k.inc(v.scalar_tensor_tensor(A5[:, 0:2048], A4[:, 2048:4096], SM[:, 423 + g:424 + g],
                                         A5[:, 0:2048], ALU.mult, ALU.add), "ve")
        m_yD = k.c["ve"]

        @k.blk.scalar
        def _(s, g=g, m_yD=m_yD):
            k.w(s, "ve", m_yD)
            k.inc(s.activation(A4[:, 2048:4096], A4[:, 0:2048], AF.Silu), "se")
        m_sz = k.c["se"]

        @k.blk.vector
        def _(v, g=g, m_sz=m_sz):
            k.w(v, "se", m_sz)
            k.inc(v.tensor_tensor(A4[:, 6144:8192], A5[:, 0:2048], A4[:, 2048:4096], ALU.mult), "ve")
        m_yT = k.c["ve"]

        @k.blk.sync
        def _(sy, g=g, m_yT=m_yT):
            k.w(sy, "ve", m_yT)
            k.inc(sy.dma_start(k.t["loc_hqT"][g * 128:(g + 1) * 128, :], A4[:, 6144:8192]), "dS", 16)
    m_yspill = k.c["dS"]


    @k.blk.gpsimd
    def _(g):
        k.w(g, "te", k.c["te"])
        k.inc(g.dma_start(A2[:, 0:16384].rearrange("p (g2 d) -> p g2 d", d=1024),
                          k.t["w_m_outT"].ap().rearrange("(g2 p) d -> p g2 d", p=128)), "dG", 16)
    m_ow = k.c["dG"]

    for t in range(NT):
        @k.blk.sync
        def _(sy, t=t, m_yspill=m_yspill):
            if t == 0:
                k.w(sy, "dS", m_yspill)
            k.w(sy, "te", k.c["te"])
            k.w(sy, "ve", k.c["ve"])
            for gi in range(16):
                k.inc(sy.dma_start(A4[:, gi * 128:(gi + 1) * 128],
                                   k.t["loc_hqT"][gi * 128:(gi + 1) * 128, t * 128:(t + 1) * 128]), "dS", 16)
            k.inc(sy.dma_start(A5[:, 0:1024], k.t["loc_u"][t * 128:(t + 1) * 128, :]), "dS", 16)
        m_ld = k.c["dS"]

        for ch in range(2):
            bank = pb[ch]

            @k.blk.tensor
            def _(te, t=t, ch=ch, bank=bank, m_ld=m_ld, m_ow=m_ow):
                k.w(te, "dS", m_ld)
                if t == 0 and ch == 0:
                    k.w(te, "dG", m_ow)
                k.w(te, "ve", k.c["ve"])
                last = None
                for gi in range(16):
                    last = te.matmul(bank[:], A4[:, gi * 128:(gi + 1) * 128],
                                     A2[:, gi * 1024 + ch * 512: gi * 1024 + ch * 512 + 512],
                                     start=(gi == 0), stop=(gi == 15))
                k.inc(last, "te")
        m_mm = k.c["te"]

        @k.blk.vector
        def _(v, t=t, m_mm=m_mm):
            k.w(v, "te", m_mm)
            v.tensor_tensor(A5[:, 1024:1536], A5[:, 0:512], pb[0][:], ALU.add)
            k.inc(v.tensor_tensor(A5[:, 1536:2048], A5[:, 512:1024], pb[1][:], ALU.add), "ve")
        m_xp = k.c["ve"]

        @k.blk.sync
        def _(sy, t=t, m_xp=m_xp):
            k.w(sy, "ve", m_xp)
            k.inc(sy.dma_start(k.t["loc_u"][t * 128:(t + 1) * 128, :], A5[:, 1024:2048]), "dS", 16)

    m_xnT2 = ln_from_hbm("loc_u", "b_m_n2g", "b_m_n2b", 294, 310, 326, 342, 358, 390)

    @k.blk.gpsimd
    def _(g):
        k.w(g, "te", k.c["te"])
        k.inc(g.dma_start(A2[:, 0:16384].rearrange("p (g2 d) -> p g2 d", d=2048),
                          k.t["w_m_ffgT"].ap().rearrange("(g2 p) d -> p g2 d", p=128)), "dG", 16)
        k.inc(g.dma_start(A3[:, 0:8192].rearrange("p (g2 d) -> p g2 d", d=1024),
                          k.t["w_m_ffpT"].ap().rearrange("(g2 p) d -> p g2 d", p=128)), "dG", 16)
    m_ffw = k.c["dG"]

    @k.blk.sync
    def _(sy):
        k.w(sy, "ve", k.c["ve"])
        k.w(sy, "se", k.c["se"])
        k.inc(sy.dma_start(A6[:, 0:1024], k.t["b_m_ffgb"][:, 0:1024]), "dS", 16)
        k.inc(sy.dma_start(A5[:, 2048:3072], k.t["b_m_ffgb"][:, 1024:2048]), "dS", 16)
    m_fb = k.c["dS"]

    @k.blk.gpsimd
    def _(g):
        k.w(g, "ve", k.c["ve"])
        k.inc(g.dma_start(A4[:, 3072:4096], k.t["b_m_ffpb"][:, :]), "dG", 16)
    m_fpb = k.c["dG"]

    for t in range(NT):
        for part in range(2):
            for ch in range(2):
                bank = pb[part * 2 + ch]

                @k.blk.tensor
                def _(te, t=t, part=part, ch=ch, bank=bank, m_ffw=m_ffw):
                    if t == 0 and part == 0 and ch == 0:
                        k.w(te, "dG", m_ffw)
                    k.w(te, "se", k.c["se"])
                    k.w(te, "ve", k.c["ve"])
                    last = None
                    for gi in range(8):
                        last = te.matmul(bank[:],
                                         A1[:, gi * 2048 + t * 128: gi * 2048 + t * 128 + 128],
                                         A2[:, gi * 2048 + part * 1024 + ch * 512:
                                              gi * 2048 + part * 1024 + ch * 512 + 512],
                                         start=(gi == 0), stop=(gi == 7))
                    k.inc(last, "te")
        m_gv = k.c["te"]

        @k.blk.vector
        def _(v, t=t, m_gv=m_gv, m_fb=m_fb):
            k.w(v, "te", m_gv)
            k.w(v, "dS", m_fb)
            k.w(v, "se", k.c["se"])
            v.tensor_tensor(A5[:, 0:512], pb[0][:], A6[:, 0:512], ALU.add)
            v.tensor_tensor(A5[:, 512:1024], pb[1][:], A6[:, 512:1024], ALU.add)
            v.tensor_tensor(A5[:, 1024:1536], pb[2][:], A5[:, 2048:2560], ALU.add)
            k.inc(v.tensor_tensor(A5[:, 1536:2048], pb[3][:], A5[:, 2560:3072], ALU.add), "ve")
        m_gb2 = k.c["ve"]

        @k.blk.scalar
        def _(s, t=t, m_gb2=m_gb2):
            k.w(s, "ve", m_gb2)
            k.inc(s.activation(A5[:, 0:1024], A5[:, 0:1024], AF.Silu), "se")
        m_sg2 = k.c["se"]

        @k.blk.vector
        def _(v, t=t, m_sg2=m_sg2):
            k.w(v, "se", m_sg2)
            k.w(v, "te", k.c["te"])
            k.inc(v.tensor_tensor(A4[:, (t % 2) * 1024:(t % 2) * 1024 + 1024],
                                  A5[:, 0:1024], A5[:, 1024:2048], ALU.mult), "ve")
        m_h2 = k.c["ve"]

        @k.blk.tensor
        def _(te, t=t, m_h2=m_h2):
            k.w(te, "ve", m_h2)
            k.w(te, "se", k.c["se"])
            last = None
            for g2 in range(8):
                last = te.transpose(ptb[g2 // 4][:, (g2 % 4) * 128:(g2 % 4) * 128 + 128],
                                    A4[:, (t % 2) * 1024 + g2 * 128: (t % 2) * 1024 + (g2 + 1) * 128],
                                    k.t["identb"][:])
            k.inc(last, "te")
        m_tr2 = k.c["te"]

        @k.blk.scalar
        def _(s, t=t, m_tr2=m_tr2):
            k.w(s, "te", m_tr2)
            last = None
            for g2 in range(8):
                src = ptb[g2 // 4]
                last = s.activation(A4[:, 2048 + g2 * 128: 2048 + (g2 + 1) * 128],
                                    src[:, (g2 % 4) * 128:(g2 % 4) * 128 + 128], AF.Copy)
            k.inc(last, "se")
        m_hT2 = k.c["se"]

        for ch in range(2):
            bank = pb[4 + ch]

            @k.blk.tensor
            def _(te, t=t, ch=ch, bank=bank, m_hT2=m_hT2):
                k.w(te, "se", m_hT2)
                k.w(te, "ve", k.c["ve"])
                last = None
                for g2 in range(8):
                    last = te.matmul(bank[:], A4[:, 2048 + g2 * 128: 2048 + (g2 + 1) * 128],
                                     A3[:, g2 * 1024 + ch * 512: g2 * 1024 + ch * 512 + 512],
                                     start=(g2 == 0), stop=(g2 == 7))
                k.inc(last, "te")
        m_ff = k.c["te"]

        @k.blk.sync
        def _(sy, t=t, m_ff=m_ff):
            k.w(sy, "ve", k.c["ve"])
            k.inc(sy.dma_start(A5[:, 0:1024], k.t["loc_u"][t * 128:(t + 1) * 128, :]), "dS", 16)
        m_xpl = k.c["dS"]

        @k.blk.vector
        def _(v, t=t, m_ff=m_ff, m_xpl=m_xpl):
            k.w(v, "te", m_ff)
            k.w(v, "dS", m_xpl)
            k.w(v, "dG", m_fpb)
            for ch in range(2):
                xt = k.t["xres"][:, t * 1024 + ch * 512: t * 1024 + ch * 512 + 512]
                v.tensor_tensor(A5[:, 1024 + ch * 512:1024 + (ch + 1) * 512], pb[4 + ch][:],
                                A4[:, 3072 + ch * 512:3072 + (ch + 1) * 512], ALU.add)
                v.tensor_tensor(A5[:, 1024 + ch * 512:1024 + (ch + 1) * 512],
                                A5[:, 1024 + ch * 512:1024 + (ch + 1) * 512],
                                A5[:, 0 + ch * 512:0 + (ch + 1) * 512], ALU.add)
            last = None
            for ch in range(2):
                xt = k.t["xres"][:, t * 1024 + ch * 512: t * 1024 + ch * 512 + 512]
                last = v.tensor_tensor(xt, xt, A5[:, 1024 + ch * 512:1024 + (ch + 1) * 512], ALU.add)
            k.inc(last, "ve")


def build(n_sub=N_SUB_TOTAL):
    import contextlib
    nc = bass.Bass("TRN2", target_bir_lowering=False)

    dram = {}

    def din(name, shape, dt=F32):
        dram[name] = nc.dram_tensor(name, shape, dt, kind="ExternalInput")

    def dint(name, shape, dt=F32):
        h = nc.dram_tensor(name, shape, dt, kind="Internal")
        dram[name] = h

    din("x0", [S, D])
    din("identf", [128, 128])
    din("mtrif", [128, 512])
    din("w_l0aa_q", [D, D])
    din("w_l0aa_k", [D, D])
    din("w_l0aa_v", [D, D])
    din("w_l0ab_q", [D, D])
    din("w_l0ab_k", [D, D])
    din("w_l0ab_v", [D, D])
    din("w_l0ab_o", [D, D])
    din("rope_neox", [S, 64])
    din("ctrif", [128, 128])
    din("w_l1aa_q", [D, D])
    din("w_l1aa_k", [D, D])
    din("w_l1aa_v", [D, D])
    din("w_l1aa_o", [D, D])
    din("rope_il", [S, 64])
    din("cm_causal", [128, 4096])
    din("w_l1ab_q", [D, D])
    din("w_l1ab_k", [D, D])
    din("w_l1ab_v", [D, D])
    din("w_l1ab_o", [D, D])
    din("w_l2aa_q", [D, D])
    din("w_l2aa_k", [D, D])
    din("w_l2aa_v", [D, D])
    din("ident1k", [D, D])
    din("w_l2f_wT", [8 * D, D])
    din("w_l2f_b", [8, D])
    din("pe_T", [32, S])
    din("l2sigT", [32, 8])
    din("w_l2ab_q", [D, D])
    din("w_l2ab_k", [D, D])
    din("w_l2ab_v", [D, D])
    din("w_l2ab_o", [D, D])
    din("b_l2ab_qcol", [128, 8])
    din("b_l2ab_kcol", [128, 8])
    din("b_l2ab_v", [1, D])
    din("bkbc", [128, D])
    din("bobc", [128, D])
    din("w_l3aa_q", [D, D])
    din("w_l3aa_k", [D, D])
    din("w_l3aa_v", [D, D])
    din("w_l3aa_o", [D, D])
    din("cm_win", [128, 4096])
    din("w_l3f_gT", [D, 8])
    din("w_l3f_segT", [128, 8])
    din("w_l3f_guT", [8 * D, 2816])
    din("w_l3f_sgT", [D, 2816])
    din("w_l3f_suT", [D, 2816])
    din("w_l3f_dT", [8 * 1408, D])
    din("w_l3f_sdT", [2816, D])
    din("w_m_inT", [D, 4096])
    din("w_m_convw", [2048, 4])
    din("w_m_convb", [128, 16])
    din("w_m_xprojT", [2048, 96])
    din("w_m_dtT", [64, 2048])
    din("w_m_dtb", [128, 16])
    din("w_m_Alog", [2048, 16])
    din("w_m_D", [128, 16])
    din("w_m_outT", [2048, D])
    din("w_m_ffgT", [D, 2048])
    din("w_m_ffpT", [D, D])
    din("b_m_n1g", [128, D])
    din("b_m_n1b", [128, D])
    din("b_m_n2g", [128, D])
    din("b_m_n2b", [128, D])
    din("b_m_ffgb", [128, 2048])
    din("b_m_ffpb", [128, D])
    din("w_l1f_gate", [2816, D])
    din("w_l1f_up", [2816, D])
    din("w_l1f_down", [D, 2816])
    din("w_l0f_w1", [8 * D, D])
    din("w_l0f_w2", [8 * D, D])
    din("w_l0f_wr", [D, 8])
    din("b_l0f_br", [1, 8])
    din("b_l0f_b1T", [D, 8])
    din("b_l0f_b2", [8, D])
    din("bvbc", [128, D])
    din("b_l0f_b2bc", [8 * 128, D])
    out_y = nc.dram_tensor("y0", [S, D], F32, kind="ExternalOutput")
    dram["y0"] = out_y

    if DBG:
        dram["dbg_sm"] = nc.dram_tensor("dbg_sm", [128, 512], F32, kind="ExternalOutput")
        dram["dbg_xnT"] = nc.dram_tensor("dbg_xnT", [128, 16384], BF16, kind="ExternalOutput")
        dram["dbg_k"] = nc.dram_tensor("dbg_k", [S, D], BF16, kind="ExternalOutput")
        dram["dbg_v"] = nc.dram_tensor("dbg_v", [S, D], BF16, kind="ExternalOutput")
        dram["dbg_qT"] = nc.dram_tensor("dbg_qT", [D, S], BF16, kind="ExternalOutput")
        dram["dbg_e"] = nc.dram_tensor("dbg_e", [S, S], F32, kind="ExternalOutput")
        dram["dbg_den"] = nc.dram_tensor("dbg_den", [S, S], F32, kind="ExternalOutput")

    dint("loc_k", [S, D], BF16)
    dint("loc_yT", [D, S], BF16)
    dint("loc_kT", [D, S], BF16)
    dint("loc_qTl", [D, S], BF16)
    dint("loc_kTl", [D, S], BF16)
    dint("dq_gateT", [D, 2816], BF16)
    dint("dq_upT", [D, 2816], BF16)
    dint("dq_downT", [2816, D], BF16)
    dint("loc_xqT", [D, S], BF16)
    dint("loc_hqT", [2816, S], BF16)
    dint("loc_l2selT", [8, 2048], BF16)
    dint("loc_u", [S, D], F32)
    dint("loc_xcT", [2048, S], BF16)
    dint("loc_zT", [2048, S], BF16)
    dint("loc_deltaT", [2048, S], BF16)
    dint("loc_bc", [32, S], F32)
    dint("loc_v", [S, D], BF16)
    dint("loc_qT", [D, S], BF16)
    dint("cc_e_in", [S, S], F32)
    dint("cc_den", [S, S], F32)

    es = contextlib.ExitStack()
    with es:
        t = {}
        t["xres"] = es.enter_context(nc.sbuf_tensor("xres", [128, 16384], F32))
        t["A1"] = es.enter_context(nc.sbuf_tensor("A1", [128, 16384], BF16))
        t["A2"] = es.enter_context(nc.sbuf_tensor("A2", [128, 16384], BF16))
        t["A3"] = es.enter_context(nc.sbuf_tensor("A3", [128, 16384], BF16))
        t["A4"] = es.enter_context(nc.sbuf_tensor("A4", [128, 8192], BF16))
        t["A5"] = es.enter_context(nc.sbuf_tensor("A5", [128, 3072], F32))
        t["A6"] = es.enter_context(nc.sbuf_tensor("A6", [128, 1024], F32))
        t["SM"] = es.enter_context(nc.sbuf_tensor("SM", [128, 512], F32))
        t["identb"] = es.enter_context(nc.sbuf_tensor("identb", [128, 128], BF16))
        t["mtri"] = es.enter_context(nc.sbuf_tensor("mtri", [128, 512], F32))
        t["ctri"] = es.enter_context(nc.sbuf_tensor("ctri", [128, 128], F32))
        t["cmask"] = es.enter_context(nc.sbuf_tensor("cmask", [128, 4096], BF16))
        t["l2sel"] = es.enter_context(nc.sbuf_tensor("l2sel", [128, 128], BF16))
        t["l3w"] = es.enter_context(nc.sbuf_tensor("l3w", [128, 144], F32))
        t["identf32"] = es.enter_context(nc.sbuf_tensor("identf32", [128, 128], F32))
        for i in range(6):
            t[f"pb{i}"] = es.enter_context(nc.psum_tensor(f"pb{i}", [128, 512], F32))
        t["pt0"] = es.enter_context(nc.psum_tensor("pt0", [128, 512], BF16))
        t["pt1"] = es.enter_context(nc.psum_tensor("pt1", [128, 512], BF16))

        sems = {}
        for name in ("dS", "dG", "te", "ve", "se", "ge", "cc", "dK"):
            sems[name] = es.enter_context(nc.semaphore(name))

        t.update(dram)
        blk = es.enter_context(nc.Block())
        k = Ctx(nc, blk, sems, t)

        @blk.gpsimd
        def _(g):
            k.inc(g.dma_start(t["identb"][:], dram["identf"][:]), "dG", 16)

        @blk.sync
        def _(sy):
            k.inc(sy.dma_start(t["identf32"][:], dram["identf"][:]), "dS", 16)
            k.inc(sy.dma_start(t["mtri"][:], dram["mtrif"][:]), "dS", 16)
            k.inc(sy.dma_start(t["ctri"][:], dram["ctrif"][:]), "dS", 16)
            k.inc(sy.dma_start(t["xres"][:].rearrange("p (t d) -> p t d", d=1024),
                               dram["x0"].ap().rearrange("(t p) d -> p t d", p=128)), "dS", 16)

        if ONLY is not None:
            {"l0_ffn": lambda: emit_l0_ffn(k),
             "l0_attn_b": lambda: emit_l0_attn_b(k),
             "l1_ffn": lambda: emit_l1_ffn(k, True),
             "l1_attn_b": lambda: emit_l1_attn_b(k),
             "l2_ffn": lambda: emit_l2_ffn(k, True),
             "l2_attn_b": lambda: emit_l2_attn_b(k),
             "l3_ffn": lambda: emit_l3_ffn(k),
             "l3_attn_b": lambda: emit_l3_attn_b(k),
             }[ONLY]()
            n_sub = 0
        if n_sub >= 1:
            emit_l0_attn_a(k)
        if n_sub >= 2:
            emit_l0_ffn(k)
        if n_sub >= 3:
            emit_l0_attn_b(k)
        if n_sub >= 4:
            emit_l0_ffn(k)
        if n_sub >= 5:
            emit_mha_T(k, dict(wq="w_l1aa_q", wk="w_l1aa_k", wv="w_l1aa_v", wo="w_l1aa_o",
                               rope="il", rt="rope_il", scale=0.125, shiftC=16.0,
                               kind="causal", cm="cm_causal",
                               cm_pats={0: 0, 128: 1, 256: 2, 384: 3}))
        if n_sub >= 6:
            emit_l1_ffn(k, first_call=True)
        if n_sub >= 7:
            emit_l1_attn_b(k)
        if n_sub >= 8:
            emit_l1_ffn(k, first_call=False)
        if n_sub >= 9:
            emit_mha_T(k, dict(wq="w_l2aa_q", wk="w_l2aa_k", wv="w_l2aa_v", wo="ident1k",
                               rope=None, scale=0.125, shiftC=16.0, kind="full"))
        if n_sub >= 10:
            emit_l2_ffn(k, first_call=True)
        if n_sub >= 11:
            emit_l2_attn_b(k)
        if n_sub >= 12:
            emit_l2_ffn(k, first_call=False)
        if n_sub >= 13:
            emit_mha_T(k, dict(wq="w_l3aa_q", wk="w_l3aa_k", wv="w_l3aa_v", wo="w_l3aa_o",
                               rope=None, scale=0.125, shiftC=16.0, kind="window",
                               cm="cm_win",
                               cm_pats={-256: 0, -128: 1, 0: 2, 128: 3, 256: 4, 384: 5, 512: 6, 640: 7}))
        if n_sub >= 14:
            emit_l3_ffn(k)
        if n_sub >= 15:
            emit_l3_attn_b(k)
        if n_sub >= 16:
            emit_l3_ffn(k)

        if DBG:
            @blk.gpsimd
            def _(g):
                k.w(g, "ve", k.c["ve"])
                k.w(g, "se", k.c["se"])
                k.w(g, "te", k.c["te"])
                k.inc(g.dma_start(dram["dbg_xnT"][:, 0:8192], t["A4"][:, 0:8192]), "dG", 16)
                k.inc(g.dma_start(dram["dbg_sm"][:, :], t["SM"][:, 0:512]), "dG", 16)
                k.inc(g.dma_start(dram["dbg_k"][:, :], dram["loc_k"][:, :]), "dG", 16)
                k.inc(g.dma_start(dram["dbg_v"][:, :], dram["loc_v"][:, :]), "dG", 16)
                k.inc(g.dma_start(dram["dbg_qT"][:, :], dram["loc_qT"][:, :]), "dG", 16)
                k.inc(g.dma_start(dram["dbg_e"][:, :], dram["cc_e_in"][:, :]), "dG", 16)
                k.inc(g.dma_start(dram["dbg_den"][:, :], dram["cc_den"][:, :]), "dG", 16)
                g.wait_ge(sems["dG"], k.c["dG"])

        @blk.sync
        def _(sy):
            k.w(sy, "ve", k.c["ve"])
            k.w(sy, "se", k.c["se"])
            k.inc(sy.dma_start(dram["y0"].ap().rearrange("(t p) d -> p t d", p=128),
                               t["xres"][:].rearrange("p (t d) -> p t d", d=1024)), "dS", 16)
            sy.wait_ge(sems["dS"], k.c["dS"])

    return nc


N_SUB = N_SUB_TOTAL


def kernel(**inputs):
    x = np.asarray(inputs["L0_attn_a_x"], dtype=np.float32)
    w = _prep_weights(inputs)
    consts = _prep_consts()
    nc = build(min(N_SUB, N_SUB_TOTAL))
    in_maps = []
    for c in range(NCORE):
        m = {"x0": np.ascontiguousarray(x[c]), "identf": consts["ident"],
             "mtrif": consts["mtri"], "ctrif": consts["ctri"],
             "rope_neox": consts["rope_neox"], "rope_il": consts["rope_il"],
             "cm_causal": consts["cm_causal"], "pe_T": consts["pe_T"],
             "cm_win": consts["cm_win"]}
        m.update(w)
        m.update(_prep_core_consts(c))
        in_maps.append(m)
    res = run_bass_kernel_spmd(nc, in_maps, core_ids=list(range(NCORE)))
    return np.stack([res.results[c]["y0"] for c in range(NCORE)], axis=0)


# version 03
# speedup vs baseline: 1.0033x; 1.0033x over previous
import math
import numpy as np
import concourse.bass as bass
import concourse.mybir as mybir
from concourse.bass_utils import run_bass_kernel_spmd

F32 = mybir.dt.float32
BF16 = mybir.dt.bfloat16
U8 = mybir.dt.uint8
I32 = mybir.dt.int32
AF = mybir.ActivationFunctionType
ALU = mybir.AluOpType
AX = mybir.AxisListType

NCORE = 8
B = 8
S = 2048
D = 1024
NT = S // 128
ND = D // 128
RG = [[0, 1, 2, 3, 4, 5, 6, 7]]

N_SUB_TOTAL = 16
DBG = False
ONLY = None


def _prep_consts():
    c = {}
    c["ident"] = np.eye(128, dtype=np.float32)
    jj = np.arange(128)[:, None]
    ii = np.arange(256)[None, :]
    p0 = (jj <= ii).astype(np.float32)
    p1 = (jj + 128 <= ii).astype(np.float32)
    c["mtri"] = np.concatenate([p0, p1], axis=1)
    jj2 = np.arange(128)[:, None]
    ii2 = np.arange(128)[None, :]
    c["ctri"] = (jj2 <= ii2).astype(np.float32)
    half = 32
    inv_freq = (10000.0 ** (-np.arange(half, dtype=np.float64) / half))
    pos = np.arange(S, dtype=np.float64)
    ang = pos[:, None] * inv_freq[None, :]
    c["rope_neox"] = np.concatenate([np.cos(ang), np.sin(ang)], axis=1).astype(np.float32)
    inv2 = 1.0 / (10000.0 ** (np.arange(0, 64, 2, dtype=np.float64) / 64.0))
    ang2 = pos[:, None] * inv2[None, :]
    c["rope_il"] = np.concatenate([np.cos(ang2), np.sin(ang2)], axis=1).astype(np.float32)
    jj3 = np.arange(128)[:, None]
    ii3 = np.arange(512)[None, :]
    pats = []
    for p in range(4):
        pats.append(np.where(jj3 + p * 128 <= ii3, 0.0, -1e30).astype(np.float32))
    for p in range(4):
        pats.append(np.zeros((128, 512), np.float32))
    c["cm_causal"] = np.concatenate(pats, axis=1)[:, 0:4096]
    wpats = []
    for delta in (-256, -128, 0, 128, 256, 384, 512, 640):
        m = np.where(np.abs(jj3 + delta - ii3) <= 256, 0.0, -1e30).astype(np.float32)
        wpats.append(m)
    c["cm_win"] = np.concatenate(wpats, axis=1)
    import math as _math
    posi = np.arange(S, dtype=np.float64)[:, None]
    divi = np.exp(np.arange(0, 32, 2, dtype=np.float64) * (-_math.log(10000.0) / 32))
    angp = posi * divi
    pe = np.zeros((S, 32), np.float64)
    pe[:, 0::2] = np.sin(angp)
    pe[:, 1::2] = np.cos(angp)
    c["pe_T"] = np.ascontiguousarray(pe.T.astype(np.float32))
    return c


def _prep_core_consts(core):
    return {}


def _prep_weights(inp):
    w = {}
    T = lambda a: np.ascontiguousarray(np.asarray(a).T)
    w["w_l0aa_q"] = T(inp["L0_attn_a_Wq"])
    w["w_l0aa_k"] = T(inp["L0_attn_a_Wk"])
    w["w_l0aa_v"] = T(inp["L0_attn_a_Wv"])
    w["w_l0f_w1"] = np.ascontiguousarray(np.asarray(inp["L0_ffn_W1"]).reshape(8 * D, D))
    w["w_l0f_w2"] = np.ascontiguousarray(np.asarray(inp["L0_ffn_W2"]).reshape(8 * D, D))
    w["w_l0f_wr"] = np.ascontiguousarray(np.asarray(inp["L0_ffn_Wr"]))
    w["b_l0f_br"] = np.ascontiguousarray(np.asarray(inp["L0_ffn_br"]).reshape(1, 8))
    w["b_l0f_b1T"] = T(inp["L0_ffn_b1"])
    w["b_l0f_b2"] = np.ascontiguousarray(np.asarray(inp["L0_ffn_b2"]))
    w["w_l0ab_q"] = T(inp["L0_attn_b_Wq"])
    w["w_l0ab_k"] = T(inp["L0_attn_b_Wk"])
    w["w_l0ab_v"] = T(inp["L0_attn_b_Wv"])
    w["w_l0ab_o"] = T(inp["L0_attn_b_Wo"])
    w["w_l1aa_q"] = np.ascontiguousarray(np.asarray(inp["L1_attn_a_wq"]))
    w["w_l1aa_k"] = np.ascontiguousarray(np.asarray(inp["L1_attn_a_wk"]))
    w["w_l1aa_v"] = np.ascontiguousarray(np.asarray(inp["L1_attn_a_wv"]))
    w["w_l1aa_o"] = np.ascontiguousarray(np.asarray(inp["L1_attn_a_wo"]))
    w["w_l1ab_q"] = np.ascontiguousarray(np.asarray(inp["L1_attn_b_Wq"]))
    w["w_l1ab_k"] = np.ascontiguousarray(np.asarray(inp["L1_attn_b_Wk"]))
    w["w_l1ab_v"] = np.ascontiguousarray(np.asarray(inp["L1_attn_b_Wv"]))
    w["w_l1ab_o"] = np.ascontiguousarray(np.asarray(inp["L1_attn_b_Wo"]))
    w["w_l1f_gate"] = np.ascontiguousarray(np.asarray(inp["L1_ffn_w_gate"]))
    w["w_l1f_up"] = np.ascontiguousarray(np.asarray(inp["L1_ffn_w_up"]))
    w["w_l1f_down"] = np.ascontiguousarray(np.asarray(inp["L1_ffn_w_down"]))
    Wqkv2 = np.asarray(inp["L2_attn_a_W_qkv"])
    w["w_l2aa_q"] = T(Wqkv2[0:1024])
    w["w_l2aa_k"] = T(Wqkv2[1024:2048])
    w["w_l2aa_v"] = T(Wqkv2[2048:3072])
    w["ident1k"] = np.eye(D, dtype=np.float32)
    w["w_l2f_wT"] = np.ascontiguousarray(
        np.asarray(inp["L2_ffn_W"]).transpose(0, 2, 1).reshape(8 * D, D))
    w["w_l2f_b"] = np.ascontiguousarray(np.asarray(inp["L2_ffn_b"]))
    w["l2sigT"] = T(inp["L2_ffn_tile_sigs"])
    w["w_l2ab_q"] = T(inp["L2_attn_b_Wq"])
    w["w_l2ab_k"] = T(inp["L2_attn_b_Wk"])
    w["w_l2ab_v"] = T(inp["L2_attn_b_Wv"])
    w["w_l2ab_o"] = T(inp["L2_attn_b_Wo"])
    w["b_l2ab_qcol"] = np.ascontiguousarray(np.asarray(inp["L2_attn_b_bq"]).reshape(8, 128).T)
    w["b_l2ab_kcol"] = np.ascontiguousarray(np.asarray(inp["L2_attn_b_bk"]).reshape(8, 128).T)
    w["b_l2ab_v"] = np.ascontiguousarray(np.asarray(inp["L2_attn_b_bv"]).reshape(1, D))
    w["bkbc"] = np.ascontiguousarray(np.broadcast_to(np.asarray(inp["L2_attn_b_bk"])[None, :], (128, D)))
    w["bobc"] = np.ascontiguousarray(np.broadcast_to(np.asarray(inp["L2_attn_b_bo"])[None, :], (128, D)))
    w["bvbc"] = np.ascontiguousarray(np.broadcast_to(np.asarray(inp["L2_attn_b_bv"])[None, :], (128, D)))
    w["b_l0f_b2bc"] = np.ascontiguousarray(
        np.broadcast_to(np.asarray(inp["L0_ffn_b2"])[:, None, :], (8, 128, D)).reshape(8 * 128, D))
    Wqkv3 = np.asarray(inp["L3_attn_a_W_qkv"])
    w["w_l3aa_q"] = T(Wqkv3[0:1024])
    w["w_l3aa_k"] = T(Wqkv3[1024:2048])
    w["w_l3aa_v"] = T(Wqkv3[2048:3072])
    w["w_l3aa_o"] = T(inp["L3_attn_a_W_o"])
    w["w_l3f_gT"] = T(inp["L3_ffn_gate_w"])
    w["w_l3f_segT"] = np.ascontiguousarray(np.asarray(inp["L3_ffn_shared_expert_gate_w"]).reshape(8, 128).T)
    w["w_l3f_guT"] = np.ascontiguousarray(
        np.asarray(inp["L3_ffn_experts_gate_up"]).transpose(0, 2, 1).reshape(8 * D, 2816))
    w["w_l3f_sgT"] = T(inp["L3_ffn_shared_gate_w"])
    w["w_l3f_suT"] = T(inp["L3_ffn_shared_up_w"])
    w["w_l3f_dT"] = np.ascontiguousarray(
        np.asarray(inp["L3_ffn_experts_down"]).transpose(0, 2, 1).reshape(8 * 1408, D))
    w["w_l3f_sdT"] = T(inp["L3_ffn_shared_down_w"])
    w["w_m_inT"] = T(inp["L3_attn_b_in_proj_w"])
    w["w_m_convw"] = np.ascontiguousarray(np.asarray(inp["L3_attn_b_conv_w"]).reshape(2048, 4))
    w["w_m_convb"] = np.ascontiguousarray(np.asarray(inp["L3_attn_b_conv_b"]).reshape(16, 128).T)
    w["w_m_xprojT"] = T(inp["L3_attn_b_x_proj_w"])
    w["w_m_dtT"] = T(inp["L3_attn_b_dt_w"])
    w["w_m_dtb"] = np.ascontiguousarray(np.asarray(inp["L3_attn_b_dt_b"]).reshape(16, 128).T)
    w["w_m_Alog"] = np.ascontiguousarray(np.asarray(inp["L3_attn_b_A_log"]))
    w["w_m_D"] = np.ascontiguousarray(np.asarray(inp["L3_attn_b_D"]).reshape(16, 128).T)
    w["w_m_outT"] = T(inp["L3_attn_b_out_w"])
    w["w_m_ffgT"] = T(inp["L3_attn_b_ffg_w"])
    w["w_m_ffpT"] = T(inp["L3_attn_b_ffp_w"])
    bc = lambda a: np.ascontiguousarray(np.broadcast_to(np.asarray(a)[None, :], (128, np.asarray(a).shape[0])))
    w["b_m_n1g"] = bc(inp["L3_attn_b_n1_g"])
    w["b_m_n1b"] = bc(inp["L3_attn_b_n1_b"])
    w["b_m_n2g"] = bc(inp["L3_attn_b_n2_g"])
    w["b_m_n2b"] = bc(inp["L3_attn_b_n2_b"])
    w["b_m_ffgb"] = bc(inp["L3_attn_b_ffg_b"])
    w["b_m_ffpb"] = bc(inp["L3_attn_b_ffp_b"])
    return w


class Ctx:
    def __init__(self, nc, blk, sems, t):
        self.nc = nc
        self.blk = blk
        self.s = sems
        self.c = {n: 0 for n in sems}
        self.t = t

    def inc(self, inst, name, n=1):
        self.c[name] += n
        inst.then_inc(self.s[name], n)
        return self.c[name]

    def w(self, eng, name, val=None):
        if val is None:
            val = self.c[name]
        if val > 0:
            eng.wait_ge(self.s[name], val)


def emit_ln_stats(k):
    nc = k.nc
    SM, A4, A5, A1 = k.t["SM"], k.t["A4"], k.t["A5"], k.t["A1"]
    xres = k.t["xres"]
    ptb = [k.t["pt0"], k.t["pt1"]]

    @k.blk.vector
    def _(v):
        v.wait_ge(k.s["dS"], k.c["dS"])
        k.w(v, "se", k.c["se"])
        last = None
        for t in range(NT):
            last = v.reduce_sum(SM[:, t:t + 1], xres[:, t * 1024:(t + 1) * 1024], axis=AX.X)
        k.inc(last, "ve")
    m_sums = k.c["ve"]

    @k.blk.scalar
    def _(s):
        s.wait_ge(k.s["dS"], k.c["dS"])
        k.w(s, "ve", k.c["ve"])
        ms = {}
        for t in range(NT):
            if t >= 2:
                s.wait_ge(k.s["se"], ms[t - 2])
            ms[t] = k.inc(s.activation(A5[:, (t % 2) * 1024:(t % 2) * 1024 + 1024],
                                       xres[:, t * 1024:(t + 1) * 1024], AF.Square,
                                       accum_out=SM[:, 16 + t:17 + t]), "se")
    m_sq = k.c["se"]

    @k.blk.vector
    def _(v):
        k.w(v, "ve", m_sums)
        k.w(v, "se", m_sq)
        def step(inst):
            m = k.inc(inst, "ve")
            v.wait_ge(k.s["ve"], m)
        step(v.tensor_scalar_mul(SM[:, 32:48], SM[:, 0:16], 1.0 / 1024.0))
        step(v.tensor_scalar_mul(SM[:, 48:64], SM[:, 16:32], 1.0 / 1024.0))
        step(v.tensor_tensor(SM[:, 96:112], SM[:, 32:48], SM[:, 32:48], ALU.mult))
        step(v.tensor_tensor(SM[:, 48:64], SM[:, 48:64], SM[:, 96:112], ALU.subtract))
        k.inc(v.tensor_scalar_add(SM[:, 48:64], SM[:, 48:64], 1e-5), "ve")
    m_var = k.c["ve"]

    @k.blk.scalar
    def _(s):
        k.w(s, "ve", m_var)
        m = k.inc(s.activation(SM[:, 96:112], SM[:, 48:64], AF.Ln), "se")
        s.wait_ge(k.s["se"], m)
        k.inc(s.activation(SM[:, 64:80], SM[:, 96:112], AF.Exp, scale=-0.5), "se")
    m_rst = k.c["se"]

    @k.blk.vector
    def _(v):
        k.w(v, "se", m_rst)
        k.inc(v.scalar_tensor_tensor(SM[:, 80:96], SM[:, 32:48], -1.0, SM[:, 64:80],
                                     ALU.mult, ALU.mult), "ve")
    m_nmr = k.c["ve"]

    return m_nmr


def emit_ln_xnT(k, also_nat=None):
    nc = k.nc
    SM, A4, A5, A1 = k.t["SM"], k.t["A4"], k.t["A5"], k.t["A1"]
    xres = k.t["xres"]
    ptb = [k.t["pt0"], k.t["pt1"]]
    m_nmr = emit_ln_stats(k)
    m_tr = {}
    m_evac = {}
    for t in range(NT):
        xt = xres[:, t * 1024:(t + 1) * 1024]
        xn = A4[:, (t % 2) * 1024:(t % 2) * 1024 + 1024]

        @k.blk.scalar
        def _(s, xt=xt, xn=xn, t=t):
            if t == 0:
                k.w(s, "ve", m_nmr)
            if t >= 2:
                k.w(s, "te", m_tr[t - 2])
            k.inc(s.activation(xn, xt, AF.Identity,
                               bias=SM[:, 80 + t:81 + t], scale=SM[:, 64 + t:65 + t]), "se")
        m_xn = k.c["se"]

        if also_nat is not None:
            @k.blk.vector
            def _(v, xn=xn, t=t, m_xn=m_xn):
                k.w(v, "se", m_xn)
                k.inc(v.tensor_copy(also_nat(t), xn), "ve")

        @k.blk.tensor
        def _(te, xn=xn, t=t, m_xn=m_xn):
            k.w(te, "se", m_xn)
            k.w(te, "dG", k.c["dG"])
            if t >= 1:
                k.w(te, "ve", m_evac[t - 1])
            last = None
            for g in range(8):
                last = te.transpose(ptb[g // 4][:, (g % 4) * 128:(g % 4) * 128 + 128],
                                    xn[:, g * 128:g * 128 + 128], k.t["identb"][:])
            m_tr[t] = k.inc(last, "te")

        @k.blk.vector
        def _(v, t=t):
            k.w(v, "te", m_tr[t])
            last = None
            for g in range(8):
                last = v.tensor_copy(A1[:, g * 2048 + t * 128:g * 2048 + t * 128 + 128],
                                     ptb[g // 4][:, (g % 4) * 128:(g % 4) * 128 + 128])
            m_evac[t] = k.inc(last, "ve")
    return k.c["ve"]


def emit_l0_attn_a(k):
    nc = k.nc
    A1, A2, A3, A4, A5, A6 = (k.t[n] for n in ("A1", "A2", "A3", "A4", "A5", "A6"))
    pb = [k.t[f"pb{i}"] for i in range(6)]
    SM = k.t["SM"]
    m_xnT = emit_ln_xnT(k)
    if DBG:
        @k.blk.sync
        def _(sy):
            k.w(sy, "ve", m_xnT)
            k.inc(sy.dma_start(k.t["dbg_xnT"][:, :], k.t["A1"][:]), "dS", 16)

    @k.blk.gpsimd
    def _(g):
        g.wait_ge(k.s["dS"], 48)
        k.inc(g.dma_start(A2[:, 0:8192].rearrange("p (g d) -> p g d", d=1024),
                    k.t["w_l0aa_q"].ap().rearrange("(g p) d -> p g d", p=128)), "dG", 16)
        k.inc(g.dma_start(A2[:, 8192:16384].rearrange("p (g d) -> p g d", d=1024),
                    k.t["w_l0aa_k"].ap().rearrange("(g p) d -> p g d", p=128)), "dG", 16)
        k.inc(g.dma_start(A3[:, 0:8192].rearrange("p (g d) -> p g d", d=1024),
                    k.t["w_l0aa_v"].ap().rearrange("(g p) d -> p g d", p=128)), "dG", 16)
    m_w = k.c["dG"]

    @k.blk.vector
    def _(v):
        v.memset(SM[:, 200:201], -20.0)

    for which, wofs, warena, ccname in (("k", 8192, A2, "loc_k"), ("v", 0, A3, "loc_v")):
        for t in range(NT):
            for ch in range(2):
                bank = pb[(t * 2 + ch) % 2]

                @k.blk.tensor
                def _(te, t=t, ch=ch, bank=bank, wofs=wofs, warena=warena):
                    if t == 0 and ch == 0:
                        k.w(te, "dG", m_w)
                        k.w(te, "ve", m_xnT)
                    k.w(te, "se", k.c["se"])
                    last = None
                    for gi in range(8):
                        last = te.matmul(bank[:],
                                         A1[:, gi * 2048 + t * 128: gi * 2048 + t * 128 + 128],
                                         warena[:, wofs + gi * 1024 + ch * 512: wofs + gi * 1024 + ch * 512 + 512],
                                         start=(gi == 0), stop=(gi == 7))
                    k.inc(last, "te")
                m_mm = k.c["te"]

                @k.blk.scalar
                def _(s, t=t, ch=ch, bank=bank, m_mm=m_mm):
                    k.w(s, "te", m_mm)
                    k.w(s, "dS", k.c["dS"])
                    dst = A4[:, ((t * 2 + ch) % 4) * 512: ((t * 2 + ch) % 4) * 512 + 512]
                    k.inc(s.activation(dst, bank[:], AF.Copy), "se")
                m_ev = k.c["se"]

                @k.blk.sync
                def _(sy, t=t, ch=ch, m_ev=m_ev, ccname=ccname):
                    k.w(sy, "se", m_ev)
                    src = A4[:, ((t * 2 + ch) % 4) * 512: ((t * 2 + ch) % 4) * 512 + 512]
                    k.inc(sy.dma_start(k.t[ccname][t * 128:(t + 1) * 128, ch * 512:(ch + 1) * 512], src), "dS", 16)

    for gout in range(8):
        for tc in range(4):
            bank = pb[(gout * 4 + tc) % 2]

            @k.blk.tensor
            def _(te, gout=gout, tc=tc, bank=bank):
                k.w(te, "se", k.c["se"])
                last = None
                for gi in range(8):
                    last = te.matmul(bank[:],
                                     A2[:, gi * 1024 + gout * 128: gi * 1024 + gout * 128 + 128],
                                     A1[:, gi * 2048 + tc * 512: gi * 2048 + tc * 512 + 512],
                                     start=(gi == 0), stop=(gi == 7))
                k.inc(last, "te")
            m_mm = k.c["te"]

            @k.blk.scalar
            def _(s, gout=gout, tc=tc, bank=bank, m_mm=m_mm):
                k.w(s, "te", m_mm)
                k.w(s, "dS", k.c["dS"])
                dst = A4[:, ((gout * 4 + tc) % 4) * 512: ((gout * 4 + tc) % 4) * 512 + 512]
                k.inc(s.activation(dst, bank[:], AF.Copy, scale=1.0 / 32.0), "se")
            m_ev = k.c["se"]

            @k.blk.sync
            def _(sy, gout=gout, tc=tc, m_ev=m_ev):
                k.w(sy, "se", m_ev)
                src = A4[:, ((gout * 4 + tc) % 4) * 512: ((gout * 4 + tc) % 4) * 512 + 512]
                k.inc(sy.dma_start(k.t["loc_qT"][gout * 128:(gout + 1) * 128, tc * 512:(tc + 1) * 512], src), "dS", 16)

    m_proj_done = k.c["dS"]

    @k.blk.sync
    def _(sy):
        k.w(sy, "dS", m_proj_done)
        k.w(sy, "te", k.c["te"])
        for g in range(8):
            k.inc(sy.dma_start(A1[:, g * 2048: g * 2048 + 1024],
                               k.t["loc_k"][g * 256: (g + 1) * 256: 2, :]), "dS", 16)
            k.inc(sy.dma_start(A1[:, g * 2048 + 1024: (g + 1) * 2048],
                               k.t["loc_k"][g * 256 + 1: (g + 1) * 256: 2, :]), "dS", 16)
    m_kr = k.c["dS"]

    for ic in range(8):
        @k.blk.sync
        def _(sy, ic=ic):
            if ic == 0:
                k.w(sy, "dS", m_kr)
            k.w(sy, "te", k.c["te"])
            for g in range(8):
                k.inc(sy.dma_start(A3[:, 8192 + g * 256: 8192 + (g + 1) * 256],
                                   k.t["loc_qT"][g * 128:(g + 1) * 128, ic * 256:(ic + 1) * 256]), "dS", 16)
        m_q = k.c["dS"]

        for jt in range(NT):
            bank = pb[jt % 2]

            @k.blk.tensor
            def _(te, ic=ic, jt=jt, bank=bank, m_q=m_q):
                k.w(te, "dS", m_q)
                k.w(te, "se", k.c["se"])
                last = None
                for g in range(8):
                    base = g * 2048 + (jt // 8) * 1024 + (jt % 8) * 128
                    last = te.matmul(bank[:, 0:256],
                                     A1[:, base: base + 128],
                                     A3[:, 8192 + g * 256: 8192 + (g + 1) * 256],
                                     start=(g == 0), stop=(g == 7))
                k.inc(last, "te")
            m_mm = k.c["te"]

            @k.blk.scalar
            def _(s, ic=ic, jt=jt, bank=bank, m_mm=m_mm):
                k.w(s, "te", m_mm)
                k.w(s, "dS", k.c["dS"])
                dst = A5[:, (jt % 4) * 512:(jt % 4) * 512 + 256]
                k.inc(s.activation(dst, bank[:, 0:256], AF.Exp, bias=SM[:, 200:201]), "se")
            m_ev = k.c["se"]

            @k.blk.sync
            def _(sy, ic=ic, jt=jt, m_ev=m_ev):
                k.w(sy, "se", m_ev)
                src = A5[:, (jt % 4) * 512:(jt % 4) * 512 + 256]
                k.inc(sy.dma_start(k.t["cc_e_in"][jt * 128:(jt + 1) * 128, ic * 256:(ic + 1) * 256], src), "dS", 16)

    m_e_done = k.c["dS"]

    @k.blk.gpsimd
    def _(g):
        k.w(g, "dS", m_e_done)
        k.inc(g.collective_compute("AllReduce", ALU.add, replica_groups=RG,
                                   ins=[k.t["cc_e_in"].ap().opt()],
                                   outs=[k.t["cc_den"].ap().opt()]), "cc")
    m_cc = k.c["cc"]

    for ic in range(8):
        for jt in range(NT):
            full_masked = (jt * 128) >= (ic * 256 + 256)
            off = jt * 128 - ic * 256
            partial = off in (0, 128)

            @k.blk.sync
            def _(sy, ic=ic, jt=jt):
                if ic == 0 and jt == 0:
                    k.w(sy, "cc", m_cc)
                k.w(sy, "ve", k.c["ve"])
                k.w(sy, "te", k.c["te"])
                if not full_masked:
                    k.inc(sy.dma_start(A5[:, 0:256],
                                       k.t["cc_e_in"][jt * 128:(jt + 1) * 128, ic * 256:(ic + 1) * 256]), "dS", 16)
                    k.inc(sy.dma_start(A5[:, 256:512],
                                       k.t["cc_den"][jt * 128:(jt + 1) * 128, ic * 256:(ic + 1) * 256]), "dS", 16)
                k.inc(sy.dma_start(A4[:, 1024 + (jt % 2) * 1024: 1024 + (jt % 2) * 1024 + 1024],
                                   k.t["loc_v"][jt * 128:(jt + 1) * 128, :]), "dS", 16)
            m_ld = k.c["dS"]

            @k.blk.vector
            def _(v, ic=ic, jt=jt, m_ld=m_ld, full_masked=full_masked, partial=partial, off=off):
                k.w(v, "dS", m_ld)
                k.w(v, "te", k.c["te"])
                at = A4[:, 0:256]
                if full_masked:
                    k.inc(v.memset(at, 1e-9), "ve")
                else:
                    v.reciprocal(A5[:, 256:512], A5[:, 256:512])
                    v.tensor_tensor(A5[:, 0:256], A5[:, 0:256], A5[:, 256:512], ALU.mult)
                    if partial:
                        mA = k.t["mtri"][:, (off // 128) * 256:(off // 128) * 256 + 256]
                        v.tensor_scalar_add(A5[:, 0:256], A5[:, 0:256], -1e-9)
                        v.tensor_tensor(A5[:, 0:256], A5[:, 0:256], mA, ALU.mult)
                        k.inc(v.tensor_scalar_add(at, A5[:, 0:256], 1e-9), "ve")
                    else:
                        k.inc(v.tensor_copy(at, A5[:, 0:256]), "ve")
            m_at = k.c["ve"]

            @k.blk.tensor
            def _(te, ic=ic, jt=jt, m_at=m_at):
                k.w(te, "ve", m_at)
                if jt == 0:
                    k.w(te, "ve", k.c["ve"])
                at = A4[:, 0:256]
                vv = A4[:, 1024 + (jt % 2) * 1024: 1024 + (jt % 2) * 1024 + 1024]
                last = None
                for isl in range(2):
                    for ch in range(2):
                        last = te.matmul(pb[2 + isl * 2 + ch][:],
                                         at[:, isl * 128: isl * 128 + 128],
                                         vv[:, ch * 512:(ch + 1) * 512],
                                         start=(jt == 0), stop=(jt == 15))
                k.inc(last, "te")
        m_mm = k.c["te"]

        @k.blk.vector
        def _(v, ic=ic, m_mm=m_mm):
            k.w(v, "te", m_mm)
            last = None
            for isl in range(2):
                for ch in range(2):
                    xs = k.t["xres"][:, (2 * ic + isl) * 1024 + ch * 512: (2 * ic + isl) * 1024 + ch * 512 + 512]
                    last = v.tensor_tensor(xs, xs, pb[2 + isl * 2 + ch][:], ALU.add)
            k.inc(last, "ve")


def emit_l0_ffn(k):
    nc = k.nc
    A1, A2, A3, A4, A5, A6 = (k.t[n] for n in ("A1", "A2", "A3", "A4", "A5", "A6"))
    pb = [k.t[f"pb{i}"] for i in range(6)]
    SM = k.t["SM"]
    m_xnT = emit_ln_xnT(k)

    @k.blk.scalar
    def _(s):
        k.inc(s.activation(A5[:, 0:1024], k.t["xres"][:, 0:1024], AF.Identity,
                           bias=SM[:, 80:81], scale=SM[:, 64:65]), "se")
    m_xn32 = k.c["se"]

    @k.blk.sync
    def _(sy):
        k.inc(sy.dma_start(SM[:, 120:184].rearrange("p (g e) -> p g e", e=8),
                           k.t["b_l0f_b1T"].ap().rearrange("(g p) e -> p g e", p=128)), "dS", 16)
        k.inc(sy.dma_start(SM[:, 190:198].rearrange("p (a e) -> p a e", a=1)[0:1],
                           k.t["b_l0f_br"][0:1, :]), "dS", 16)
        k.inc(sy.dma_start(SM[:, 200:264].rearrange("p (g e) -> p g e", e=8),
                           k.t["w_l0f_wr"].ap().rearrange("(g p) e -> p g e", p=128)), "dS", 16)
    m_rload = k.c["dS"]

    @k.blk.tensor
    def _(te):
        k.w(te, "se", m_xn32)
        k.w(te, "ve", k.c["ve"])
        last = None
        for g in range(8):
            bank = pb[g // 4]
            last = te.transpose(bank[:, (g % 4) * 128:(g % 4) * 128 + 128],
                                A5[:, g * 128:(g + 1) * 128], k.t["identf32"][:])
        k.inc(last, "te")
    m_tr0 = k.c["te"]

    @k.blk.vector
    def _(v):
        k.w(v, "te", m_tr0)
        last = None
        for g in range(8):
            bank = pb[g // 4]
            last = v.tensor_copy(A6[:, g:g + 1], bank[:, (g % 4) * 128:(g % 4) * 128 + 1])
        k.inc(last, "ve")
    m_x0col = k.c["ve"]

    @k.blk.tensor
    def _(te):
        k.w(te, "ve", m_x0col)
        k.w(te, "dS", m_rload)
        last = None
        for g in range(8):
            last = te.matmul(pb[2][0:1, 0:8], A6[:, g:g + 1], SM[:, 200 + g * 8:208 + g * 8],
                             start=(g == 0), stop=(g == 7))
        k.inc(last, "te")
    m_sc = k.c["te"]

    @k.blk.vector
    def _(v):
        k.w(v, "te", m_sc)
        def step(inst):
            m = k.inc(inst, "ve")
            v.wait_ge(k.s["ve"], m)
        sc = SM[0:1, 264:272]
        step(v.tensor_tensor(sc, pb[2][0:1, 0:8], SM[0:1, 190:198], ALU.add))
        m1 = SM[0:1, 272:273]
        step(v.reduce_max(m1, sc, axis=AX.X))
        is1 = SM[0:1, 274:282]
        step(v.tensor_single_scalar(is1, sc, m1, ALU.is_equal))
        ones8 = SM[0:1, 284:292]
        step(v.memset(ones8, 1.0))
        cum = SM[0:1, 294:302]
        step(v.tensor_tensor_scan(cum, ones8, is1, 0.0, ALU.mult, ALU.add))
        f1 = SM[0:1, 304:312]
        step(v.tensor_single_scalar(cum, cum, 1.0, ALU.is_equal))
        step(v.tensor_tensor(f1, is1, cum, ALU.mult))
        s2 = SM[0:1, 314:322]
        step(v.scalar_tensor_tensor(s2, f1, -1e30, sc, ALU.mult, ALU.add))
        m2 = SM[0:1, 324:325]
        step(v.reduce_max(m2, s2, axis=AX.X))
        ind = SM[0:1, 326:334]
        step(v.tensor_single_scalar(ind, sc, m2, ALU.is_ge))
        k.inc(v.tensor_scalar_mul(SM[0:1, 326:334], ind, 0.5), "ve")
    m_ind = k.c["ve"]

    @k.blk.vector
    def _(v):
        v.memset(SM[0:1, 336:464], 1.0)
        k.inc(v.memset(A6[:, 512:1024], 0.0), "ve")
    m_ones = k.c["ve"]

    @k.blk.tensor
    def _(te):
        k.w(te, "ve", max(m_ind, m_ones))
        k.inc(te.matmul(pb[3][:, 0:8], SM[0:1, 336:464], SM[0:1, 326:334],
                        start=True, stop=True), "te")
    m_wb = k.c["te"]

    @k.blk.vector
    def _(v):
        k.w(v, "te", m_wb)
        k.inc(v.tensor_copy(SM[:, 112:120], pb[3][:, 0:8]), "ve")
    m_wbc = k.c["ve"]

    if DBG:
        @k.blk.sync
        def _(sy):
            k.w(sy, "ve", m_wbc)
            k.inc(sy.dma_start(k.t["dbg_sm"][:, :], SM[:]), "dS", 16)

    @k.blk.vector
    def _(v):
        k.w(v, "ve", m_wbc)
        k.inc(v.memset(A6[:, 0:1024], 0.0), "ve")

    for e in range(8):
        @k.blk.sync
        def _(sy, e=e):
            k.w(sy, "ve", k.c["ve"])
            k.inc(sy.dma_start(A5[:, 1024:2048],
                               k.t["b_l0f_b2bc"][e * 128:(e + 1) * 128, :]), "dS", 16)
        m_b2 = k.c["dS"]

        @k.blk.vector
        def _(v, e=e, m_b2=m_b2):
            k.w(v, "dS", m_b2)
            k.inc(v.scalar_tensor_tensor(A6[:, 0:1024], A5[:, 1024:2048],
                                         SM[:, 112 + e:113 + e], A6[:, 0:1024],
                                         ALU.mult, ALU.add), "ve")
    m_bsel = k.c["ve"]

    for t in range(NT):
        @k.blk.vector
        def _(v, t=t):
            if t == 0:
                k.w(v, "ve", m_bsel)
            xt = k.t["xres"][:, t * 1024:(t + 1) * 1024]
            k.inc(v.tensor_tensor(xt, xt, A6[:, 0:1024], ALU.add), "ve")

    for e in range(8):
        @k.blk.gpsimd
        def _(g, e=e):
            k.w(g, "te", k.c["te"])
            k.inc(g.dma_start(A2[:, 0:8192].rearrange("p (g2 d) -> p g2 d", d=1024),
                              k.t["w_l0f_w1"][e * 1024:(e + 1) * 1024, :].rearrange("(g2 p) d -> p g2 d", p=128)), "dG", 16)
            k.inc(g.dma_start(A2[:, 8192:16384].rearrange("p (g2 d) -> p g2 d", d=1024),
                              k.t["w_l0f_w2"][e * 1024:(e + 1) * 1024, :].rearrange("(g2 p) d -> p g2 d", p=128)), "dG", 16)
        m_we = k.c["dG"]

        for gout in range(8):
            for tc in range(4):
                bank = pb[(gout * 4 + tc) % 2]

                @k.blk.tensor
                def _(te, e=e, gout=gout, tc=tc, bank=bank, m_we=m_we):
                    k.w(te, "dG", m_we)
                    if gout == 0 and tc == 0:
                        k.w(te, "ve", k.c["ve"])
                    k.w(te, "se", k.c["se"])
                    last = None
                    for gi in range(8):
                        last = te.matmul(bank[:],
                                         A2[:, gi * 1024 + gout * 128: gi * 1024 + gout * 128 + 128],
                                         A1[:, gi * 2048 + tc * 512: gi * 2048 + tc * 512 + 512],
                                         start=(gi == 0), stop=(gi == 7))
                    k.inc(last, "te")
                m_mm = k.c["te"]

                @k.blk.scalar
                def _(s, e=e, gout=gout, tc=tc, bank=bank, m_mm=m_mm):
                    k.w(s, "te", m_mm)
                    k.w(s, "ve", k.c["ve"])
                    dst = A3[:, gout * 2048 + tc * 512: gout * 2048 + (tc + 1) * 512]
                    k.inc(s.activation(dst, bank[:], AF.Gelu,
                                       bias=SM[:, 120 + gout * 8 + e:121 + gout * 8 + e]), "se")
        m_h = k.c["se"]

        @k.blk.vector
        def _(v, e=e, m_h=m_h):
            k.w(v, "se", m_h)
            last = None
            for gout in range(8):
                last = v.tensor_scalar_mul(A3[:, gout * 2048:(gout + 1) * 2048],
                                           A3[:, gout * 2048:(gout + 1) * 2048],
                                           SM[:, 112 + e:113 + e])
            k.inc(last, "ve")
        m_hw = k.c["ve"]

        for t in range(NT):
            for ch in range(2):
                bank = pb[2 + (t * 2 + ch) % 2]

                @k.blk.tensor
                def _(te, e=e, t=t, ch=ch, bank=bank, m_hw=m_hw):
                    k.w(te, "ve", k.c["ve"])
                    last = None
                    for gi in range(8):
                        last = te.matmul(bank[:],
                                         A3[:, gi * 2048 + t * 128: gi * 2048 + t * 128 + 128],
                                         A2[:, 8192 + gi * 1024 + ch * 512: 8192 + gi * 1024 + ch * 512 + 512],
                                         start=(gi == 0), stop=(gi == 7))
                    k.inc(last, "te")
                m_mm = k.c["te"]

                @k.blk.vector
                def _(v, t=t, ch=ch, bank=bank, m_mm=m_mm):
                    k.w(v, "te", m_mm)
                    xt = k.t["xres"][:, t * 1024 + ch * 512: t * 1024 + ch * 512 + 512]
                    k.inc(v.tensor_tensor(xt, xt, bank[:], ALU.add), "ve")


def emit_l0_attn_b(k):
    nc = k.nc
    A1, A2, A3, A4, A5, A6 = (k.t[n] for n in ("A1", "A2", "A3", "A4", "A5", "A6"))
    pb = [k.t[f"pb{i}"] for i in range(6)]
    SM = k.t["SM"]
    m_xnT = emit_ln_xnT(k)

    @k.blk.gpsimd
    def _(g):
        k.w(g, "te", k.c["te"])
        k.inc(g.dma_start(A2[:, 0:8192].rearrange("p (g2 d) -> p g2 d", d=1024),
                          k.t["w_l0ab_q"].ap().rearrange("(g2 p) d -> p g2 d", p=128)), "dG", 16)
        k.inc(g.dma_start(A2[:, 8192:16384].rearrange("p (g2 d) -> p g2 d", d=1024),
                          k.t["w_l0ab_k"].ap().rearrange("(g2 p) d -> p g2 d", p=128)), "dG", 16)
        k.inc(g.dma_start(A3[:, 0:8192].rearrange("p (g2 d) -> p g2 d", d=1024),
                          k.t["w_l0ab_v"].ap().rearrange("(g2 p) d -> p g2 d", p=128)), "dG", 16)
        k.inc(g.dma_start(A3[:, 8192:16384].rearrange("p (g2 d) -> p g2 d", d=1024),
                          k.t["w_l0ab_o"].ap().rearrange("(g2 p) d -> p g2 d", p=128)), "dG", 16)
    m_w = k.c["dG"]

    KVs = A4[:, 7360:7360 + 520]

    @k.blk.vector
    def _(v):
        k.inc(v.memset(KVs, 0.0), "ve")
    m_kvz = k.c["ve"]

    pkv = [pb[4], pb[5]]

    for c in range(NT):
        for which, wofs, warena, aofs in (("q", 0, A2, 0), ("k", 8192, A2, 1024), ("v", 0, A3, 2048)):
            for ch in range(2):
                bank = pb[ch % 2]

                @k.blk.tensor
                def _(te, c=c, ch=ch, bank=bank, wofs=wofs, warena=warena):
                    if c == 0 and wofs == 0 and ch == 0:
                        k.w(te, "dG", m_w)
                        k.w(te, "ve", m_xnT)
                    k.w(te, "se", k.c["se"])
                    k.w(te, "ve", k.c["ve"])
                    last = None
                    for gi in range(8):
                        last = te.matmul(bank[:],
                                         A1[:, gi * 2048 + c * 128: gi * 2048 + c * 128 + 128],
                                         warena[:, wofs + gi * 1024 + ch * 512: wofs + gi * 1024 + ch * 512 + 512],
                                         start=(gi == 0), stop=(gi == 7))
                    k.inc(last, "te")
                m_mm = k.c["te"]

                @k.blk.scalar
                def _(s, c=c, ch=ch, bank=bank, m_mm=m_mm, aofs=aofs):
                    k.w(s, "te", m_mm)
                    k.w(s, "ve", k.c["ve"])
                    k.inc(s.activation(A4[:, aofs + ch * 512: aofs + ch * 512 + 512], bank[:], AF.Copy), "se")
        m_qkv = k.c["se"]

        @k.blk.sync
        def _(sy, c=c):
            k.w(sy, "ve", k.c["ve"])
            k.inc(sy.dma_start(A6[:, 0:64], k.t["rope_neox"][c * 128:(c + 1) * 128, :]), "dS", 16)
        m_rt = k.c["dS"]

        @k.blk.vector
        def _(v, c=c, m_qkv=m_qkv, m_rt=m_rt):
            k.w(v, "se", m_qkv)
            k.w(v, "dS", m_rt)
            cosb = A6[:, 0:32].rearrange("p (a i) -> p a i", a=1).broadcast_to([128, 16, 32])
            sinb = A6[:, 32:64].rearrange("p (a i) -> p a i", a=1).broadcast_to([128, 16, 32])
            for base in (0, 1024):
                x3 = A4[:, base:base + 1024].rearrange("p (h i) -> p h i", i=64)
                xa = x3[:, :, 0:32]
                xb = x3[:, :, 32:64]
                t1 = A4[:, 3072:4096].rearrange("p (h i) -> p h i", i=64)[:, :, 0:32]
                t2 = A4[:, 3072:4096].rearrange("p (h i) -> p h i", i=64)[:, :, 32:64]
                v.tensor_tensor(t1, xa, cosb, ALU.mult)
                v.tensor_tensor(t2, xb, sinb, ALU.mult)
                v.tensor_tensor(t1, t1, t2, ALU.subtract)
                v.tensor_tensor(t2, xa, sinb, ALU.mult)
                v.tensor_tensor(xb, xb, cosb, ALU.mult)
                v.tensor_tensor(xb, t2, xb, ALU.add)
                v.tensor_copy(xa, t1)
            v.tensor_scalar_min(A4[:, 3072:4096], A4[:, 0:1024], 0.0)
            k.inc(v.tensor_scalar_min(A4[:, 4096:5120], A4[:, 1024:2048], 0.0), "ve")
        m_rope = k.c["ve"]

        @k.blk.scalar
        def _(s, c=c, m_rope=m_rope):
            k.w(s, "ve", m_rope)
            s.activation(A4[:, 3072:4096], A4[:, 3072:4096], AF.Exp)
            k.inc(s.activation(A4[:, 4096:5120], A4[:, 4096:5120], AF.Exp), "se")
        m_exp = k.c["se"]

        @k.blk.vector
        def _(v, c=c, m_exp=m_exp):
            k.w(v, "se", m_exp)
            v.tensor_scalar_max(A4[:, 0:1024], A4[:, 0:1024], 0.0)
            v.tensor_tensor(A4[:, 0:1024], A4[:, 0:1024], A4[:, 3072:4096], ALU.add)
            v.tensor_scalar_max(A4[:, 1024:2048], A4[:, 1024:2048], 0.0)
            v.tensor_tensor(A4[:, 1024:2048], A4[:, 1024:2048], A4[:, 4096:5120], ALU.add)
            v.memset(A4[:, 3072:4112], 1.0)
            k.inc(v.tensor_copy(A4[:, 3072:4112].rearrange("p (h i) -> p h i", i=65)[:, :, 0:64],
                                A4[:, 2048:3072].rearrange("p (h i) -> p h i", i=64)), "ve")
        m_phi = k.c["ve"]

        @k.blk.tensor
        def _(te, c=c, m_phi=m_phi):
            k.w(te, "ve", m_phi)
            k.w(te, "se", k.c["se"])
            last = None
            for g in range(8):
                last = te.transpose(k.t["pt0"][:, (g % 4) * 128:(g % 4) * 128 + 128] if g < 4
                                    else k.t["pt1"][:, (g % 4) * 128:(g % 4) * 128 + 128],
                                    A4[:, g * 128:(g + 1) * 128], k.t["identb"][:])
            k.inc(last, "te")
        m_trq = k.c["te"]

        @k.blk.scalar
        def _(s, c=c, m_trq=m_trq):
            k.w(s, "te", m_trq)
            last = None
            for g in range(8):
                src = k.t["pt0"] if g < 4 else k.t["pt1"]
                last = s.activation(A4[:, 4160 + g * 128: 4160 + (g + 1) * 128],
                                    src[:, (g % 4) * 128:(g % 4) * 128 + 128], AF.Copy)
            k.inc(last, "se")
        m_eq = k.c["se"]

        @k.blk.tensor
        def _(te, c=c, m_eq=m_eq):
            k.w(te, "se", m_eq)
            last = None
            for g in range(8):
                last = te.transpose(k.t["pt0"][:, (g % 4) * 128:(g % 4) * 128 + 128] if g < 4
                                    else k.t["pt1"][:, (g % 4) * 128:(g % 4) * 128 + 128],
                                    A4[:, 1024 + g * 128: 1024 + (g + 1) * 128], k.t["identb"][:])
            k.inc(last, "te")
        m_trk = k.c["te"]

        @k.blk.scalar
        def _(s, c=c, m_trk=m_trk):
            k.w(s, "te", m_trk)
            last = None
            for g in range(8):
                src = k.t["pt0"] if g < 4 else k.t["pt1"]
                last = s.activation(A4[:, 5184 + g * 128: 5184 + (g + 1) * 128],
                                    src[:, (g % 4) * 128:(g % 4) * 128 + 128], AF.Copy)
            k.inc(last, "se")
        m_ek = k.c["se"]

        for h in range(16):
            qT_h = A4[:, 4160 + (h // 2) * 128:4160 + (h // 2) * 128 + 128][(h % 2) * 64:(h % 2) * 64 + 64, :]
            kT_h = A4[:, 5184 + (h // 2) * 128:5184 + (h // 2) * 128 + 128][(h % 2) * 64:(h % 2) * 64 + 64, :]
            k_h = A4[:, 1024 + h * 64:1024 + (h + 1) * 64]
            vp_h = A4[:, 3072 + h * 65:3072 + (h + 1) * 65]
            kv_bank = pkv[h // 8]
            kv_sl = kv_bank[(h % 2) * 64:(h % 2) * 64 + 64, ((h % 8) // 2) * 65:((h % 8) // 2) * 65 + 65]
            kvs_sl = KVs[(h % 2) * 64:(h % 2) * 64 + 64, ((h // 2) % 8) * 65:((h // 2) % 8) * 65 + 65] \
                if False else KVs[(h % 2) * 64:(h % 2) * 64 + 64, (h // 2) * 65:(h // 2) * 65 + 65]

            @k.blk.tensor
            def _(te, c=c, h=h, m_ek=m_ek, kT_h=kT_h, qT_h=qT_h):
                k.w(te, "se", m_ek)
                k.w(te, "ve", k.c["ve"])
                k.inc(te.matmul(pb[2][:, 0:128], kT_h, qT_h, start=True, stop=True), "te")
            m_st = k.c["te"]

            @k.blk.vector
            def _(v, c=c, h=h, m_st=m_st):
                k.w(v, "te", m_st)
                k.inc(v.tensor_tensor(A4[:, 6208:6336], pb[2][:, 0:128], k.t["ctri"][:], ALU.mult), "ve")
            m_T = k.c["ve"]

            @k.blk.tensor
            def _(te, c=c, h=h, m_T=m_T, qT_h=qT_h, vp_h=vp_h, kvs_sl=kvs_sl, k_h=k_h, kv_sl=kv_sl):
                k.w(te, "ve", m_T if c > 0 else max(m_T, m_kvz))
                te.matmul(pb[3][:, 0:65], A4[:, 6208:6336], vp_h, start=True, stop=False)
                k.inc(te.matmul(pb[3][:, 0:65], qT_h, kvs_sl, start=False, stop=True), "te")
                k.inc(te.matmul(kv_sl, k_h, vp_h, start=(c == 0), stop=True), "te")
            m_num = k.c["te"]

            @k.blk.vector
            def _(v, c=c, h=h, m_num=m_num, kv_sl=kv_sl, kvs_sl=kvs_sl):
                k.w(v, "te", m_num)
                def step(inst):
                    m = k.inc(inst, "ve")
                    v.wait_ge(k.s["ve"], m)
                dd = SM[:, 300:301]
                step(v.tensor_scalar_max(dd, pb[3][:, 64:65], 1e-6))
                step(v.reciprocal(dd, dd))
                v.tensor_scalar(A4[:, 6336 + h * 64: 6336 + (h + 1) * 64], pb[3][:, 0:64],
                                dd, None, ALU.mult)
                k.inc(v.tensor_copy(kvs_sl, kv_sl), "ve")

        m_y = k.c["ve"]

        @k.blk.tensor
        def _(te, c=c, m_y=m_y):
            k.w(te, "ve", m_y)
            k.w(te, "se", k.c["se"])
            last = None
            for g in range(8):
                last = te.transpose(k.t["pt0"][:, (g % 4) * 128:(g % 4) * 128 + 128] if g < 4
                                    else k.t["pt1"][:, (g % 4) * 128:(g % 4) * 128 + 128],
                                    A4[:, 6336 + g * 128: 6336 + (g + 1) * 128], k.t["identb"][:])
            k.inc(last, "te")
        m_try = k.c["te"]

        @k.blk.scalar
        def _(s, c=c, m_try=m_try):
            k.w(s, "te", m_try)
            k.w(s, "dS", k.c["dS"])
            last = None
            for g in range(8):
                src = k.t["pt0"] if g < 4 else k.t["pt1"]
                last = s.activation(A4[:, 5184 + g * 128: 5184 + (g + 1) * 128],
                                    src[:, (g % 4) * 128:(g % 4) * 128 + 128], AF.Copy)
            k.inc(last, "se")
        m_ey = k.c["se"]

        @k.blk.sync
        def _(sy, c=c, m_ey=m_ey):
            k.w(sy, "se", m_ey)
            for g in range(8):
                k.inc(sy.dma_start(k.t["loc_yT"][g * 128:(g + 1) * 128, c * 128:(c + 1) * 128],
                                   A4[:, 5184 + g * 128: 5184 + (g + 1) * 128]), "dS", 16)

    m_spill = k.c["dS"]

    @k.blk.sync
    def _(sy):
        k.w(sy, "dS", m_spill)
        k.w(sy, "te", k.c["te"])
        k.inc(sy.dma_start(A1[:].rearrange("p (g t) -> p g t", t=2048),
                           k.t["loc_yT"].ap().rearrange("(g p) t -> p g t", p=128)), "dS", 16)
    m_yT = k.c["dS"]

    for t in range(NT):
        for ch in range(2):
            bank = pb[(t * 2 + ch) % 2]

            @k.blk.tensor
            def _(te, t=t, ch=ch, bank=bank, m_yT=m_yT):
                k.w(te, "dS", m_yT)
                k.w(te, "ve", k.c["ve"])
                last = None
                for gi in range(8):
                    last = te.matmul(bank[:],
                                     A1[:, gi * 2048 + t * 128: gi * 2048 + t * 128 + 128],
                                     A3[:, 8192 + gi * 1024 + ch * 512: 8192 + gi * 1024 + ch * 512 + 512],
                                     start=(gi == 0), stop=(gi == 7))
                k.inc(last, "te")
            m_mm = k.c["te"]

            @k.blk.vector
            def _(v, t=t, ch=ch, bank=bank, m_mm=m_mm):
                k.w(v, "te", m_mm)
                xt = k.t["xres"][:, t * 1024 + ch * 512: t * 1024 + ch * 512 + 512]
                k.inc(v.tensor_tensor(xt, xt, bank[:], ALU.add), "ve")


def emit_wo_phase(k, yT_name, w_name):
    nc = k.nc
    A1, A2 = k.t["A1"], k.t["A2"]
    pb = [k.t[f"pb{i}"] for i in range(6)]

    @k.blk.sync
    def _(sy):
        k.w(sy, "dS", k.c["dS"])
        k.w(sy, "dG", k.c["dG"])
        k.w(sy, "te", k.c["te"])
        k.inc(sy.dma_start(A1[:].rearrange("p (g t) -> p g t", t=2048),
                           k.t[yT_name].ap().rearrange("(g p) t -> p g t", p=128)), "dS", 16)

    @k.blk.gpsimd
    def _(g):
        k.w(g, "te", k.c["te"])
        k.inc(g.dma_start(A2[:, 0:8192].rearrange("p (g2 d) -> p g2 d", d=1024),
                          k.t[w_name].ap().rearrange("(g2 p) d -> p g2 d", p=128)), "dG", 16)
    m_ld_s, m_ld_g = k.c["dS"], k.c["dG"]

    for t in range(NT):
        for ch in range(2):
            bank = pb[(t * 2 + ch) % 2]

            @k.blk.tensor
            def _(te, t=t, ch=ch, bank=bank, m_ld_s=m_ld_s, m_ld_g=m_ld_g):
                if t == 0 and ch == 0:
                    k.w(te, "dS", m_ld_s)
                    k.w(te, "dG", m_ld_g)
                k.w(te, "ve", k.c["ve"])
                last = None
                for gi in range(8):
                    last = te.matmul(bank[:],
                                     A1[:, gi * 2048 + t * 128: gi * 2048 + t * 128 + 128],
                                     A2[:, gi * 1024 + ch * 512: gi * 1024 + ch * 512 + 512],
                                     start=(gi == 0), stop=(gi == 7))
                k.inc(last, "te")
            m_mm = k.c["te"]

            @k.blk.vector
            def _(v, t=t, ch=ch, bank=bank, m_mm=m_mm):
                k.w(v, "te", m_mm)
                xt = k.t["xres"][:, t * 1024 + ch * 512: t * 1024 + ch * 512 + 512]
                k.inc(v.tensor_tensor(xt, xt, bank[:], ALU.add), "ve")


def emit_mha_T(k, cfg):
    nc = k.nc
    A1, A2, A3, A4, A5, A6 = (k.t[n] for n in ("A1", "A2", "A3", "A4", "A5", "A6"))
    pb = [k.t[f"pb{i}"] for i in range(6)]
    SM = k.t["SM"]
    m_xnT = emit_ln_xnT(k)

    @k.blk.gpsimd
    def _(g):
        k.w(g, "te", k.c["te"])
        k.inc(g.dma_start(A2[:, 0:8192].rearrange("p (g2 d) -> p g2 d", d=1024),
                          k.t[cfg["wq"]].ap().rearrange("(g2 p) d -> p g2 d", p=128)), "dG", 16)
        k.inc(g.dma_start(A2[:, 8192:16384].rearrange("p (g2 d) -> p g2 d", d=1024),
                          k.t[cfg["wk"]].ap().rearrange("(g2 p) d -> p g2 d", p=128)), "dG", 16)
        k.inc(g.dma_start(A3[:, 0:8192].rearrange("p (g2 d) -> p g2 d", d=1024),
                          k.t[cfg["wv"]].ap().rearrange("(g2 p) d -> p g2 d", p=128)), "dG", 16)
    m_w = k.c["dG"]

    if cfg.get("cm"):
        @k.blk.gpsimd
        def _(g):
            k.w(g, "ve", k.c["ve"])
            k.inc(g.dma_start(k.t["cmask"][:], k.t[cfg["cm"]][:, :]), "dG", 16)

    @k.blk.vector
    def _(v):
        v.memset(SM[:, 300:301], -float(cfg["shiftC"]))
        k.inc(v.memset(SM[0:1, 336:400], 1.0), "ve")

    for t in range(NT):
        for which, wofs, warena, aofs in (("q", 0, A2, 0), ("k", 8192, A2, 1024), ("v", 0, A3, 2048)):
            for ch in range(2):
                bank = pb[ch % 2]

                @k.blk.tensor
                def _(te, t=t, ch=ch, bank=bank, wofs=wofs, warena=warena):
                    if t == 0 and wofs == 0 and ch == 0 and warena is A2:
                        k.w(te, "dG", m_w)
                        k.w(te, "ve", m_xnT)
                    k.w(te, "se", k.c["se"])
                    k.w(te, "ve", k.c["ve"])
                    last = None
                    for gi in range(8):
                        last = te.matmul(bank[:],
                                         A1[:, gi * 2048 + t * 128: gi * 2048 + t * 128 + 128],
                                         warena[:, wofs + gi * 1024 + ch * 512: wofs + gi * 1024 + ch * 512 + 512],
                                         start=(gi == 0), stop=(gi == 7))
                    k.inc(last, "te")
                m_mm = k.c["te"]

                @k.blk.scalar
                def _(s, t=t, ch=ch, bank=bank, m_mm=m_mm, aofs=aofs):
                    k.w(s, "te", m_mm)
                    k.w(s, "ve", k.c["ve"])
                    k.inc(s.activation(A4[:, aofs + ch * 512: aofs + ch * 512 + 512], bank[:], AF.Copy), "se")
        m_qkv = k.c["se"]

        if cfg.get("rope") == "il":
            @k.blk.sync
            def _(sy, t=t):
                k.w(sy, "ve", k.c["ve"])
                k.inc(sy.dma_start(A6[:, 0:64], k.t[cfg["rt"]][t * 128:(t + 1) * 128, :]), "dS", 16)
            m_rt = k.c["dS"]

            @k.blk.vector
            def _(v, t=t, m_qkv=m_qkv, m_rt=m_rt):
                k.w(v, "se", m_qkv)
                k.w(v, "dS", m_rt)
                last = None
                for base in (0, 1024):
                    x3 = A4[:, base:base + 1024].rearrange("p (h i) -> p h i", i=64)
                    xa = x3[:, :, 0:64:2]
                    xb = x3[:, :, 1:64:2]
                    t3 = A4[:, 3072:4096].rearrange("p (h i) -> p h i", i=64)
                    t1 = t3[:, :, 0:64:2]
                    t2 = t3[:, :, 1:64:2]
                    cb = A6[:, 0:32].rearrange("p (a r) -> p a r", a=1).broadcast_to([128, 16, 32])
                    sb = A6[:, 32:64].rearrange("p (a r) -> p a r", a=1).broadcast_to([128, 16, 32])
                    v.tensor_tensor(t1, xa, cb, ALU.mult)
                    v.tensor_tensor(t2, xb, sb, ALU.mult)
                    v.tensor_tensor(t1, t1, t2, ALU.subtract)
                    v.tensor_tensor(t2, xa, sb, ALU.mult)
                    v.tensor_tensor(xb, xb, cb, ALU.mult)
                    v.tensor_tensor(xb, t2, xb, ALU.add)
                    last = v.tensor_copy(xa, t1)
                k.inc(last, "ve")
            m_qkv = k.c["ve"]
            sem_qkv = "ve"
        else:
            sem_qkv = "se"

        for which2, aofs2, dst in (("q", 0, "loc_qT"), ("k", 1024, "loc_kT")):
            @k.blk.tensor
            def _(te, t=t, aofs2=aofs2, m_qkv=m_qkv, sem_qkv=sem_qkv):
                k.w(te, sem_qkv, m_qkv)
                k.w(te, "se", k.c["se"])
                last = None
                for g in range(8):
                    last = te.transpose(k.t["pt0"][:, (g % 4) * 128:(g % 4) * 128 + 128] if g < 4
                                        else k.t["pt1"][:, (g % 4) * 128:(g % 4) * 128 + 128],
                                        A4[:, aofs2 + g * 128: aofs2 + (g + 1) * 128], k.t["identb"][:])
                k.inc(last, "te")
            m_tr = k.c["te"]

            @k.blk.scalar
            def _(s, t=t, m_tr=m_tr):
                k.w(s, "te", m_tr)
                k.w(s, "dS", k.c["dS"])
                last = None
                for g in range(8):
                    src = k.t["pt0"] if g < 4 else k.t["pt1"]
                    last = s.activation(A4[:, 3072 + g * 128: 3072 + (g + 1) * 128],
                                        src[:, (g % 4) * 128:(g % 4) * 128 + 128], AF.Copy)
                k.inc(last, "se")
            m_ev = k.c["se"]

            @k.blk.sync
            def _(sy, t=t, m_ev=m_ev, dst=dst):
                k.w(sy, "se", m_ev)
                for g in range(8):
                    k.inc(sy.dma_start(k.t[dst][g * 128:(g + 1) * 128, t * 128:(t + 1) * 128],
                                       A4[:, 3072 + g * 128: 3072 + (g + 1) * 128]), "dS", 16)

        @k.blk.sync
        def _(sy, t=t, m_qkv=m_qkv, sem_qkv=sem_qkv):
            k.w(sy, sem_qkv, m_qkv)
            k.w(sy, "se", k.c["se"])
            k.inc(sy.dma_start(k.t["loc_v"][t * 128:(t + 1) * 128, :], A4[:, 2048:3072]), "dS", 16)

    m_spill = k.c["dS"]

    @k.blk.sync
    def _(sy):
        k.w(sy, "dS", m_spill)
        k.w(sy, "te", k.c["te"])
        k.inc(sy.dma_start(A1[:].rearrange("p (g t) -> p g t", t=2048),
                           k.t["loc_kT"].ap().rearrange("(g p) t -> p g t", p=128)), "dS", 16)
        k.inc(sy.dma_start(A2[:].rearrange("p (g t) -> p g t", t=2048),
                           k.t["loc_qT"].ap().rearrange("(g p) t -> p g t", p=128)), "dS", 16)
    m_kq = k.c["dS"]

    vp_t = lambda t: (A3[:, t * 1040:(t + 1) * 1040] if t < 15
                      else A5[:, 0:520].bitcast(BF16))
    for t in range(NT):
        @k.blk.sync
        def _(sy, t=t):
            if t == 0:
                k.w(sy, "te", k.c["te"])
            k.w(sy, "ve", k.c["ve"])
            k.inc(sy.dma_start(A4[:, 4096 + (t % 2) * 1024: 4096 + (t % 2) * 1024 + 1024],
                               k.t["loc_v"][t * 128:(t + 1) * 128, :]), "dS", 16)
        m_vl = k.c["dS"]

        @k.blk.vector
        def _(v, t=t, m_vl=m_vl):
            k.w(v, "dS", m_vl)
            vp = vp_t(t)
            v.memset(vp, 1.0)
            k.inc(v.tensor_copy(vp.rearrange("p (h i) -> p h i", i=65)[:, :, 0:64],
                                A4[:, 4096 + (t % 2) * 1024: 4096 + (t % 2) * 1024 + 1024]
                                .rearrange("p (h i) -> p h i", i=64)), "ve")
    m_vp = k.c["ve"]

    kind = cfg["kind"]
    for ic in range(4):
        if kind == "causal":
            jts = list(range(0, 4 * ic + 4))
        elif kind == "window":
            jts = [j for j in range(16) if (4 * ic - 2) <= j <= (4 * ic + 5)]
        else:
            jts = list(range(16))
        for h in range(16):
            g = h // 2
            hp = (h % 2) * 64
            kT_h = lambda jt, g=g, hp=hp: A1[:, g * 2048 + jt * 128: g * 2048 + (jt + 1) * 128][hp:hp + 64, :]
            qT_h = A2[:, g * 2048 + ic * 512: g * 2048 + (ic + 1) * 512][hp:hp + 64, :]

            for jj, jt in enumerate(jts):
                bank = pb[jj % 2]

                @k.blk.tensor
                def _(te, ic=ic, h=h, jt=jt, bank=bank, kT_h=kT_h, qT_h=qT_h):
                    if ic == 0 and h == 0 and jt == jts[0]:
                        k.w(te, "dS", m_kq)
                        k.w(te, "ve", m_vp)
                    k.w(te, "se", k.c["se"])
                    k.inc(te.matmul(bank[:], kT_h(jt), qT_h, start=True, stop=True), "te")
                m_mm = k.c["te"]

                delta = jt * 128 - ic * 512
                pat = cfg.get("cm_pats", {}).get(delta) if cfg.get("cm") else None
                if pat is not None:
                    @k.blk.vector
                    def _(v, bank=bank, m_mm=m_mm, pat=pat):
                        k.w(v, "te", m_mm)
                        k.w(v, "dG", k.c["dG"])
                        k.inc(v.tensor_tensor(bank[:], bank[:], k.t["cmask"][:, pat * 512:(pat + 1) * 512],
                                              ALU.add), "ve")
                    m_mm = ("ve", k.c["ve"])
                else:
                    m_mm = ("te", m_mm)

                @k.blk.scalar
                def _(s, ic=ic, h=h, jt=jt, bank=bank, m_mm=m_mm):
                    k.w(s, m_mm[0], m_mm[1])
                    k.inc(s.activation(A4[:, jt * 512:(jt + 1) * 512], bank[:], AF.Exp,
                                       bias=SM[:, 300:301], scale=float(cfg["scale"])), "se")
            m_e = k.c["se"]

            @k.blk.tensor
            def _(te, ic=ic, h=h, m_e=m_e):
                k.w(te, "se", m_e)
                k.w(te, "ve", k.c["ve"])
                last = None
                for jj, jt in enumerate(jts):
                    last = te.matmul(pb[2][0:65, :], vp_t(jt)[:, h * 65:(h + 1) * 65],
                                     A4[:, jt * 512:(jt + 1) * 512],
                                     start=(jj == 0), stop=(jj == len(jts) - 1))
                k.inc(last, "te")
            m_o = k.c["te"]

            @k.blk.vector
            def _(v, m_o=m_o):
                k.w(v, "te", m_o)
                m = k.inc(v.tensor_scalar_max(A6[0:1, 0:512], pb[2][64:65, :], 1e-30), "ve")
                v.wait_ge(k.s["ve"], m)
                k.inc(v.reciprocal(A6[0:1, 0:512], A6[0:1, 0:512]), "ve")
            m_rc = k.c["ve"]

            @k.blk.tensor
            def _(te, m_rc=m_rc):
                k.w(te, "ve", m_rc)
                k.w(te, "se", k.c["se"])
                k.inc(te.matmul(pb[3][0:64, :], SM[0:1, 336:400], A6[0:1, 0:512],
                                start=True, stop=True), "te")
            m_rb = k.c["te"]

            @k.blk.scalar
            def _(s, m_rb=m_rb):
                k.w(s, "te", m_rb)
                k.w(s, "ve", k.c["ve"])
                k.inc(s.activation(A6[0:64, 0:512], pb[3][0:64, :], AF.Copy), "se")
            m_rbe = k.c["se"]

            @k.blk.vector
            def _(v, m_rbe=m_rbe):
                k.w(v, "se", m_rbe)
                k.w(v, "dG", k.c["dG"])
                k.inc(v.tensor_tensor(A6[0:64, 512:1024], pb[2][0:64, :], A6[0:64, 0:512],
                                      ALU.mult), "ve")
            m_on = k.c["ve"]

            @k.blk.gpsimd
            def _(gp, ic=ic, h=h, m_on=m_on):
                k.w(gp, "ve", m_on)
                k.inc(gp.dma_start(k.t["loc_yT"][h * 64:(h + 1) * 64, ic * 512:(ic + 1) * 512],
                                   A6[0:64, 512:1024]), "dG", 16)

    emit_wo_phase(k, "loc_yT", cfg["wo"])


def emit_fp4qd(k, src, dst, W, n_ap, a1_ap, sg_ap, gsc, gsc2, m_src, src_sem):
    G = W // 16

    @k.blk.scalar
    def _(sc):
        k.w(sc, src_sem, m_src)
        k.inc(sc.activation(sg_ap, src, AF.Sign), "se")
    m_sg = k.c["se"]

    @k.blk.vector
    def _(v):
        k.w(v, src_sem, m_src)
        def ss(inst):
            m = k.inc(inst, "ve")
            v.wait_ge(k.s["ve"], m)
        src3 = src.rearrange("p (g q) -> p g q", q=16)
        ss(v.tensor_reduce(gsc.rearrange("p (g q) -> p g q", q=1), src3, axis=AX.X,
                           op=ALU.max, apply_absolute_value=True))
        ss(v.tensor_scalar(gsc, gsc, 1.0 / 6.0, 1e-12, ALU.mult, ALU.max))
        ss(v.reciprocal(gsc2, gsc))
        rb = gsc2.rearrange("p (g q) -> p g q", q=1).broadcast_to([128, G, 16])
        sb = gsc.rearrange("p (g q) -> p g q", q=1).broadcast_to([128, G, 16])
        n3 = n_ap.rearrange("p (g q) -> p g q", q=16)
        a13 = a1_ap.rearrange("p (g q) -> p g q", q=16)
        v.tensor_tensor(n3, src3, rb, ALU.mult)
        v.tensor_scalar_mul(a1_ap, n_ap, -1.0)
        v.tensor_tensor(n_ap, n_ap, a1_ap, ALU.max)
        v.tensor_single_scalar(a1_ap, n_ap, 0.25, ALU.is_gt)
        for mj in (0.75, 1.25, 1.75):
            v.scalar_tensor_tensor(a1_ap, n_ap, mj, a1_ap, ALU.is_gt, ALU.add)
        v.tensor_single_scalar(dst, n_ap, 2.5, ALU.is_gt)
        v.scalar_tensor_tensor(dst, n_ap, 3.5, dst, ALU.is_gt, ALU.add)
        v.scalar_tensor_tensor(a1_ap, a1_ap, 0.5, dst, ALU.mult, ALU.add)
        v.tensor_single_scalar(n_ap, n_ap, 5.0, ALU.is_gt)
        v.scalar_tensor_tensor(n_ap, n_ap, 2.0, a1_ap, ALU.mult, ALU.add)
        n3b = n_ap.rearrange("p (g q) -> p g q", q=16)
        v.tensor_tensor(n3b, n3b, sb, ALU.mult)
        k.w(v, "se", m_sg)
        k.inc(v.tensor_tensor(dst, n_ap, sg_ap, ALU.mult), "ve")
    return ("ve", k.c["ve"])


def emit_l1_ffn(k, first_call):
    nc = k.nc
    A1, A2, A3, A4, A5, A6 = (k.t[n] for n in ("A1", "A2", "A3", "A4", "A5", "A6"))
    pb = [k.t[f"pb{i}"] for i in range(6)]
    SM = k.t["SM"]
    ptb = [k.t["pt0"], k.t["pt1"]]

    def transpose_spill(src_bf16_off, n_tiles, dst_name, dst_row0, dst_col0, m_ready, ready_sem, tag):
        for i0 in range(0, n_tiles, 8):
            nn = min(8, n_tiles - i0)

            @k.blk.tensor
            def _(te, i0=i0, nn=nn, m_ready=m_ready, ready_sem=ready_sem):
                k.w(te, ready_sem, m_ready)
                k.w(te, "se", k.c["se"])
                last = None
                for j in range(nn):
                    g = j
                    last = te.transpose(ptb[g // 4][:, (g % 4) * 128:(g % 4) * 128 + 128],
                                        A4[:, src_bf16_off + (i0 + j) * 128: src_bf16_off + (i0 + j + 1) * 128],
                                        k.t["identb"][:])
                k.inc(last, "te")
            m_tr = k.c["te"]

            @k.blk.scalar
            def _(s, i0=i0, nn=nn, m_tr=m_tr):
                k.w(s, "te", m_tr)
                k.w(s, "dS", k.c["dS"])
                last = None
                for j in range(nn):
                    src = ptb[j // 4]
                    last = s.activation(A4[:, 6400 + j * 128: 6400 + (j + 1) * 128],
                                        src[:, (j % 4) * 128:(j % 4) * 128 + 128], AF.Copy)
                k.inc(last, "se")
            m_ev = k.c["se"]

            @k.blk.sync
            def _(sy, i0=i0, nn=nn, m_ev=m_ev):
                k.w(sy, "se", m_ev)
                for j in range(nn):
                    k.inc(sy.dma_start(
                        k.t[dst_name][dst_row0 + (i0 + j) * 128: dst_row0 + (i0 + j + 1) * 128,
                                      dst_col0:dst_col0 + 128],
                        A4[:, 6400 + j * 128: 6400 + (j + 1) * 128]), "dS", 16)

    if first_call:
        for wname, dq, nrt, wcols in (("w_l1f_gate", "dq_gateT", 22, 1024),
                                      ("w_l1f_up", "dq_upT", 22, 1024)):
            for rt in range(nrt):
                @k.blk.sync
                def _(sy, wname=wname, rt=rt):
                    k.w(sy, "ve", k.c["ve"])
                    k.w(sy, "te", k.c["te"])
                    k.inc(sy.dma_start(A5[:, 0:1024], k.t[wname][rt * 128:(rt + 1) * 128, :]), "dS", 16)
                m_ld = k.c["dS"]
                mq = emit_fp4qd(k, A5[:, 0:1024], A4[:, 0:1024], 1024,
                                A5[:, 1024:2048], A5[:, 2048:3072], A4[:, 1024:2048],
                                A6[:, 0:64], A6[:, 64:128], m_ld, "dS")
                transpose_spill(0, 8, dq, 0, rt * 128, mq[1], mq[0], f"{wname}{rt}")
        for rt in range(8):
            for chq in range(4):
                @k.blk.sync
                def _(sy, rt=rt, chq=chq):
                    k.w(sy, "ve", k.c["ve"])
                    k.w(sy, "te", k.c["te"])
                    k.inc(sy.dma_start(A5[:, 0:704],
                                       k.t["w_l1f_down"][rt * 128:(rt + 1) * 128, chq * 704:(chq + 1) * 704]), "dS", 16)
                m_ld = k.c["dS"]
                mq = emit_fp4qd(k, A5[:, 0:704], A4[:, chq * 704:(chq + 1) * 704], 704,
                                A5[:, 704:1408], A5[:, 1408:2112], A4[:, 2816 + chq * 704:2816 + (chq + 1) * 704],
                                A6[:, 0:44], A6[:, 64:108], m_ld, "dS")
            transpose_spill(0, 22, "dq_downT", 0, rt * 128, mq[1], mq[0], f"down{rt}")

    m_nmr = emit_ln_stats(k)
    for t in range(NT):
        @k.blk.scalar
        def _(s, t=t):
            if t == 0:
                k.w(s, "ve", m_nmr)
            k.w(s, "ve", k.c["ve"])
            k.w(s, "te", k.c["te"])
            k.inc(s.activation(A5[:, 0:1024], k.t["xres"][:, t * 1024:(t + 1) * 1024], AF.Identity,
                               bias=SM[:, 80 + t:81 + t], scale=SM[:, 64 + t:65 + t]), "se")
        m_xn = k.c["se"]
        mq = emit_fp4qd(k, A5[:, 0:1024], A4[:, 0:1024], 1024,
                        A5[:, 1024:2048], A5[:, 2048:3072], A4[:, 1024:2048],
                        A6[:, 0:64], A6[:, 64:128], m_xn, "se")
        transpose_spill(0, 8, "loc_xqT", 0, t * 128, mq[1], mq[0], f"xq{t}")

    for half in range(2):
        c0 = half * 1408

        @k.blk.gpsimd
        def _(g, half=half):
            k.w(g, "te", k.c["te"])
            k.w(g, "dS", k.c["dS"])
            k.inc(g.dma_start(A2[:, 0:11264].rearrange("p (g2 d) -> p g2 d", d=1408),
                              k.t["dq_gateT"][0:1024, c0:c0 + 1408].rearrange("(g2 p) d -> p g2 d", p=128)), "dG", 16)
            k.inc(g.dma_start(A3[:, 0:11264].rearrange("p (g2 d) -> p g2 d", d=1408),
                              k.t["dq_upT"][0:1024, c0:c0 + 1408].rearrange("(g2 p) d -> p g2 d", p=128)), "dG", 16)
        m_w = k.c["dG"]

        for t in range(NT):
            @k.blk.sync
            def _(sy, t=t):
                k.w(sy, "te", k.c["te"])
                k.inc(sy.dma_start(A4[:, 1024:2048].rearrange("p (g2 d) -> p g2 d", d=128),
                                   k.t["loc_xqT"].ap().rearrange("(g2 p) s -> p g2 s", p=128)
                                   [:, :, t * 128:(t + 1) * 128]), "dS", 16)
            m_xqT = k.c["dS"]

            for which, warena, pbase in (("g", A2, 0), ("u", A3, 3)):
                for ci, (o0, w0) in enumerate(((0, 512), (512, 512), (1024, 384))):
                    bank = pb[pbase + ci]

                    @k.blk.tensor
                    def _(te, t=t, warena=warena, bank=bank, o0=o0, w0=w0, m_xqT=m_xqT):
                        k.w(te, "dG", m_w)
                        k.w(te, "dS", m_xqT)
                        k.w(te, "ve", k.c["ve"])
                        k.w(te, "se", k.c["se"])
                        last = None
                        for gi in range(8):
                            last = te.matmul(bank[:, 0:w0],
                                             A4[:, 1024 + gi * 128: 1024 + (gi + 1) * 128],
                                             warena[:, gi * 1408 + o0: gi * 1408 + o0 + w0],
                                             start=(gi == 0), stop=(gi == 7))
                        k.inc(last, "te")
            m_gu = k.c["te"]

            @k.blk.scalar
            def _(s, t=t, m_gu=m_gu):
                k.w(s, "te", m_gu)
                k.w(s, "ve", k.c["ve"])
                s.activation(A5[:, 0:512], pb[0][:], AF.Silu)
                s.activation(A5[:, 512:1024], pb[1][:], AF.Silu)
                k.inc(s.activation(A5[:, 1024:1408], pb[2][:, 0:384], AF.Silu), "se")
            m_sg = k.c["se"]

            @k.blk.vector
            def _(v, t=t, m_sg=m_sg):
                k.w(v, "se", m_sg)
                v.tensor_tensor(A5[:, 1408:1920], A5[:, 0:512], pb[3][:], ALU.mult)
                v.tensor_tensor(A5[:, 1920:2432], A5[:, 512:1024], pb[4][:], ALU.mult)
                k.inc(v.tensor_tensor(A5[:, 2432:2816], A5[:, 1024:1408], pb[5][:, 0:384], ALU.mult), "ve")
            m_h = k.c["ve"]

            for chq in range(2):
                mq = emit_fp4qd(k, A5[:, 1408 + chq * 704:1408 + (chq + 1) * 704],
                                A4[:, 2048 + chq * 704:2048 + (chq + 1) * 704], 704,
                                A5[:, chq * 704:(chq + 1) * 704],
                                A5[:, (1 - chq) * 704:(2 - chq) * 704],
                                A4[:, 3456 + chq * 704:3456 + (chq + 1) * 704],
                                A6[:, 0:44], A6[:, 64:108], m_h, "ve")
            transpose_spill(2048, 11, "loc_hqT", c0, t * 128, mq[1], mq[0], f"hq{t}")

        @k.blk.gpsimd
        def _(g, half=half):
            k.w(g, "te", k.c["te"])
            k.w(g, "dS", k.c["dS"])
            k.inc(g.dma_start(A2[:, 0:11264].rearrange("p (g2 d) -> p g2 d", d=1024),
                              k.t["dq_downT"][c0:c0 + 1408, :].rearrange("(g2 p) d -> p g2 d", p=128)), "dG", 16)
        m_wd = k.c["dG"]

        @k.blk.sync
        def _(sy, half=half):
            k.w(sy, "dS", k.c["dS"])
            k.w(sy, "te", k.c["te"])
            k.inc(sy.dma_start(A1[:].rearrange("p (g2 s2) -> p g2 s2", s2=2048),
                               k.t["loc_hqT"][c0:c0 + 1024, :].rearrange("(g2 p) s2 -> p g2 s2", p=128)), "dS", 16)
            k.inc(sy.dma_start(A3[:, 0:6144].rearrange("p (g2 s2) -> p g2 s2", s2=2048),
                               k.t["loc_hqT"][c0 + 1024:c0 + 1408, :].rearrange("(g2 p) s2 -> p g2 s2", p=128)), "dS", 16)
        m_hT = k.c["dS"]

        for t in range(NT):
            for ch in range(2):
                bank = pb[(t * 2 + ch) % 2]

                @k.blk.tensor
                def _(te, t=t, ch=ch, bank=bank, m_hT=m_hT, m_wd=m_wd):
                    k.w(te, "dS", m_hT)
                    k.w(te, "dG", m_wd)
                    k.w(te, "ve", k.c["ve"])
                    last = None
                    for gi in range(11):
                        src = (A1[:, gi * 2048 + t * 128: gi * 2048 + t * 128 + 128] if gi < 8
                               else A3[:, (gi - 8) * 2048 + t * 128: (gi - 8) * 2048 + t * 128 + 128])
                        last = te.matmul(bank[:],
                                         src,
                                         A2[:, gi * 1024 + ch * 512: gi * 1024 + ch * 512 + 512],
                                         start=(gi == 0), stop=(gi == 10))
                    k.inc(last, "te")
                m_mm = k.c["te"]

                @k.blk.vector
                def _(v, t=t, ch=ch, bank=bank, m_mm=m_mm):
                    k.w(v, "te", m_mm)
                    xt = k.t["xres"][:, t * 1024 + ch * 512: t * 1024 + ch * 512 + 512]
                    k.inc(v.tensor_tensor(xt, xt, bank[:], ALU.add), "ve")


def emit_l1_attn_b(k):
    nc = k.nc
    A1, A2, A3, A4, A5, A6 = (k.t[n] for n in ("A1", "A2", "A3", "A4", "A5", "A6"))
    pb = [k.t[f"pb{i}"] for i in range(6)]
    SM = k.t["SM"]
    ptb = [k.t["pt0"], k.t["pt1"]]
    A6b = A6[:].bitcast(BF16)
    m_nmr = emit_ln_stats(k)

    for t in range(NT):
        @k.blk.scalar
        def _(s, t=t):
            if t == 0:
                k.w(s, "ve", m_nmr)
            k.w(s, "ve", k.c["ve"])
            k.w(s, "te", k.c["te"])
            m = k.inc(s.activation(A5[:, 0:1024], k.t["xres"][:, t * 1024:(t + 1) * 1024], AF.Identity,
                                   bias=SM[:, 80 + t:81 + t], scale=SM[:, 64 + t:65 + t]), "se")
            s.wait_ge(k.s["se"], m)
            k.inc(s.activation(A4[:, 0:1024], A5[:, 0:1024], AF.Copy), "se")
        m_h = k.c["se"]

        @k.blk.vector
        def _(v, t=t, m_h=m_h):
            k.w(v, "se", m_h)
            k.inc(v.tensor_tensor(A4[:, 1024:2048], A5[:, 0:1024], A4[:, 0:1024], ALU.subtract), "ve")
        m_l = k.c["ve"]

        for src_off, dstname, dst_arena in ((0, "h", A1), (1024, "l", A3)):
            @k.blk.tensor
            def _(te, t=t, src_off=src_off, m_l=m_l):
                k.w(te, "ve", m_l)
                k.w(te, "se", k.c["se"])
                k.w(te, "ve", k.c["ve"])
                last = None
                for g in range(8):
                    last = te.transpose(ptb[g // 4][:, (g % 4) * 128:(g % 4) * 128 + 128],
                                        A4[:, src_off + g * 128: src_off + (g + 1) * 128], k.t["identb"][:])
                k.inc(last, "te")
            m_tr = k.c["te"]

            @k.blk.vector
            def _(v, t=t, m_tr=m_tr, dst_arena=dst_arena):
                k.w(v, "te", m_tr)
                last = None
                for g in range(8):
                    last = v.tensor_copy(dst_arena[:, g * 2048 + t * 128: g * 2048 + t * 128 + 128],
                                         ptb[g // 4][:, (g % 4) * 128:(g % 4) * 128 + 128])
                k.inc(last, "ve")
    m_xsplit = k.c["ve"]

    for wname, dh, dl in (("w_l1ab_q", "loc_qT", "loc_qTl"), ("w_l1ab_k", "loc_kT", "loc_kTl")):
        for gi in range(8):
            @k.blk.sync
            def _(sy, wname=wname, gi=gi):
                k.w(sy, "te", k.c["te"])
                k.w(sy, "se", k.c["se"])
                k.inc(sy.dma_start(A5[:, 0:1024], k.t[wname][gi * 128:(gi + 1) * 128, :]), "dS", 16)
            m_wl = k.c["dS"]

            @k.blk.scalar
            def _(s, gi=gi, m_wl=m_wl):
                k.w(s, "dS", m_wl)
                k.inc(s.activation(A2[:, gi * 1024:(gi + 1) * 1024], A5[:, 0:1024], AF.Copy), "se")
            m_wh = k.c["se"]

            @k.blk.vector
            def _(v, gi=gi, m_wh=m_wh):
                k.w(v, "se", m_wh)
                k.inc(v.tensor_tensor(A2[:, 8192 + gi * 1024: 8192 + (gi + 1) * 1024],
                                      A5[:, 0:1024], A2[:, gi * 1024:(gi + 1) * 1024], ALU.subtract), "ve")
        m_wpair = k.c["ve"]

        for gout in range(8):
            for tc in range(4):
                bank = pb[(gout * 4 + tc) % 2]

                @k.blk.tensor
                def _(te, gout=gout, tc=tc, bank=bank, m_wpair=m_wpair):
                    k.w(te, "ve", m_wpair)
                    k.w(te, "se", k.c["se"])
                    last = None
                    for gi in range(8):
                        wh = A2[:, gi * 1024 + gout * 128: gi * 1024 + gout * 128 + 128]
                        wl = A2[:, 8192 + gi * 1024 + gout * 128: 8192 + gi * 1024 + gout * 128 + 128]
                        xh = A1[:, gi * 2048 + tc * 512: gi * 2048 + tc * 512 + 512]
                        xl = A3[:, gi * 2048 + tc * 512: gi * 2048 + tc * 512 + 512]
                        te.matmul(bank[:], wh, xh, start=(gi == 0), stop=False)
                        te.matmul(bank[:], wl, xh, start=False, stop=False)
                        last = te.matmul(bank[:], wh, xl, start=False, stop=(gi == 7))
                    k.inc(last, "te")
                m_mm = k.c["te"]

                @k.blk.scalar
                def _(s, gout=gout, tc=tc, bank=bank, m_mm=m_mm):
                    k.w(s, "te", m_mm)
                    k.w(s, "dS", k.c["dS"])
                    k.w(s, "ve", k.c["ve"])
                    k.inc(s.activation(A4[:, 2048:2560], bank[:], AF.Copy), "se")
                m_evh = k.c["se"]

                @k.blk.vector
                def _(v, gout=gout, tc=tc, bank=bank, m_evh=m_evh):
                    k.w(v, "se", m_evh)
                    k.inc(v.tensor_tensor(A4[:, 2560:3072], bank[:], A4[:, 2048:2560], ALU.subtract), "ve")
                m_evl = k.c["ve"]

                @k.blk.sync
                def _(sy, gout=gout, tc=tc, m_evl=m_evl, dh=dh, dl=dl):
                    k.w(sy, "ve", m_evl)
                    k.inc(sy.dma_start(k.t[dh][gout * 128:(gout + 1) * 128, tc * 512:(tc + 1) * 512],
                                       A4[:, 2048:2560]), "dS", 16)
                    k.inc(sy.dma_start(k.t[dl][gout * 128:(gout + 1) * 128, tc * 512:(tc + 1) * 512],
                                       A4[:, 2560:3072]), "dS", 16)

    @k.blk.gpsimd
    def _(g):
        k.w(g, "te", k.c["te"])
        k.inc(g.dma_start(A2[:, 0:8192].rearrange("p (g2 d) -> p g2 d", d=1024),
                          k.t["w_l1ab_v"].ap().rearrange("(g2 p) d -> p g2 d", p=128)), "dG", 16)
    m_wv = k.c["dG"]

    for t in range(NT):
        for ch in range(2):
            bank = pb[2 + (t * 2 + ch) % 2]

            @k.blk.tensor
            def _(te, t=t, ch=ch, bank=bank, m_wv=m_wv):
                k.w(te, "dG", m_wv)
                k.w(te, "se", k.c["se"])
                last = None
                for gi in range(8):
                    last = te.matmul(bank[:],
                                     A1[:, gi * 2048 + t * 128: gi * 2048 + t * 128 + 128],
                                     A2[:, gi * 1024 + ch * 512: gi * 1024 + ch * 512 + 512],
                                     start=(gi == 0), stop=(gi == 7))
                k.inc(last, "te")
            m_mm = k.c["te"]

            @k.blk.scalar
            def _(s, t=t, ch=ch, bank=bank, m_mm=m_mm):
                k.w(s, "te", m_mm)
                k.w(s, "dS", k.c["dS"])
                dst2 = A4[:, 2048 + ((t * 2 + ch) % 2) * 512: 2048 + ((t * 2 + ch) % 2) * 512 + 512]
                k.inc(s.activation(dst2, bank[:], AF.Copy), "se")
            m_ev = k.c["se"]

            @k.blk.sync
            def _(sy, t=t, ch=ch, m_ev=m_ev):
                k.w(sy, "se", m_ev)
                src = A4[:, 2048 + ((t * 2 + ch) % 2) * 512: 2048 + ((t * 2 + ch) % 2) * 512 + 512]
                k.inc(sy.dma_start(k.t["loc_v"][t * 128:(t + 1) * 128, ch * 512:(ch + 1) * 512], src), "dS", 16)
    m_spill = k.c["dS"]

    @k.blk.sync
    def _(sy):
        k.w(sy, "dS", m_spill)
        k.w(sy, "te", k.c["te"])
        k.inc(sy.dma_start(A1[:].rearrange("p (g t2) -> p g t2", t2=2048),
                           k.t["loc_kT"].ap().rearrange("(g p) t2 -> p g t2", p=128)), "dS", 16)
        k.inc(sy.dma_start(A3[:].rearrange("p (g t2) -> p g t2", t2=2048),
                           k.t["loc_kTl"].ap().rearrange("(g p) t2 -> p g t2", p=128)), "dS", 16)
    m_kq = k.c["dS"]

    for qt in range(NT):
        @k.blk.sync
        def _(sy, qt=qt):
            if qt == 0:
                k.w(sy, "dS", m_kq)
            k.w(sy, "te", k.c["te"])
            for g in range(8):
                k.inc(sy.dma_start(A6b[:, g * 128:(g + 1) * 128],
                                   k.t["loc_qT"][g * 128:(g + 1) * 128, qt * 128:(qt + 1) * 128]), "dS", 16)
                k.inc(sy.dma_start(A6b[:, 1024 + g * 128: 1024 + (g + 1) * 128],
                                   k.t["loc_qTl"][g * 128:(g + 1) * 128, qt * 128:(qt + 1) * 128]), "dS", 16)
        m_q = k.c["dS"]

        for jc in range(4):
            bank = pb[jc]

            @k.blk.tensor
            def _(te, qt=qt, jc=jc, bank=bank, m_q=m_q):
                k.w(te, "dS", m_q)
                k.w(te, "se", k.c["se"])
                k.w(te, "ve", k.c["ve"])
                last = None
                for gi in range(8):
                    qh = A6b[:, gi * 128:(gi + 1) * 128]
                    ql = A6b[:, 1024 + gi * 128: 1024 + (gi + 1) * 128]
                    kh = A1[:, gi * 2048 + jc * 512: gi * 2048 + (jc + 1) * 512]
                    kl = A3[:, gi * 2048 + jc * 512: gi * 2048 + (jc + 1) * 512]
                    te.matmul(bank[:], qh, kh, start=(gi == 0), stop=False)
                    te.matmul(bank[:], qh, kl, start=False, stop=False)
                    last = te.matmul(bank[:], ql, kh, start=False, stop=(gi == 7))
                k.inc(last, "te")
        m_sc = k.c["te"]

        @k.blk.vector
        def _(v, qt=qt, m_sc=m_sc):
            k.w(v, "te", m_sc)
            def ss(inst):
                m = k.inc(inst, "ve")
                v.wait_ge(k.s["ve"], m)
            for jc in range(4):
                ss(v.reduce_max(SM[:, 304 + jc:305 + jc], pb[jc][:], axis=AX.X))
            ss(v.tensor_tensor(SM[:, 308:309], SM[:, 304:305], SM[:, 305:306], ALU.max))
            ss(v.tensor_tensor(SM[:, 309:310], SM[:, 306:307], SM[:, 307:308], ALU.max))
            ss(v.tensor_tensor(SM[:, 308:309], SM[:, 308:309], SM[:, 309:310], ALU.max))
            k.inc(v.tensor_scalar_mul(SM[:, 310:311], SM[:, 308:309], -8.0), "ve")
        m_max = k.c["ve"]

        @k.blk.scalar
        def _(s, qt=qt, m_max=m_max):
            k.w(s, "ve", m_max)
            last = None
            for jc in range(4):
                last = s.activation(A4[:, jc * 512:(jc + 1) * 512], pb[jc][:], AF.Exp,
                                    bias=SM[:, 310:311], scale=8.0,
                                    accum_out=SM[:, 312 + jc:313 + jc])
            k.inc(last, "se")
        m_e = k.c["se"]

        @k.blk.vector
        def _(v, qt=qt, m_e=m_e):
            k.w(v, "se", m_e)
            def ss(inst):
                m = k.inc(inst, "ve")
                v.wait_ge(k.s["ve"], m)
            ss(v.tensor_tensor(SM[:, 316:317], SM[:, 312:313], SM[:, 313:314], ALU.add))
            ss(v.tensor_tensor(SM[:, 317:318], SM[:, 314:315], SM[:, 315:316], ALU.add))
            ss(v.tensor_tensor(SM[:, 316:317], SM[:, 316:317], SM[:, 317:318], ALU.add))
            k.inc(v.reciprocal(SM[:, 318:319], SM[:, 316:317]), "ve")
        m_den = k.c["ve"]

        for r in range(2):
            @k.blk.tensor
            def _(te, qt=qt, r=r, m_e=m_e):
                k.w(te, "se", k.c["se"])
                k.w(te, "ve", k.c["ve"])
                last = None
                for j in range(8):
                    jt = r * 8 + j
                    last = te.transpose(ptb[j // 4][:, (j % 4) * 128:(j % 4) * 128 + 128],
                                        A4[:, jt * 128:(jt + 1) * 128], k.t["identb"][:])
                k.inc(last, "te")
            m_tr = k.c["te"]

            @k.blk.vector
            def _(v, qt=qt, r=r, m_tr=m_tr):
                k.w(v, "te", m_tr)
                last = None
                for j in range(8):
                    jt = r * 8 + j
                    last = v.tensor_copy(A4[:, 4096 + jt * 128: 4096 + (jt + 1) * 128],
                                         ptb[j // 4][:, (j % 4) * 128:(j % 4) * 128 + 128])
                k.inc(last, "ve")
        m_at = k.c["ve"]

        for jt in range(NT):
            @k.blk.sync
            def _(sy, qt=qt, jt=jt):
                k.w(sy, "te", k.c["te"])
                k.inc(sy.dma_start(A4[:, 6144 + (jt % 2) * 1024: 6144 + (jt % 2) * 1024 + 1024],
                                   k.t["loc_v"][jt * 128:(jt + 1) * 128, :]), "dS", 16)
            m_vl = k.c["dS"]

            @k.blk.tensor
            def _(te, qt=qt, jt=jt, m_vl=m_vl, m_at=m_at):
                k.w(te, "dS", m_vl)
                k.w(te, "ve", m_at)
                if jt == 0:
                    k.w(te, "ve", k.c["ve"])
                last = None
                for ch in range(2):
                    last = te.matmul(pb[4 + ch][:],
                                     A4[:, 4096 + jt * 128: 4096 + (jt + 1) * 128],
                                     A4[:, 6144 + (jt % 2) * 1024 + ch * 512: 6144 + (jt % 2) * 1024 + ch * 512 + 512],
                                     start=(jt == 0), stop=(jt == 15))
                k.inc(last, "te")
        m_out = k.c["te"]

        @k.blk.vector
        def _(v, qt=qt, m_out=m_out):
            k.w(v, "te", m_out)
            last = None
            for ch in range(2):
                last = v.tensor_scalar(A4[:, 2048 + ch * 512: 2048 + ch * 512 + 512],
                                       pb[4 + ch][:], SM[:, 318:319], None, ALU.mult)
            k.inc(last, "ve")
        m_on = k.c["ve"]

        @k.blk.tensor
        def _(te, qt=qt, m_on=m_on):
            k.w(te, "ve", m_on)
            k.w(te, "se", k.c["se"])
            last = None
            for g in range(8):
                last = te.transpose(ptb[g // 4][:, (g % 4) * 128:(g % 4) * 128 + 128],
                                    A4[:, 2048 + g * 128: 2048 + (g + 1) * 128], k.t["identb"][:])
            k.inc(last, "te")
        m_oT = k.c["te"]

        @k.blk.scalar
        def _(s, qt=qt, m_oT=m_oT):
            k.w(s, "te", m_oT)
            k.w(s, "dS", k.c["dS"])
            last = None
            for g in range(8):
                src = ptb[g // 4]
                last = s.activation(A4[:, 3072 + g * 128: 3072 + (g + 1) * 128],
                                    src[:, (g % 4) * 128:(g % 4) * 128 + 128], AF.Copy)
            k.inc(last, "se")
        m_oe = k.c["se"]

        @k.blk.sync
        def _(sy, qt=qt, m_oe=m_oe):
            k.w(sy, "se", m_oe)
            for g in range(8):
                k.inc(sy.dma_start(k.t["loc_yT"][g * 128:(g + 1) * 128, qt * 128:(qt + 1) * 128],
                                   A4[:, 3072 + g * 128: 3072 + (g + 1) * 128]), "dS", 16)

    emit_wo_phase(k, "loc_yT", "w_l1ab_o")


def emit_l2_ffn(k, first_call):
    nc = k.nc
    A1, A2, A3, A4, A5, A6 = (k.t[n] for n in ("A1", "A2", "A3", "A4", "A5", "A6"))
    pb = [k.t[f"pb{i}"] for i in range(6)]
    SM = k.t["SM"]
    m_xnT = emit_ln_xnT(k)

    if first_call:
        @k.blk.sync
        def _(sy):
            k.w(sy, "ve", k.c["ve"])
            k.w(sy, "se", k.c["se"])
            k.inc(sy.dma_start(A5[0:32, 0:2048], k.t["pe_T"][:, :]), "dS", 16)
            k.inc(sy.dma_start(SM[0:32, 480:488], k.t["l2sigT"][:, :]), "dS", 16)
        m_pe = k.c["dS"]

        @k.blk.scalar
        def _(s):
            k.w(s, "dS", m_pe)
            k.inc(s.activation(SM[0:32, 488:496], SM[0:32, 480:488], AF.Sign), "se")
        m_sig = k.c["se"]

        for t in range(NT):
            @k.blk.tensor
            def _(te, t=t, m_sig=m_sig):
                k.w(te, "se", m_sig)
                k.w(te, "ve", k.c["ve"])
                k.inc(te.matmul(pb[5][:, 0:8], A5[0:32, t * 128:(t + 1) * 128],
                                SM[0:32, 488:496], start=True, stop=True), "te")
            m_sc = k.c["te"]

            @k.blk.vector
            def _(v, t=t, m_sc=m_sc):
                k.w(v, "te", m_sc)
                def ss(inst):
                    m = k.inc(inst, "ve")
                    v.wait_ge(k.s["ve"], m)
                sc = SM[:, 440:448]
                ss(v.tensor_copy(sc, pb[5][:, 0:8]))
                m1 = SM[:, 448:449]
                ss(v.reduce_max(m1, sc, axis=AX.X))
                is1 = SM[:, 450:458]
                ss(v.tensor_single_scalar(is1, sc, m1, ALU.is_equal))
                ones8 = SM[:, 458:466]
                ss(v.memset(ones8, 1.0))
                cum = SM[:, 466:474]
                ss(v.tensor_tensor_scan(cum, ones8, is1, 0.0, ALU.mult, ALU.add))
                ss(v.tensor_single_scalar(cum, cum, 1.0, ALU.is_equal))
                k.inc(v.tensor_tensor(k.t["l2sel"][:, t * 8:(t + 1) * 8], is1, cum, ALU.mult), "ve")
            m_ind = k.c["ve"]

            @k.blk.tensor
            def _(te, t=t, m_ind=m_ind):
                k.w(te, "ve", m_ind)
                k.w(te, "se", k.c["se"])
                k.inc(te.transpose(k.t["pt0"][0:8, 0:128], k.t["l2sel"][:, t * 8:(t + 1) * 8],
                                   k.t["identb"][:]), "te")
            m_tr = k.c["te"]

            @k.blk.scalar
            def _(s, t=t, m_tr=m_tr):
                k.w(s, "te", m_tr)
                k.w(s, "dS", k.c["dS"])
                k.inc(s.activation(A4[:, 0:128][0:8, :], k.t["pt0"][0:8, 0:128], AF.Copy), "se")
            m_se = k.c["se"]

            @k.blk.sync
            def _(sy, t=t, m_se=m_se):
                k.w(sy, "se", m_se)
                k.inc(sy.dma_start(k.t["loc_l2selT"][0:8, t * 128:(t + 1) * 128],
                                   A4[:, 0:128][0:8, :]), "dS", 16)

    for tt in range(8):
        @k.blk.gpsimd
        def _(g, tt=tt):
            k.w(g, "te", k.c["te"])
            k.inc(g.dma_start(A2[:, 0:8192].rearrange("p (g2 d) -> p g2 d", d=1024),
                              k.t["w_l2f_wT"][tt * 1024:(tt + 1) * 1024, :]
                              .rearrange("(g2 p) d -> p g2 d", p=128)), "dG", 16)
        m_w = k.c["dG"]

        for t in range(NT):
            for ch in range(2):
                bank = pb[(t * 2 + ch) % 2]

                @k.blk.tensor
                def _(te, tt=tt, t=t, ch=ch, bank=bank, m_w=m_w):
                    k.w(te, "dG", m_w)
                    if tt == 0 and t == 0 and ch == 0:
                        k.w(te, "ve", m_xnT)
                        k.w(te, "se", k.c["se"])
                    k.w(te, "ve", k.c["ve"])
                    last = None
                    for gi in range(8):
                        last = te.matmul(bank[:],
                                         A1[:, gi * 2048 + t * 128: gi * 2048 + t * 128 + 128],
                                         A2[:, gi * 1024 + ch * 512: gi * 1024 + ch * 512 + 512],
                                         start=(gi == 0), stop=(gi == 7))
                    k.inc(last, "te")
                m_mm = k.c["te"]

                @k.blk.vector
                def _(v, tt=tt, t=t, ch=ch, bank=bank, m_mm=m_mm):
                    k.w(v, "te", m_mm)
                    xt = k.t["xres"][:, t * 1024 + ch * 512: t * 1024 + ch * 512 + 512]
                    k.inc(v.scalar_tensor_tensor(xt, bank[:], k.t["l2sel"][:, t * 8 + tt:t * 8 + tt + 1],
                                                 xt, ALU.mult, ALU.add), "ve")

    A6b2 = A6[:].bitcast(BF16)

    @k.blk.sync
    def _(sy):
        k.w(sy, "ve", k.c["ve"])
        k.w(sy, "te", k.c["te"])
        k.inc(sy.dma_start(A6b2[0:8, 0:2048], k.t["loc_l2selT"][0:8, :]), "dS", 16)

    @k.blk.gpsimd
    def _(g):
        k.w(g, "te", k.c["te"])
        k.w(g, "ve", k.c["ve"])
        k.inc(g.dma_start(A4[:, 0:1024][0:8, :], k.t["w_l2f_b"][:, :]), "dG", 16)
    m_btab = (k.c["dS"], k.c["dG"])

    for t in range(NT):
        for ch in range(2):
            bank = pb[2 + (t * 2 + ch) % 2]

            @k.blk.tensor
            def _(te, t=t, ch=ch, bank=bank):
                if t == 0 and ch == 0:
                    k.w(te, "dS", m_btab[0])
                    k.w(te, "dG", m_btab[1])
                k.w(te, "se", k.c["se"])
                k.w(te, "ve", k.c["ve"])
                k.inc(te.matmul(bank[:], A6b2[0:8, t * 128:(t + 1) * 128],
                                A4[:, 0:1024][0:8, ch * 512:(ch + 1) * 512],
                                start=True, stop=True), "te")
            m_mm = k.c["te"]

            @k.blk.vector
            def _(v, t=t, ch=ch, bank=bank, m_mm=m_mm):
                k.w(v, "te", m_mm)
                xt = k.t["xres"][:, t * 1024 + ch * 512: t * 1024 + ch * 512 + 512]
                k.inc(v.tensor_tensor(xt, xt, bank[:], ALU.add), "ve")


def emit_l2_attn_b(k):
    nc = k.nc
    A1, A2, A3, A4, A5, A6 = (k.t[n] for n in ("A1", "A2", "A3", "A4", "A5", "A6"))
    pb = [k.t[f"pb{i}"] for i in range(6)]
    SM = k.t["SM"]
    m_xnT = emit_ln_xnT(k)

    @k.blk.gpsimd
    def _(g):
        k.w(g, "te", k.c["te"])
        k.inc(g.dma_start(A2[:, 0:8192].rearrange("p (g2 d) -> p g2 d", d=1024),
                          k.t["w_l2ab_q"].ap().rearrange("(g2 p) d -> p g2 d", p=128)), "dG", 16)
        k.inc(g.dma_start(A2[:, 8192:16384].rearrange("p (g2 d) -> p g2 d", d=1024),
                          k.t["w_l2ab_k"].ap().rearrange("(g2 p) d -> p g2 d", p=128)), "dG", 16)
        k.inc(g.dma_start(A3[:, 0:8192].rearrange("p (g2 d) -> p g2 d", d=1024),
                          k.t["w_l2ab_v"].ap().rearrange("(g2 p) d -> p g2 d", p=128)), "dG", 16)
        k.inc(g.dma_start(A3[:, 8192:16384].rearrange("p (g2 d) -> p g2 d", d=1024),
                          k.t["w_l2ab_o"].ap().rearrange("(g2 p) d -> p g2 d", p=128)), "dG", 16)
    m_w = k.c["dG"]

    @k.blk.sync
    def _(sy):
        k.w(sy, "ve", k.c["ve"])
        k.inc(sy.dma_start(SM[:, 480:488], k.t["b_l2ab_qcol"][:, :]), "dS", 16)
        k.inc(sy.dma_start(A6[:, 0:1024], k.t["bvbc"][:, :]), "dS", 16)
        k.inc(sy.dma_start(A5[:, 1024:2048], k.t["bkbc"][:, :]), "dS", 16)
    m_b = k.c["dS"]

    for t in range(NT):
        base = (t % 2) * 2048
        m_kv_staged = None
        for ch in range(2):
            bank = pb[ch]
            bankv = pb[2 + ch]

            @k.blk.tensor
            def _(te, t=t, ch=ch, bank=bank, bankv=bankv):
                if t == 0 and ch == 0:
                    k.w(te, "dG", m_w)
                    k.w(te, "ve", m_xnT)
                k.w(te, "se", k.c["se"])
                k.w(te, "ve", k.c["ve"])
                last = None
                for gi in range(8):
                    te.matmul(bank[:],
                              A1[:, gi * 2048 + t * 128: gi * 2048 + t * 128 + 128],
                              A2[:, 8192 + gi * 1024 + ch * 512: 8192 + gi * 1024 + ch * 512 + 512],
                              start=(gi == 0), stop=(gi == 7))
                    last = te.matmul(bankv[:],
                                     A1[:, gi * 2048 + t * 128: gi * 2048 + t * 128 + 128],
                                     A3[:, gi * 1024 + ch * 512: gi * 1024 + ch * 512 + 512],
                                     start=(gi == 0), stop=(gi == 7))
                k.inc(last, "te")
            m_mm = k.c["te"]

            @k.blk.vector
            def _(v, t=t, ch=ch, bank=bank, m_mm=m_mm):
                k.w(v, "te", m_mm)
                k.w(v, "dS", m_b)
                k.w(v, "se", k.c["se"])
                k.w(v, "ve", k.c["ve"])
                m = k.inc(v.tensor_tensor(A5[:, 0:512], bank[:], A5[:, 1024 + ch * 512:1024 + (ch + 1) * 512], ALU.add), "ve")
                v.wait_ge(k.s["ve"], m)
                k.inc(v.tensor_scalar_min(A5[:, 512:1024], A5[:, 0:512], 0.0), "ve")
            m_pre = k.c["ve"]

            @k.blk.scalar
            def _(s, t=t, ch=ch, m_pre=m_pre):
                k.w(s, "ve", m_pre)
                k.inc(s.activation(A5[:, 512:1024], A5[:, 512:1024], AF.Exp), "se")
            m_exp = k.c["se"]

            @k.blk.vector
            def _(v, t=t, ch=ch, bankv=bankv, m_exp=m_exp, base=base):
                k.w(v, "se", m_exp)
                k.w(v, "ve", k.c["ve"])
                ksl = A4[:, base + ch * 512: base + (ch + 1) * 512]
                vsl = A4[:, base + 1024 + ch * 512: base + 1024 + (ch + 1) * 512]
                m = k.inc(v.tensor_scalar_max(A5[:, 0:512], A5[:, 0:512], 0.0), "ve")
                v.wait_ge(k.s["ve"], m)
                m = k.inc(v.tensor_tensor(ksl, A5[:, 0:512], A5[:, 512:1024], ALU.add), "ve")
                v.wait_ge(k.s["ve"], m)
                k.inc(v.tensor_tensor(vsl, bankv[:], A6[:, ch * 512:(ch + 1) * 512], ALU.add), "ve")
            m_kv_staged = k.c["ve"]

            if DBG:
                @k.blk.sync
                def _(sy, t=t, ch=ch, m_kv_staged=m_kv_staged, base=base):
                    k.w(sy, "ve", m_kv_staged)
                    k.inc(sy.dma_start(k.t["loc_k"][t * 128:(t + 1) * 128, ch * 512:(ch + 1) * 512],
                                       A4[:, base + ch * 512: base + (ch + 1) * 512]), "dS", 16)
                    k.inc(sy.dma_start(k.t["loc_v"][t * 128:(t + 1) * 128, ch * 512:(ch + 1) * 512],
                                       A4[:, base + 1024 + ch * 512: base + 1024 + (ch + 1) * 512]), "dS", 16)

        @k.blk.tensor
        def _(te, t=t, base=base, m_kv_staged=m_kv_staged):
            k.w(te, "ve", m_kv_staged)
            last = None
            for h in range(16):
                off = (h // 8) * 512 + (h % 8) * 64
                kk = A4[:, base + off: base + off + 64]
                vv = A4[:, base + 1024 + off: base + 1024 + off + 64]
                last = te.matmul(pb[4][(h % 2) * 64:(h % 2) * 64 + 64, (h // 2) * 64:(h // 2) * 64 + 64],
                                 kk, vv, start=True, stop=True)
            k.inc(last, "te")
        m_kvmm = k.c["te"]

        @k.blk.vector
        def _(v, t=t, m_kvmm=m_kvmm):
            k.w(v, "te", m_kvmm)
            KV = A5[:, 2048:2560]
            if t == 0:
                k.inc(v.tensor_copy(KV, pb[4][:]), "ve")
            else:
                k.inc(v.tensor_tensor(KV, pb[4][:], KV, ALU.add), "ve")
    m_kv = k.c["ve"]

    @k.blk.vector
    def _(v):
        k.w(v, "ve", m_kv)
        k.inc(v.tensor_copy(A4[:, 0:512], A5[:, 2048:2560]), "ve")
    m_kvs = k.c["ve"]

    for tc in range(4):
        for gout in range(8):
            bank = pb[2 + (tc * 8 + gout) % 2]

            @k.blk.tensor
            def _(te, tc=tc, gout=gout, bank=bank):
                k.w(te, "se", k.c["se"])
                k.w(te, "ve", k.c["ve"])
                last = None
                for gi in range(8):
                    last = te.matmul(bank[:],
                                     A2[:, gi * 1024 + gout * 128: gi * 1024 + gout * 128 + 128],
                                     A1[:, gi * 2048 + tc * 512: gi * 2048 + tc * 512 + 512],
                                     start=(gi == 0), stop=(gi == 7))
                k.inc(last, "te")
            m_mm = k.c["te"]

            @k.blk.vector
            def _(v, tc=tc, gout=gout, bank=bank, m_mm=m_mm):
                k.w(v, "te", m_mm)
                k.w(v, "se", k.c["se"])
                v.tensor_scalar(A5[:, 0:512], bank[:], SM[:, 480 + gout:481 + gout], 0.0, ALU.add, ALU.min)
                k.inc(v.tensor_scalar(A5[:, 512:1024], bank[:], SM[:, 480 + gout:481 + gout], 0.0, ALU.add, ALU.max), "ve")
            m_pre = k.c["ve"]

            @k.blk.scalar
            def _(s, tc=tc, gout=gout, m_pre=m_pre):
                k.w(s, "ve", m_pre)
                k.inc(s.activation(A5[:, 0:512], A5[:, 0:512], AF.Exp), "se")
            m_exp = k.c["se"]

            @k.blk.vector
            def _(v, tc=tc, gout=gout, m_exp=m_exp):
                k.w(v, "se", m_exp)
                k.w(v, "ve", k.c["ve"])
                k.inc(v.tensor_tensor(A4[:, 4096 + gout * 512: 4096 + (gout + 1) * 512],
                                      A5[:, 0:512], A5[:, 512:1024], ALU.add), "ve")
        m_qstage = k.c["ve"]

        @k.blk.vector
        def _(v, tc=tc, m_qstage=m_qstage):
            k.w(v, "te", k.c["te"])
            last = None
            for g in range(8):
                last = v.tensor_copy(A1[:, g * 2048 + tc * 512: g * 2048 + (tc + 1) * 512],
                                     A4[:, 4096 + g * 512: 4096 + (g + 1) * 512])
            k.inc(last, "ve")
        if DBG:
            @k.blk.sync
            def _(sy, tc=tc):
                k.w(sy, "ve", k.c["ve"])
                for g in range(8):
                    k.inc(sy.dma_start(k.t["loc_qT"][g * 128:(g + 1) * 128, tc * 512:(tc + 1) * 512],
                                       A1[:, g * 2048 + tc * 512: g * 2048 + (tc + 1) * 512]), "dS", 16)
    m_qT = k.c["ve"]

    @k.blk.sync
    def _(sy):
        k.w(sy, "ve", k.c["ve"])
        k.inc(sy.dma_start(A6[:, 0:1024], k.t["bobc"][:, :]), "dS", 16)
    m_bo = k.c["dS"]

    for ic in range(4):
        for hp in range(8):
            @k.blk.tensor
            def _(te, ic=ic, hp=hp, m_qT=m_qT, m_kvs=m_kvs):
                if ic == 0 and hp == 0:
                    k.w(te, "ve", max(m_qT, m_kvs))
                k.w(te, "se", k.c["se"])
                last = None
                for sub in range(2):
                    h = hp * 2 + sub
                    kv_sl = A4[:, 0:512][(h % 2) * 64:(h % 2) * 64 + 64, (h // 2) * 64:(h // 2) * 64 + 64]
                    qT_h = A1[:, (h // 2) * 2048 + ic * 512: (h // 2) * 2048 + (ic + 1) * 512][(h % 2) * 64:(h % 2) * 64 + 64, :]
                    last = te.matmul(pb[5][sub * 64:sub * 64 + 64, :], kv_sl, qT_h, start=True, stop=True,
                                     tile_position=(sub * 64, sub * 64))
                k.inc(last, "te")
            m_ot = k.c["te"]

            @k.blk.scalar
            def _(s, ic=ic, hp=hp, m_ot=m_ot):
                k.w(s, "te", m_ot)
                k.inc(s.activation(A4[:, 1024 + hp * 512: 1024 + (hp + 1) * 512], pb[5][:], AF.Copy), "se")
        m_allo = k.c["se"]

        for isl in range(4):
            for ch in range(2):
                bank = pb[(isl * 2 + ch) % 2]

                @k.blk.tensor
                def _(te, ic=ic, isl=isl, ch=ch, bank=bank, m_allo=m_allo):
                    k.w(te, "se", m_allo)
                    k.w(te, "ve", k.c["ve"])
                    last = None
                    for hp in range(8):
                        last = te.matmul(bank[:],
                                         A4[:, 1024 + hp * 512 + isl * 128: 1024 + hp * 512 + isl * 128 + 128],
                                         A3[:, 8192 + hp * 1024 + ch * 512: 8192 + hp * 1024 + ch * 512 + 512],
                                         start=(hp == 0), stop=(hp == 7))
                    k.inc(last, "te")
                m_mm = k.c["te"]

                @k.blk.vector
                def _(v, ic=ic, isl=isl, ch=ch, bank=bank, m_mm=m_mm):
                    k.w(v, "te", m_mm)
                    t = ic * 4 + isl
                    xt = k.t["xres"][:, t * 1024 + ch * 512: t * 1024 + ch * 512 + 512]
                    k.w(v, "dS", m_bo)
                    k.w(v, "ve", k.c["ve"])
                    m = k.inc(v.tensor_tensor(A5[:, 0:512], bank[:], A6[:, ch * 512:(ch + 1) * 512], ALU.add), "ve")
                    v.wait_ge(k.s["ve"], m)
                    k.inc(v.tensor_tensor(xt, xt, A5[:, 0:512], ALU.add), "ve")




def emit_l3_ffn(k):
    nc = k.nc
    A1, A2, A3, A4, A5, A6 = (k.t[n] for n in ("A1", "A2", "A3", "A4", "A5", "A6"))
    pb = [k.t[f"pb{i}"] for i in range(6)]
    SM = k.t["SM"]
    m_xnT = emit_ln_xnT(k)

    @k.blk.sync
    def _(sy):
        k.w(sy, "ve", k.c["ve"])
        k.inc(sy.dma_start(SM[:, 200:264].rearrange("p (g e) -> p g e", e=8),
                           k.t["w_l3f_gT"].ap().rearrange("(g p) e -> p g e", p=128)), "dS", 16)
        k.inc(sy.dma_start(SM[:, 264:272], k.t["w_l3f_segT"][:, :]), "dS", 16)
    m_gw = k.c["dS"]

    for t in range(NT):
        @k.blk.scalar
        def _(s, t=t):
            k.w(s, "ve", k.c["ve"])
            k.w(s, "te", k.c["te"])
            k.inc(s.activation(A5[:, 0:1024], k.t["xres"][:, t * 1024:(t + 1) * 1024], AF.Identity,
                               bias=SM[:, 80 + t:81 + t], scale=SM[:, 64 + t:65 + t]), "se")
        m_xn32 = k.c["se"]

        @k.blk.tensor
        def _(te, t=t, m_xn32=m_xn32):
            k.w(te, "se", m_xn32)
            k.w(te, "ve", k.c["ve"])
            last = None
            for g in range(8):
                last = te.transpose(pb[g // 4][:, (g % 4) * 128:(g % 4) * 128 + 128],
                                    A5[:, g * 128:(g + 1) * 128], k.t["identf32"][:])
            k.inc(last, "te")
        m_tr = k.c["te"]

        @k.blk.vector
        def _(v, t=t, m_tr=m_tr):
            k.w(v, "te", m_tr)
            last = None
            for g in range(8):
                last = v.tensor_copy(A6[:, g * 128:(g + 1) * 128],
                                     pb[g // 4][:, (g % 4) * 128:(g % 4) * 128 + 128])
            k.inc(last, "ve")
        m_x32 = k.c["ve"]

        @k.blk.tensor
        def _(te, t=t, m_x32=m_x32):
            k.w(te, "ve", m_x32)
            k.w(te, "dS", m_gw)
            k.w(te, "se", k.c["se"])
            last = None
            for g in range(8):
                te.matmul(pb[4][:, 0:8], A6[:, g * 128:(g + 1) * 128], SM[:, 200 + g * 8:208 + g * 8],
                          start=(g == 0), stop=(g == 7))
                last = te.matmul(pb[5][:, 0:1], A6[:, g * 128:(g + 1) * 128], SM[:, 264 + g:265 + g],
                                 start=(g == 0), stop=(g == 7))
            k.inc(last, "te")
        m_lg = k.c["te"]

        @k.blk.vector
        def _(v, t=t, m_lg=m_lg):
            k.w(v, "te", m_lg)
            def ss(inst):
                m = k.inc(inst, "ve")
                v.wait_ge(k.s["ve"], m)
            lg = SM[:, 430:438]
            ss(v.tensor_copy(lg, pb[4][:, 0:8]))
            m1 = SM[:, 438:439]
            ss(v.reduce_max(m1, lg, axis=AX.X))
            ss(v.tensor_scalar_mul(SM[:, 439:440], m1, -1.0))
            k.inc(v.engine_nop(), "ve")
        m_pre = k.c["ve"]

        @k.blk.scalar
        def _(s, t=t, m_pre=m_pre):
            k.w(s, "ve", m_pre)
            s.activation(SM[:, 440:448], SM[:, 430:438], AF.Exp, bias=SM[:, 439:440],
                         accum_out=SM[:, 448:449])
            k.inc(s.activation(SM[:, 429:430], pb[5][:, 0:1], AF.Sigmoid), "se")
        m_exp = k.c["se"]

        @k.blk.vector
        def _(v, t=t, m_exp=m_exp):
            k.w(v, "se", m_exp)
            def ss(inst):
                m = k.inc(inst, "ve")
                v.wait_ge(k.s["ve"], m)
            sc = SM[:, 450:458]
            ss(v.reciprocal(SM[:, 449:450], SM[:, 448:449]))
            ss(v.tensor_scalar(sc, SM[:, 440:448], SM[:, 449:450], None, ALU.mult))
            m1b = SM[:, 458:459]
            ss(v.reduce_max(m1b, sc, axis=AX.X))
            is1 = SM[:, 459:467]
            ss(v.tensor_single_scalar(is1, sc, m1b, ALU.is_equal))
            ones8 = SM[:, 467:475]
            ss(v.memset(ones8, 1.0))
            cum = SM[:, 475:483]
            ss(v.tensor_tensor_scan(cum, ones8, is1, 0.0, ALU.mult, ALU.add))
            ss(v.tensor_single_scalar(cum, cum, 1.0, ALU.is_equal))
            f1 = SM[:, 483:491]
            ss(v.tensor_tensor(f1, is1, cum, ALU.mult))
            s2 = SM[:, 491:499]
            ss(v.scalar_tensor_tensor(s2, f1, -1e30, sc, ALU.mult, ALU.add))
            m2 = SM[:, 499:500]
            ss(v.reduce_max(m2, s2, axis=AX.X))
            ind = SM[:, 500:508]
            ss(v.tensor_single_scalar(ind, sc, m2, ALU.is_ge))
            ss(v.tensor_tensor(k.t["l3w"][:, t * 9:t * 9 + 8], sc, ind, ALU.mult))
            k.inc(v.tensor_copy(k.t["l3w"][:, t * 9 + 8:t * 9 + 9], SM[:, 429:430]), "ve")
    m_route = k.c["ve"]

    for e in range(10):
        @k.blk.gpsimd
        def _(g, e=e):
            k.w(g, "te", k.c["te"])
            k.w(g, "dS", k.c["dS"])
            if e < 8:
                gsrc = k.t["w_l3f_guT"][e * 1024:(e + 1) * 1024, 0:1408]
                usrc = k.t["w_l3f_guT"][e * 1024:(e + 1) * 1024, 1408:2816]
            else:
                gsrc = k.t["w_l3f_sgT"][0:1024, (e - 8) * 1408:(e - 7) * 1408]
                usrc = k.t["w_l3f_suT"][0:1024, (e - 8) * 1408:(e - 7) * 1408]
            k.inc(g.dma_start(A2[:, 0:11264].rearrange("p (g2 d) -> p g2 d", d=1408),
                              gsrc.rearrange("(g2 p) d -> p g2 d", p=128)), "dG", 16)
            k.inc(g.dma_start(A3[:, 0:11264].rearrange("p (g2 d) -> p g2 d", d=1408),
                              usrc.rearrange("(g2 p) d -> p g2 d", p=128)), "dG", 16)
        m_w = k.c["dG"]

        for it in range(11):
            for tc in range(4):
                for arena, bk in ((A2, pb[0]), (A3, pb[1])):
                    @k.blk.tensor
                    def _(te, e=e, it=it, tc=tc, arena=arena, bk=bk, m_w=m_w):
                        k.w(te, "dG", m_w)
                        if e == 0 and it == 0 and tc == 0 and arena is A2:
                            k.w(te, "ve", m_route)
                        k.w(te, "se", k.c["se"])
                        k.w(te, "ve", k.c["ve"])
                        last = None
                        for gi in range(8):
                            last = te.matmul(bk[:],
                                             arena[:, gi * 1408 + it * 128: gi * 1408 + it * 128 + 128],
                                             A1[:, gi * 2048 + tc * 512: gi * 2048 + (tc + 1) * 512],
                                             start=(gi == 0), stop=(gi == 7))
                        k.inc(last, "te")
                m_gu = k.c["te"]

                @k.blk.scalar
                def _(s, e=e, it=it, tc=tc, m_gu=m_gu):
                    k.w(s, "te", m_gu)
                    k.w(s, "ve", k.c["ve"])
                    k.inc(s.activation(A5[:, 0:512], pb[0][:], AF.Silu), "se")
                m_sl = k.c["se"]

                @k.blk.vector
                def _(v, e=e, it=it, tc=tc, m_sl=m_sl):
                    k.w(v, "se", m_sl)
                    k.w(v, "dS", k.c["dS"])
                    k.inc(v.tensor_tensor(A4[:, 0:512], A5[:, 0:512], pb[1][:], ALU.mult), "ve")
                m_h = k.c["ve"]

                @k.blk.sync
                def _(sy, e=e, it=it, tc=tc, m_h=m_h):
                    k.w(sy, "ve", m_h)
                    k.inc(sy.dma_start(k.t["loc_hqT"][it * 128:(it + 1) * 128, tc * 512:(tc + 1) * 512],
                                       A4[:, 0:512]), "dS", 16)
        m_hspill = k.c["dS"]

        @k.blk.gpsimd
        def _(g, e=e, m_hspill=m_hspill):
            k.w(g, "te", k.c["te"])
            if e < 8:
                src = k.t["w_l3f_dT"][e * 1408:(e + 1) * 1408, :]
            else:
                src = k.t["w_l3f_sdT"][(e - 8) * 1408:(e - 7) * 1408, :]
            k.inc(g.dma_start(A2[:, 0:11264].rearrange("p (g2 d) -> p g2 d", d=1024),
                              src.rearrange("(g2 p) d -> p g2 d", p=128)), "dG", 16)
        m_wd = k.c["dG"]

        for t in range(NT):
            @k.blk.sync
            def _(sy, e=e, t=t, m_hspill=m_hspill):
                if t == 0:
                    k.w(sy, "dS", m_hspill)
                k.w(sy, "te", k.c["te"])
                for gi in range(11):
                    k.inc(sy.dma_start(A4[:, 1024 + gi * 128: 1024 + (gi + 1) * 128],
                                       k.t["loc_hqT"][gi * 128:(gi + 1) * 128, t * 128:(t + 1) * 128]),
                          "dS", 16)
            m_hT = k.c["dS"]

            for ch in range(2):
                bank = pb[2 + (t * 2 + ch) % 2]

                @k.blk.tensor
                def _(te, e=e, t=t, ch=ch, bank=bank, m_hT=m_hT, m_wd=m_wd):
                    k.w(te, "dS", m_hT)
                    if t == 0 and ch == 0:
                        k.w(te, "dG", m_wd)
                    k.w(te, "ve", k.c["ve"])
                    last = None
                    for gi in range(11):
                        src = A4[:, 1024 + gi * 128: 1024 + (gi + 1) * 128]
                        last = te.matmul(bank[:], src,
                                         A2[:, gi * 1024 + ch * 512: gi * 1024 + ch * 512 + 512],
                                         start=(gi == 0), stop=(gi == 10))
                    k.inc(last, "te")
                m_mm = k.c["te"]

                @k.blk.vector
                def _(v, e=e, t=t, ch=ch, bank=bank, m_mm=m_mm):
                    k.w(v, "te", m_mm)
                    xt = k.t["xres"][:, t * 1024 + ch * 512: t * 1024 + ch * 512 + 512]
                    wcol = k.t["l3w"][:, t * 9 + min(e, 8): t * 9 + min(e, 8) + 1]
                    k.inc(v.scalar_tensor_tensor(xt, bank[:], wcol, xt, ALU.mult, ALU.add), "ve")


def emit_l3_attn_b(k):
    nc = k.nc
    A1, A2, A3, A4, A5, A6 = (k.t[n] for n in ("A1", "A2", "A3", "A4", "A5", "A6"))
    pb = [k.t[f"pb{i}"] for i in range(6)]
    SM = k.t["SM"]
    A6b = A6[:].bitcast(BF16)
    ptb = [k.t["pt0"], k.t["pt1"]]

    def ln_from_hbm(src_name, gname, bname, sumofs, sqofs, meanofs, varofs, rstofs, nmrofs):
        for t in range(NT):
            @k.blk.sync
            def _(sy, t=t):
                k.w(sy, "se", k.c["se"])
                k.w(sy, "ve", k.c["ve"])
                k.inc(sy.dma_start(A5[:, 0:1024], k.t[src_name][t * 128:(t + 1) * 128, :]), "dS", 16)
            m_ld = k.c["dS"]

            @k.blk.scalar
            def _(s, t=t, m_ld=m_ld):
                k.w(s, "dS", m_ld)
                k.inc(s.activation(A5[:, 1024:2048], A5[:, 0:1024], AF.Square,
                                   accum_out=SM[:, sqofs + t:sqofs + t + 1]), "se")
            m_sq = k.c["se"]

            @k.blk.vector
            def _(v, t=t, m_sq=m_sq):
                k.w(v, "se", m_sq)
                k.inc(v.reduce_sum(SM[:, sumofs + t:sumofs + t + 1], A5[:, 0:1024], axis=AX.X), "ve")
        m_stats = k.c["ve"]

        @k.blk.vector
        def _(v):
            k.w(v, "ve", m_stats)
            k.w(v, "se", k.c["se"])
            def ss(inst):
                m = k.inc(inst, "ve")
                v.wait_ge(k.s["ve"], m)
            ss(v.tensor_scalar_mul(SM[:, meanofs:meanofs + 16], SM[:, sumofs:sumofs + 16], 1.0 / 1024.0))
            ss(v.tensor_scalar_mul(SM[:, varofs:varofs + 16], SM[:, sqofs:sqofs + 16], 1.0 / 1024.0))
            ss(v.tensor_tensor(SM[:, rstofs:rstofs + 16], SM[:, meanofs:meanofs + 16],
                               SM[:, meanofs:meanofs + 16], ALU.mult))
            ss(v.tensor_tensor(SM[:, varofs:varofs + 16], SM[:, varofs:varofs + 16],
                               SM[:, rstofs:rstofs + 16], ALU.subtract))
            k.inc(v.tensor_scalar_add(SM[:, varofs:varofs + 16], SM[:, varofs:varofs + 16], 1e-5), "ve")
        m_var = k.c["ve"]

        @k.blk.scalar
        def _(s):
            k.w(s, "ve", m_var)
            m = k.inc(s.activation(SM[:, nmrofs:nmrofs + 16], SM[:, varofs:varofs + 16], AF.Ln), "se")
            s.wait_ge(k.s["se"], m)
            k.inc(s.activation(SM[:, rstofs:rstofs + 16], SM[:, nmrofs:nmrofs + 16], AF.Exp, scale=-0.5), "se")
        m_rst = k.c["se"]

        @k.blk.vector
        def _(v):
            k.w(v, "se", m_rst)
            k.inc(v.scalar_tensor_tensor(SM[:, nmrofs:nmrofs + 16], SM[:, meanofs:meanofs + 16], -1.0,
                                         SM[:, rstofs:rstofs + 16], ALU.mult, ALU.mult), "ve")
        m_nmr = k.c["ve"]

        @k.blk.sync
        def _(sy):
            k.w(sy, "ve", k.c["ve"])
            k.w(sy, "se", k.c["se"])
            k.inc(sy.dma_start(A6[:, 0:1024], k.t[gname][:, :]), "dS", 16)
            k.inc(sy.dma_start(A5[:, 2048:3072], k.t[bname][:, :]), "dS", 16)
        m_gb = k.c["dS"]

        m_tr = {}
        m_ev = {}
        for t in range(NT):
            @k.blk.sync
            def _(sy, t=t):
                k.w(sy, "se", k.c["se"])
                k.inc(sy.dma_start(A5[:, 0:1024], k.t[src_name][t * 128:(t + 1) * 128, :]), "dS", 16)
            m_ld = k.c["dS"]

            @k.blk.scalar
            def _(s, t=t, m_ld=m_ld):
                k.w(s, "dS", m_ld)
                if t == 0:
                    k.w(s, "ve", m_nmr)
                    k.w(s, "dS", m_gb)
                k.inc(s.activation(A5[:, 1024:2048], A5[:, 0:1024], AF.Identity,
                                   bias=SM[:, nmrofs + t:nmrofs + t + 1],
                                   scale=SM[:, rstofs + t:rstofs + t + 1]), "se")
            m_xn = k.c["se"]

            @k.blk.vector
            def _(v, t=t, m_xn=m_xn):
                k.w(v, "se", m_xn)
                if t >= 2:
                    k.w(v, "te", m_tr.get(t - 2, 0))
                v.tensor_tensor(A5[:, 1024:2048], A5[:, 1024:2048], A6[:, 0:1024], ALU.mult)
                k.inc(v.tensor_tensor(A4[:, (t % 2) * 1024:(t % 2) * 1024 + 1024],
                                      A5[:, 1024:2048], A5[:, 2048:3072], ALU.add), "ve")
            m_aff = k.c["ve"]

            @k.blk.tensor
            def _(te, t=t, m_aff=m_aff):
                k.w(te, "ve", m_aff)
                if t >= 1:
                    k.w(te, "ve", m_ev.get(t - 1, 0))
                k.w(te, "se", k.c["se"])
                last = None
                for g in range(8):
                    last = te.transpose(ptb[g // 4][:, (g % 4) * 128:(g % 4) * 128 + 128],
                                        A4[:, (t % 2) * 1024 + g * 128: (t % 2) * 1024 + (g + 1) * 128],
                                        k.t["identb"][:])
                m_tr[t] = k.inc(last, "te")

            @k.blk.vector
            def _(v, t=t):
                k.w(v, "te", m_tr[t])
                last = None
                for g in range(8):
                    src = ptb[g // 4]
                    last = v.tensor_copy(A1[:, g * 2048 + t * 128: g * 2048 + t * 128 + 128],
                                        src[:, (g % 4) * 128:(g % 4) * 128 + 128])
                m_ev[t] = k.inc(last, "ve")
        return k.c["ve"]

    m_nmr0 = emit_ln_stats(k)
    for t in range(NT):
        @k.blk.scalar
        def _(s, t=t):
            if t == 0:
                k.w(s, "ve", m_nmr0)
            k.w(s, "dS", k.c["dS"])
            k.w(s, "te", k.c["te"])
            k.inc(s.activation(A5[:, 0:1024], k.t["xres"][:, t * 1024:(t + 1) * 1024], AF.Identity,
                               bias=SM[:, 80 + t:81 + t], scale=SM[:, 64 + t:65 + t]), "se")
        m_u = k.c["se"]

        @k.blk.sync
        def _(sy, t=t, m_u=m_u):
            k.w(sy, "se", m_u)
            k.inc(sy.dma_start(k.t["loc_u"][t * 128:(t + 1) * 128, :], A5[:, 0:1024]), "dS", 16)

    m_xnT = ln_from_hbm("loc_u", "b_m_n1g", "b_m_n1b", 294, 310, 326, 342, 358, 390)

    @k.blk.sync
    def _(sy):
        k.w(sy, "se", k.c["se"])
        k.w(sy, "ve", k.c["ve"])
        k.inc(sy.dma_start(SM[:, 200:264].rearrange("p (g c) -> p g c", c=4),
                           k.t["w_m_convw"].ap().rearrange("(g p) c -> p g c", p=128)), "dS", 16)
        k.inc(sy.dma_start(SM[:, 264:280], k.t["w_m_convb"][:, :]), "dS", 16)
        k.inc(sy.dma_start(SM[:, 280:296], k.t["w_m_dtb"][:, :]), "dS", 16)
        k.inc(sy.dma_start(SM[:, 406:422], k.t["w_m_D"][:, :]), "dS", 16)
    m_small = k.c["dS"]

    @k.blk.gpsimd
    def _(g):
        k.w(g, "te", k.c["te"])
        k.inc(g.dma_start(A2[:, 0:16384].rearrange("p (g2 d) -> p g2 d", d=2048),
                          k.t["w_m_inT"][0:1024, 0:2048].rearrange("(g2 p) d -> p g2 d", p=128)), "dG", 16)
        k.inc(g.dma_start(A3[:, 0:16384].rearrange("p (g2 d) -> p g2 d", d=2048),
                          k.t["w_m_inT"][0:1024, 2048:4096].rearrange("(g2 p) d -> p g2 d", p=128)), "dG", 16)
        k.inc(g.dma_start(A4[:, 4096:4864].rearrange("p (g2 d) -> p g2 d", d=96),
                          k.t["w_m_xprojT"][0:1024, :].rearrange("(g2 p) d -> p g2 d", p=128)), "dG", 16)
        k.inc(g.dma_start(A4[:, 4864:5632].rearrange("p (g2 d) -> p g2 d", d=96),
                          k.t["w_m_xprojT"][1024:2048, :].rearrange("(g2 p) d -> p g2 d", p=128)), "dG", 16)
    m_w1 = k.c["dG"]

    for gout in range(16):
        for tc in range(4):
            bank = pb[tc % 2]

            @k.blk.tensor
            def _(te, gout=gout, tc=tc, bank=bank, m_w1=m_w1):
                if gout == 0 and tc == 0:
                    k.w(te, "dG", m_w1)
                    k.w(te, "ve", m_xnT)
                    k.w(te, "dS", m_small)
                k.w(te, "se", k.c["se"])
                last = None
                for gi in range(8):
                    last = te.matmul(bank[:],
                                     A2[:, gi * 2048 + gout * 128: gi * 2048 + gout * 128 + 128],
                                     A1[:, gi * 2048 + tc * 512: gi * 2048 + (tc + 1) * 512],
                                     start=(gi == 0), stop=(gi == 7))
                k.inc(last, "te")
            m_mm = k.c["te"]

            @k.blk.scalar
            def _(s, gout=gout, tc=tc, bank=bank, m_mm=m_mm):
                k.w(s, "te", m_mm)
                k.w(s, "ve", k.c["ve"])
                k.inc(s.activation(A5[:, tc * 512:(tc + 1) * 512], bank[:], AF.Copy), "se")
        m_row = k.c["se"]

        for lc in range(2):
            @k.blk.vector
            def _(v, gout=gout, lc=lc, m_row=m_row):
                k.w(v, "se", m_row)
                acc = A5[:, 2048 + 0:3072]
                src0 = 1024 * lc
                wc = lambda tau: SM[:, 200 + gout * 4 + tau:200 + gout * 4 + tau + 1]
                v.tensor_scalar(acc, A5[:, src0:src0 + 1024], wc(3), None, ALU.mult)
                if lc == 0:
                    v.scalar_tensor_tensor(acc[:, 1:1024], A5[:, 0:1023], wc(2), acc[:, 1:1024], ALU.mult, ALU.add)
                    v.scalar_tensor_tensor(acc[:, 2:1024], A5[:, 0:1022], wc(1), acc[:, 2:1024], ALU.mult, ALU.add)
                    k.inc(v.scalar_tensor_tensor(acc[:, 3:1024], A5[:, 0:1021], wc(0), acc[:, 3:1024], ALU.mult, ALU.add), "ve")
                else:
                    v.scalar_tensor_tensor(acc, A5[:, src0 - 1:src0 + 1023], wc(2), acc, ALU.mult, ALU.add)
                    v.scalar_tensor_tensor(acc, A5[:, src0 - 2:src0 + 1022], wc(1), acc, ALU.mult, ALU.add)
                    k.inc(v.scalar_tensor_tensor(acc, A5[:, src0 - 3:src0 + 1021], wc(0), acc, ALU.mult, ALU.add), "ve")
            m_cv = k.c["ve"]

            @k.blk.scalar
            def _(s, gout=gout, lc=lc, m_cv=m_cv):
                k.w(s, "ve", m_cv)
                k.w(s, "te", k.c["te"])
                k.inc(s.activation(A4[:, lc * 1024:(lc + 1) * 1024], A5[:, 2048:3072], AF.Silu,
                                   bias=SM[:, 264 + gout:265 + gout]), "se")
        m_xc = k.c["se"]

        @k.blk.sync
        def _(sy, gout=gout, m_xc=m_xc):
            k.w(sy, "se", m_xc)
            k.inc(sy.dma_start(k.t["loc_xcT"][gout * 128:(gout + 1) * 128, :], A4[:, 0:2048]), "dS", 16)

        @k.blk.tensor
        def _(te, gout=gout, m_xc=m_xc):
            k.w(te, "se", m_xc)
            xp = A4[:, 4096 + (gout // 8) * 768 + (gout % 8) * 96: 4096 + (gout // 8) * 768 + (gout % 8) * 96 + 96]
            last = None
            for tc in range(4):
                last = te.matmul(pb[2 + tc][0:96, :], xp, A4[:, tc * 512:(tc + 1) * 512],
                                 start=(gout == 0), stop=(gout == 15))
            k.inc(last, "te")
    m_xdbl = k.c["te"]

    for gout in range(16):
        for tc in range(4):
            bank = pb[tc % 2]

            @k.blk.tensor
            def _(te, gout=gout, tc=tc, bank=bank):
                k.w(te, "se", k.c["se"])
                last = None
                for gi in range(8):
                    last = te.matmul(bank[:],
                                     A3[:, gi * 2048 + gout * 128: gi * 2048 + gout * 128 + 128],
                                     A1[:, gi * 2048 + tc * 512: gi * 2048 + (tc + 1) * 512],
                                     start=(gi == 0), stop=(gi == 7))
                k.inc(last, "te")
            m_mm = k.c["te"]

            @k.blk.scalar
            def _(s, gout=gout, tc=tc, bank=bank, m_mm=m_mm):
                k.w(s, "te", m_mm)
                k.w(s, "dS", k.c["dS"])
                k.inc(s.activation(A4[:, 6144 + (tc % 2) * 512: 6144 + (tc % 2) * 512 + 512], bank[:], AF.Copy), "se")
            m_ev2 = k.c["se"]

            @k.blk.sync
            def _(sy, gout=gout, tc=tc, m_ev2=m_ev2):
                k.w(sy, "se", m_ev2)
                k.inc(sy.dma_start(k.t["loc_zT"][gout * 128:(gout + 1) * 128, tc * 512:(tc + 1) * 512],
                                   A4[:, 6144 + (tc % 2) * 512: 6144 + (tc % 2) * 512 + 512]), "dS", 16)

    @k.blk.vector
    def _(v):
        k.w(v, "te", m_xdbl)
        k.w(v, "se", k.c["se"])
        last = None
        for tc in range(4):
            last = v.tensor_copy(A5[0:96, tc * 512:(tc + 1) * 512], pb[2 + tc][0:96, :])
        k.inc(last, "ve")
    m_xd = k.c["ve"]

    @k.blk.sync
    def _(sy):
        k.w(sy, "ve", m_xd)
        k.inc(sy.dma_start(k.t["loc_bc"][0:32, :], A5[64:96, 0:2048]), "dS", 16)
    m_bc = k.c["dS"]

    @k.blk.scalar
    def _(s):
        k.w(s, "ve", m_xd)
        k.w(s, "dS", k.c["dS"])
        k.inc(s.activation(A4[0:64, 6144:8192], A5[0:64, 0:2048], AF.Copy), "se")
    m_dtT = k.c["se"]

    @k.blk.gpsimd
    def _(g):
        k.w(g, "te", k.c["te"])
        k.inc(g.dma_start(A4[0:64, 0:2048], k.t["w_m_dtT"][:, :]), "dG", 16)
    m_dtw = k.c["dG"]

    for dt_tile in range(16):
        for tc in range(4):
            bank = pb[(dt_tile * 4 + tc) % 2]

            @k.blk.tensor
            def _(te, dt_tile=dt_tile, tc=tc, bank=bank, m_dtT=m_dtT, m_dtw=m_dtw):
                if dt_tile == 0 and tc == 0:
                    k.w(te, "se", m_dtT)
                    k.w(te, "dG", m_dtw)
                k.w(te, "ve", k.c["ve"])
                k.inc(te.matmul(bank[:], A4[0:64, dt_tile * 128:(dt_tile + 1) * 128],
                                A4[0:64, 6144 + tc * 512: 6144 + (tc + 1) * 512],
                                start=True, stop=True), "te")
            m_mm = k.c["te"]

            @k.blk.vector
            def _(v, dt_tile=dt_tile, tc=tc, bank=bank, m_mm=m_mm):
                k.w(v, "te", m_mm)
                k.w(v, "se", k.c["se"])
                t1 = A5[:, 2048:2560]
                t2 = A5[:, 2560:3072]
                v.tensor_scalar(t1, bank[:], SM[:, 280 + dt_tile:281 + dt_tile], None, ALU.add)
                v.tensor_scalar_max(t2, t1, 0.0)
                k.inc(v.scalar_tensor_tensor(t1, t2, -2.0, t1, ALU.mult, ALU.add), "ve")
            m_sp1 = k.c["ve"]

            @k.blk.scalar
            def _(s, dt_tile=dt_tile, tc=tc, m_sp1=m_sp1):
                k.w(s, "ve", m_sp1)
                s.activation(A5[:, 2048:2560], A5[:, 2048:2560], AF.Exp)
                k.inc(s.activation(A5[:, 2048:2560], A5[:, 2048:2560], AF.Ln, bias=1.0), "se")
            m_sp2 = k.c["se"]

            @k.blk.vector
            def _(v, dt_tile=dt_tile, tc=tc, m_sp2=m_sp2):
                k.w(v, "se", m_sp2)
                k.inc(v.tensor_tensor(A4[:, 2048 + (tc % 2) * 512: 2048 + (tc % 2) * 512 + 512],
                                      A5[:, 2048:2560], A5[:, 2560:3072], ALU.add), "ve")
            m_sp3 = k.c["ve"]

            @k.blk.sync
            def _(sy, dt_tile=dt_tile, tc=tc, m_sp3=m_sp3):
                k.w(sy, "ve", m_sp3)
                k.inc(sy.dma_start(k.t["loc_deltaT"][dt_tile * 128:(dt_tile + 1) * 128, tc * 512:(tc + 1) * 512],
                                   A4[:, 2048 + (tc % 2) * 512: 2048 + (tc % 2) * 512 + 512]), "dS", 16)
    m_delta = k.c["dS"]

    for g in range(16):
        @k.blk.sync
        def _(sy, g=g):
            k.w(sy, "dS", m_delta)
            k.w(sy, "ve", k.c["ve"])
            k.w(sy, "te", k.c["te"])
            k.inc(sy.dma_start(A4[:, 0:2048], k.t["loc_deltaT"][g * 128:(g + 1) * 128, :]), "dS", 16)
            k.inc(sy.dma_start(A4[:, 4096:6144], k.t["loc_xcT"][g * 128:(g + 1) * 128, :]), "dS", 16)
            k.inc(sy.dma_start(SM[:, 374:390], k.t["w_m_Alog"][g * 128:(g + 1) * 128, :]), "dS", 16)
        m_ld = k.c["dS"]

        @k.blk.scalar
        def _(s, g=g, m_ld=m_ld):
            k.w(s, "dS", m_ld)
            k.inc(s.activation(SM[:, 390:406], SM[:, 374:390], AF.Exp), "se")
        m_A1 = k.c["se"]

        @k.blk.vector
        def _(v, g=g, m_A1=m_A1):
            k.w(v, "se", m_A1)
            v.tensor_scalar_mul(SM[:, 390:406], SM[:, 390:406], -1.0)
            v.tensor_tensor(A4[:, 2048:4096], A4[:, 0:2048], A4[:, 4096:6144], ALU.mult)
            k.inc(v.memset(A5[:, 0:2048], 0.0), "ve")
        m_ug = k.c["ve"]

        for n in range(16):
            for lc in range(2):
                @k.blk.gpsimd
                def _(gp, g=g, n=n, lc=lc):
                    k.w(gp, "ve", k.c["ve"])
                    k.inc(gp.dma_start(A6b[:, 0:1024],
                                       k.t["loc_bc"][n:n + 1, lc * 1024:(lc + 1) * 1024]
                                       .broadcast_to([128, 1024])), "dG", 16)
                    k.inc(gp.dma_start(A6b[:, 1024:2048],
                                       k.t["loc_bc"][16 + n:17 + n, lc * 1024:(lc + 1) * 1024]
                                       .broadcast_to([128, 1024])), "dG", 16)
                m_bcl = k.c["dG"]

                @k.blk.scalar
                def _(s, g=g, n=n, lc=lc, m_ug=m_ug):
                    k.w(s, "ve", k.c["ve"])
                    k.inc(s.activation(A4[:, 4096 + lc * 0:4096 + 1024] if False else A4[:, 4096:5120],
                                       A4[:, lc * 1024:(lc + 1) * 1024], AF.Exp,
                                       scale=SM[:, 390 + n:391 + n]), "se")
                m_dA = k.c["se"]

                @k.blk.vector
                def _(v, g=g, n=n, lc=lc, m_dA=m_dA, m_bcl=m_bcl):
                    k.w(v, "se", m_dA)
                    k.w(v, "dG", m_bcl)
                    def ss(inst):
                        m = k.inc(inst, "ve")
                        v.wait_ge(k.s["ve"], m)
                    v.tensor_tensor(A4[:, 5120:6144], A4[:, 2048 + lc * 1024:2048 + (lc + 1) * 1024],
                                    A6b[:, 0:1024], ALU.mult)
                    init = 0.0 if lc == 0 else SM[:, 422:423]
                    ss(v.tensor_tensor_scan(A4[:, 6144:7168], A4[:, 4096:5120], A4[:, 5120:6144],
                                            init, ALU.mult, ALU.add))
                    ss(v.tensor_copy(SM[:, 422:423], A4[:, 7167:7168]))
                    k.inc(v.scalar_tensor_tensor(A5[:, lc * 1024:(lc + 1) * 1024], A4[:, 6144:7168], 1.0,
                                                 A5[:, lc * 1024:(lc + 1) * 1024], ALU.mult, ALU.add)
                          if False else
                          v.tensor_tensor(A4[:, 7168:8192], A4[:, 6144:7168], A6b[:, 1024:2048], ALU.mult), "ve")
                m_hC = k.c["ve"]

                @k.blk.vector
                def _(v, g=g, n=n, lc=lc, m_hC=m_hC):
                    k.w(v, "ve", m_hC)
                    k.inc(v.tensor_tensor(A5[:, lc * 1024:(lc + 1) * 1024],
                                          A5[:, lc * 1024:(lc + 1) * 1024], A4[:, 7168:8192], ALU.add), "ve")
        m_y = k.c["ve"]

        @k.blk.sync
        def _(sy, g=g, m_y=m_y):
            k.w(sy, "ve", m_y)
            k.inc(sy.dma_start(A4[:, 0:2048], k.t["loc_zT"][g * 128:(g + 1) * 128, :]), "dS", 16)
        m_z = k.c["dS"]

        @k.blk.sync
        def _(sy, g=g, m_z=m_z):
            k.w(sy, "dS", m_z)
            k.w(sy, "ve", k.c["ve"])
            k.inc(sy.dma_start(A4[:, 2048:4096], k.t["loc_xcT"][g * 128:(g + 1) * 128, :]), "dS", 16)
        m_xc2 = k.c["dS"]

        @k.blk.vector
        def _(v, g=g, m_xc2=m_xc2):
            k.w(v, "dS", m_xc2)
            k.inc(v.scalar_tensor_tensor(A5[:, 0:2048], A4[:, 2048:4096], SM[:, 423 + g:424 + g],
                                         A5[:, 0:2048], ALU.mult, ALU.add), "ve")
        m_yD = k.c["ve"]

        @k.blk.scalar
        def _(s, g=g, m_yD=m_yD):
            k.w(s, "ve", m_yD)
            k.inc(s.activation(A4[:, 2048:4096], A4[:, 0:2048], AF.Silu), "se")
        m_sz = k.c["se"]

        @k.blk.vector
        def _(v, g=g, m_sz=m_sz):
            k.w(v, "se", m_sz)
            k.inc(v.tensor_tensor(A4[:, 6144:8192], A5[:, 0:2048], A4[:, 2048:4096], ALU.mult), "ve")
        m_yT = k.c["ve"]

        @k.blk.sync
        def _(sy, g=g, m_yT=m_yT):
            k.w(sy, "ve", m_yT)
            k.inc(sy.dma_start(k.t["loc_hqT"][g * 128:(g + 1) * 128, :], A4[:, 6144:8192]), "dS", 16)
    m_yspill = k.c["dS"]


    @k.blk.gpsimd
    def _(g):
        k.w(g, "te", k.c["te"])
        k.inc(g.dma_start(A2[:, 0:16384].rearrange("p (g2 d) -> p g2 d", d=1024),
                          k.t["w_m_outT"].ap().rearrange("(g2 p) d -> p g2 d", p=128)), "dG", 16)
    m_ow = k.c["dG"]

    for t in range(NT):
        @k.blk.sync
        def _(sy, t=t, m_yspill=m_yspill):
            if t == 0:
                k.w(sy, "dS", m_yspill)
            k.w(sy, "te", k.c["te"])
            k.w(sy, "ve", k.c["ve"])
            for gi in range(16):
                k.inc(sy.dma_start(A4[:, gi * 128:(gi + 1) * 128],
                                   k.t["loc_hqT"][gi * 128:(gi + 1) * 128, t * 128:(t + 1) * 128]), "dS", 16)
            k.inc(sy.dma_start(A5[:, 0:1024], k.t["loc_u"][t * 128:(t + 1) * 128, :]), "dS", 16)
        m_ld = k.c["dS"]

        for ch in range(2):
            bank = pb[ch]

            @k.blk.tensor
            def _(te, t=t, ch=ch, bank=bank, m_ld=m_ld, m_ow=m_ow):
                k.w(te, "dS", m_ld)
                if t == 0 and ch == 0:
                    k.w(te, "dG", m_ow)
                k.w(te, "ve", k.c["ve"])
                last = None
                for gi in range(16):
                    last = te.matmul(bank[:], A4[:, gi * 128:(gi + 1) * 128],
                                     A2[:, gi * 1024 + ch * 512: gi * 1024 + ch * 512 + 512],
                                     start=(gi == 0), stop=(gi == 15))
                k.inc(last, "te")
        m_mm = k.c["te"]

        @k.blk.vector
        def _(v, t=t, m_mm=m_mm):
            k.w(v, "te", m_mm)
            v.tensor_tensor(A5[:, 1024:1536], A5[:, 0:512], pb[0][:], ALU.add)
            k.inc(v.tensor_tensor(A5[:, 1536:2048], A5[:, 512:1024], pb[1][:], ALU.add), "ve")
        m_xp = k.c["ve"]

        @k.blk.sync
        def _(sy, t=t, m_xp=m_xp):
            k.w(sy, "ve", m_xp)
            k.inc(sy.dma_start(k.t["loc_u"][t * 128:(t + 1) * 128, :], A5[:, 1024:2048]), "dS", 16)

    m_xnT2 = ln_from_hbm("loc_u", "b_m_n2g", "b_m_n2b", 294, 310, 326, 342, 358, 390)

    @k.blk.gpsimd
    def _(g):
        k.w(g, "te", k.c["te"])
        k.inc(g.dma_start(A2[:, 0:16384].rearrange("p (g2 d) -> p g2 d", d=2048),
                          k.t["w_m_ffgT"].ap().rearrange("(g2 p) d -> p g2 d", p=128)), "dG", 16)
        k.inc(g.dma_start(A3[:, 0:8192].rearrange("p (g2 d) -> p g2 d", d=1024),
                          k.t["w_m_ffpT"].ap().rearrange("(g2 p) d -> p g2 d", p=128)), "dG", 16)
    m_ffw = k.c["dG"]

    @k.blk.sync
    def _(sy):
        k.w(sy, "ve", k.c["ve"])
        k.w(sy, "se", k.c["se"])
        k.inc(sy.dma_start(A6[:, 0:1024], k.t["b_m_ffgb"][:, 0:1024]), "dS", 16)
        k.inc(sy.dma_start(A5[:, 2048:3072], k.t["b_m_ffgb"][:, 1024:2048]), "dS", 16)
    m_fb = k.c["dS"]

    @k.blk.gpsimd
    def _(g):
        k.w(g, "ve", k.c["ve"])
        k.inc(g.dma_start(A4[:, 3072:4096], k.t["b_m_ffpb"][:, :]), "dG", 16)
    m_fpb = k.c["dG"]

    for t in range(NT):
        for part in range(2):
            for ch in range(2):
                bank = pb[part * 2 + ch]

                @k.blk.tensor
                def _(te, t=t, part=part, ch=ch, bank=bank, m_ffw=m_ffw):
                    if t == 0 and part == 0 and ch == 0:
                        k.w(te, "dG", m_ffw)
                    k.w(te, "se", k.c["se"])
                    k.w(te, "ve", k.c["ve"])
                    last = None
                    for gi in range(8):
                        last = te.matmul(bank[:],
                                         A1[:, gi * 2048 + t * 128: gi * 2048 + t * 128 + 128],
                                         A2[:, gi * 2048 + part * 1024 + ch * 512:
                                              gi * 2048 + part * 1024 + ch * 512 + 512],
                                         start=(gi == 0), stop=(gi == 7))
                    k.inc(last, "te")
        m_gv = k.c["te"]

        @k.blk.vector
        def _(v, t=t, m_gv=m_gv, m_fb=m_fb):
            k.w(v, "te", m_gv)
            k.w(v, "dS", m_fb)
            k.w(v, "se", k.c["se"])
            v.tensor_tensor(A5[:, 0:512], pb[0][:], A6[:, 0:512], ALU.add)
            v.tensor_tensor(A5[:, 512:1024], pb[1][:], A6[:, 512:1024], ALU.add)
            v.tensor_tensor(A5[:, 1024:1536], pb[2][:], A5[:, 2048:2560], ALU.add)
            k.inc(v.tensor_tensor(A5[:, 1536:2048], pb[3][:], A5[:, 2560:3072], ALU.add), "ve")
        m_gb2 = k.c["ve"]

        @k.blk.scalar
        def _(s, t=t, m_gb2=m_gb2):
            k.w(s, "ve", m_gb2)
            k.inc(s.activation(A5[:, 0:1024], A5[:, 0:1024], AF.Silu), "se")
        m_sg2 = k.c["se"]

        @k.blk.vector
        def _(v, t=t, m_sg2=m_sg2):
            k.w(v, "se", m_sg2)
            k.w(v, "te", k.c["te"])
            k.inc(v.tensor_tensor(A4[:, (t % 2) * 1024:(t % 2) * 1024 + 1024],
                                  A5[:, 0:1024], A5[:, 1024:2048], ALU.mult), "ve")
        m_h2 = k.c["ve"]

        @k.blk.tensor
        def _(te, t=t, m_h2=m_h2):
            k.w(te, "ve", m_h2)
            k.w(te, "se", k.c["se"])
            last = None
            for g2 in range(8):
                last = te.transpose(ptb[g2 // 4][:, (g2 % 4) * 128:(g2 % 4) * 128 + 128],
                                    A4[:, (t % 2) * 1024 + g2 * 128: (t % 2) * 1024 + (g2 + 1) * 128],
                                    k.t["identb"][:])
            k.inc(last, "te")
        m_tr2 = k.c["te"]

        @k.blk.scalar
        def _(s, t=t, m_tr2=m_tr2):
            k.w(s, "te", m_tr2)
            last = None
            for g2 in range(8):
                src = ptb[g2 // 4]
                last = s.activation(A4[:, 2048 + g2 * 128: 2048 + (g2 + 1) * 128],
                                    src[:, (g2 % 4) * 128:(g2 % 4) * 128 + 128], AF.Copy)
            k.inc(last, "se")
        m_hT2 = k.c["se"]

        for ch in range(2):
            bank = pb[4 + ch]

            @k.blk.tensor
            def _(te, t=t, ch=ch, bank=bank, m_hT2=m_hT2):
                k.w(te, "se", m_hT2)
                k.w(te, "ve", k.c["ve"])
                last = None
                for g2 in range(8):
                    last = te.matmul(bank[:], A4[:, 2048 + g2 * 128: 2048 + (g2 + 1) * 128],
                                     A3[:, g2 * 1024 + ch * 512: g2 * 1024 + ch * 512 + 512],
                                     start=(g2 == 0), stop=(g2 == 7))
                k.inc(last, "te")
        m_ff = k.c["te"]

        @k.blk.sync
        def _(sy, t=t, m_ff=m_ff):
            k.w(sy, "ve", k.c["ve"])
            k.inc(sy.dma_start(A5[:, 0:1024], k.t["loc_u"][t * 128:(t + 1) * 128, :]), "dS", 16)
        m_xpl = k.c["dS"]

        @k.blk.vector
        def _(v, t=t, m_ff=m_ff, m_xpl=m_xpl):
            k.w(v, "te", m_ff)
            k.w(v, "dS", m_xpl)
            k.w(v, "dG", m_fpb)
            for ch in range(2):
                xt = k.t["xres"][:, t * 1024 + ch * 512: t * 1024 + ch * 512 + 512]
                v.tensor_tensor(A5[:, 1024 + ch * 512:1024 + (ch + 1) * 512], pb[4 + ch][:],
                                A4[:, 3072 + ch * 512:3072 + (ch + 1) * 512], ALU.add)
                v.tensor_tensor(A5[:, 1024 + ch * 512:1024 + (ch + 1) * 512],
                                A5[:, 1024 + ch * 512:1024 + (ch + 1) * 512],
                                A5[:, 0 + ch * 512:0 + (ch + 1) * 512], ALU.add)
            last = None
            for ch in range(2):
                xt = k.t["xres"][:, t * 1024 + ch * 512: t * 1024 + ch * 512 + 512]
                last = v.tensor_tensor(xt, xt, A5[:, 1024 + ch * 512:1024 + (ch + 1) * 512], ALU.add)
            k.inc(last, "ve")


def build(n_sub=N_SUB_TOTAL):
    import contextlib
    nc = bass.Bass("TRN2", target_bir_lowering=False)

    dram = {}

    def din(name, shape, dt=F32):
        dram[name] = nc.dram_tensor(name, shape, dt, kind="ExternalInput")

    def dint(name, shape, dt=F32):
        h = nc.dram_tensor(name, shape, dt, kind="Internal")
        dram[name] = h

    din("x0", [S, D])
    din("identf", [128, 128])
    din("mtrif", [128, 512])
    din("w_l0aa_q", [D, D])
    din("w_l0aa_k", [D, D])
    din("w_l0aa_v", [D, D])
    din("w_l0ab_q", [D, D])
    din("w_l0ab_k", [D, D])
    din("w_l0ab_v", [D, D])
    din("w_l0ab_o", [D, D])
    din("rope_neox", [S, 64])
    din("ctrif", [128, 128])
    din("w_l1aa_q", [D, D])
    din("w_l1aa_k", [D, D])
    din("w_l1aa_v", [D, D])
    din("w_l1aa_o", [D, D])
    din("rope_il", [S, 64])
    din("cm_causal", [128, 4096])
    din("w_l1ab_q", [D, D])
    din("w_l1ab_k", [D, D])
    din("w_l1ab_v", [D, D])
    din("w_l1ab_o", [D, D])
    din("w_l2aa_q", [D, D])
    din("w_l2aa_k", [D, D])
    din("w_l2aa_v", [D, D])
    din("ident1k", [D, D])
    din("w_l2f_wT", [8 * D, D])
    din("w_l2f_b", [8, D])
    din("pe_T", [32, S])
    din("l2sigT", [32, 8])
    din("w_l2ab_q", [D, D])
    din("w_l2ab_k", [D, D])
    din("w_l2ab_v", [D, D])
    din("w_l2ab_o", [D, D])
    din("b_l2ab_qcol", [128, 8])
    din("b_l2ab_kcol", [128, 8])
    din("b_l2ab_v", [1, D])
    din("bkbc", [128, D])
    din("bobc", [128, D])
    din("w_l3aa_q", [D, D])
    din("w_l3aa_k", [D, D])
    din("w_l3aa_v", [D, D])
    din("w_l3aa_o", [D, D])
    din("cm_win", [128, 4096])
    din("w_l3f_gT", [D, 8])
    din("w_l3f_segT", [128, 8])
    din("w_l3f_guT", [8 * D, 2816])
    din("w_l3f_sgT", [D, 2816])
    din("w_l3f_suT", [D, 2816])
    din("w_l3f_dT", [8 * 1408, D])
    din("w_l3f_sdT", [2816, D])
    din("w_m_inT", [D, 4096])
    din("w_m_convw", [2048, 4])
    din("w_m_convb", [128, 16])
    din("w_m_xprojT", [2048, 96])
    din("w_m_dtT", [64, 2048])
    din("w_m_dtb", [128, 16])
    din("w_m_Alog", [2048, 16])
    din("w_m_D", [128, 16])
    din("w_m_outT", [2048, D])
    din("w_m_ffgT", [D, 2048])
    din("w_m_ffpT", [D, D])
    din("b_m_n1g", [128, D])
    din("b_m_n1b", [128, D])
    din("b_m_n2g", [128, D])
    din("b_m_n2b", [128, D])
    din("b_m_ffgb", [128, 2048])
    din("b_m_ffpb", [128, D])
    din("w_l1f_gate", [2816, D])
    din("w_l1f_up", [2816, D])
    din("w_l1f_down", [D, 2816])
    din("w_l0f_w1", [8 * D, D])
    din("w_l0f_w2", [8 * D, D])
    din("w_l0f_wr", [D, 8])
    din("b_l0f_br", [1, 8])
    din("b_l0f_b1T", [D, 8])
    din("b_l0f_b2", [8, D])
    din("bvbc", [128, D])
    din("b_l0f_b2bc", [8 * 128, D])
    out_y = nc.dram_tensor("y0", [S, D], F32, kind="ExternalOutput")
    dram["y0"] = out_y

    if DBG:
        dram["dbg_sm"] = nc.dram_tensor("dbg_sm", [128, 512], F32, kind="ExternalOutput")
        dram["dbg_xnT"] = nc.dram_tensor("dbg_xnT", [128, 16384], BF16, kind="ExternalOutput")
        dram["dbg_k"] = nc.dram_tensor("dbg_k", [S, D], BF16, kind="ExternalOutput")
        dram["dbg_v"] = nc.dram_tensor("dbg_v", [S, D], BF16, kind="ExternalOutput")
        dram["dbg_qT"] = nc.dram_tensor("dbg_qT", [D, S], BF16, kind="ExternalOutput")
        dram["dbg_e"] = nc.dram_tensor("dbg_e", [S, S], F32, kind="ExternalOutput")
        dram["dbg_den"] = nc.dram_tensor("dbg_den", [S, S], F32, kind="ExternalOutput")

    dint("loc_k", [S, D], BF16)
    dint("loc_yT", [D, S], BF16)
    dint("loc_kT", [D, S], BF16)
    dint("loc_qTl", [D, S], BF16)
    dint("loc_kTl", [D, S], BF16)
    dint("dq_gateT", [D, 2816], BF16)
    dint("dq_upT", [D, 2816], BF16)
    dint("dq_downT", [2816, D], BF16)
    dint("loc_xqT", [D, S], BF16)
    dint("loc_hqT", [2816, S], BF16)
    dint("loc_l2selT", [8, 2048], BF16)
    dint("loc_u", [S, D], F32)
    dint("loc_xcT", [2048, S], BF16)
    dint("loc_zT", [2048, S], BF16)
    dint("loc_deltaT", [2048, S], BF16)
    dint("loc_bc", [32, S], F32)
    dint("loc_v", [S, D], BF16)
    dint("loc_qT", [D, S], BF16)
    dint("cc_e_in", [S, S], F32)
    dint("cc_den", [S, S], F32)

    es = contextlib.ExitStack()
    with es:
        t = {}
        t["xres"] = es.enter_context(nc.sbuf_tensor("xres", [128, 16384], F32))
        t["A1"] = es.enter_context(nc.sbuf_tensor("A1", [128, 16384], BF16))
        t["A2"] = es.enter_context(nc.sbuf_tensor("A2", [128, 16384], BF16))
        t["A3"] = es.enter_context(nc.sbuf_tensor("A3", [128, 16384], BF16))
        t["A4"] = es.enter_context(nc.sbuf_tensor("A4", [128, 8192], BF16))
        t["A5"] = es.enter_context(nc.sbuf_tensor("A5", [128, 3072], F32))
        t["A6"] = es.enter_context(nc.sbuf_tensor("A6", [128, 1024], F32))
        t["SM"] = es.enter_context(nc.sbuf_tensor("SM", [128, 512], F32))
        t["identb"] = es.enter_context(nc.sbuf_tensor("identb", [128, 128], BF16))
        t["mtri"] = es.enter_context(nc.sbuf_tensor("mtri", [128, 512], F32))
        t["ctri"] = es.enter_context(nc.sbuf_tensor("ctri", [128, 128], F32))
        t["cmask"] = es.enter_context(nc.sbuf_tensor("cmask", [128, 4096], BF16))
        t["l2sel"] = es.enter_context(nc.sbuf_tensor("l2sel", [128, 128], BF16))
        t["l3w"] = es.enter_context(nc.sbuf_tensor("l3w", [128, 144], F32))
        t["identf32"] = es.enter_context(nc.sbuf_tensor("identf32", [128, 128], F32))
        for i in range(6):
            t[f"pb{i}"] = es.enter_context(nc.psum_tensor(f"pb{i}", [128, 512], F32))
        t["pt0"] = es.enter_context(nc.psum_tensor("pt0", [128, 512], BF16))
        t["pt1"] = es.enter_context(nc.psum_tensor("pt1", [128, 512], BF16))

        sems = {}
        for name in ("dS", "dG", "te", "ve", "se", "ge", "cc", "dK"):
            sems[name] = es.enter_context(nc.semaphore(name))

        t.update(dram)
        blk = es.enter_context(nc.Block())
        k = Ctx(nc, blk, sems, t)

        @blk.gpsimd
        def _(g):
            k.inc(g.dma_start(t["identb"][:], dram["identf"][:]), "dG", 16)

        @blk.sync
        def _(sy):
            k.inc(sy.dma_start(t["identf32"][:], dram["identf"][:]), "dS", 16)
            k.inc(sy.dma_start(t["mtri"][:], dram["mtrif"][:]), "dS", 16)
            k.inc(sy.dma_start(t["ctri"][:], dram["ctrif"][:]), "dS", 16)
            k.inc(sy.dma_start(t["xres"][:].rearrange("p (t d) -> p t d", d=1024),
                               dram["x0"].ap().rearrange("(t p) d -> p t d", p=128)), "dS", 16)

        if ONLY is not None:
            {"l0_ffn": lambda: emit_l0_ffn(k),
             "l0_attn_b": lambda: emit_l0_attn_b(k),
             "l1_ffn": lambda: emit_l1_ffn(k, True),
             "l1_attn_b": lambda: emit_l1_attn_b(k),
             "l2_ffn": lambda: emit_l2_ffn(k, True),
             "l2_attn_b": lambda: emit_l2_attn_b(k),
             "l3_ffn": lambda: emit_l3_ffn(k),
             "l3_attn_b": lambda: emit_l3_attn_b(k),
             }[ONLY]()
            n_sub = 0
        if n_sub >= 1:
            emit_l0_attn_a(k)
        if n_sub >= 2:
            emit_l0_ffn(k)
        if n_sub >= 3:
            emit_l0_attn_b(k)
        if n_sub >= 4:
            emit_l0_ffn(k)
        if n_sub >= 5:
            emit_mha_T(k, dict(wq="w_l1aa_q", wk="w_l1aa_k", wv="w_l1aa_v", wo="w_l1aa_o",
                               rope="il", rt="rope_il", scale=0.125, shiftC=16.0,
                               kind="causal", cm="cm_causal",
                               cm_pats={0: 0, 128: 1, 256: 2, 384: 3}))
        if n_sub >= 6:
            emit_l1_ffn(k, first_call=True)
        if n_sub >= 7:
            emit_l1_attn_b(k)
        if n_sub >= 8:
            emit_l1_ffn(k, first_call=False)
        if n_sub >= 9:
            emit_mha_T(k, dict(wq="w_l2aa_q", wk="w_l2aa_k", wv="w_l2aa_v", wo="ident1k",
                               rope=None, scale=0.125, shiftC=16.0, kind="full"))
        if n_sub >= 10:
            emit_l2_ffn(k, first_call=True)
        if n_sub >= 11:
            emit_l2_attn_b(k)
        if n_sub >= 12:
            emit_l2_ffn(k, first_call=False)
        if n_sub >= 13:
            emit_mha_T(k, dict(wq="w_l3aa_q", wk="w_l3aa_k", wv="w_l3aa_v", wo="w_l3aa_o",
                               rope=None, scale=0.125, shiftC=16.0, kind="window",
                               cm="cm_win",
                               cm_pats={-256: 0, -128: 1, 0: 2, 128: 3, 256: 4, 384: 5, 512: 6, 640: 7}))
        if n_sub >= 14:
            emit_l3_ffn(k)
        if n_sub >= 15:
            emit_l3_attn_b(k)
        if n_sub >= 16:
            emit_l3_ffn(k)

        if DBG:
            @blk.gpsimd
            def _(g):
                k.w(g, "ve", k.c["ve"])
                k.w(g, "se", k.c["se"])
                k.w(g, "te", k.c["te"])
                k.inc(g.dma_start(dram["dbg_xnT"][:, 0:8192], t["A4"][:, 0:8192]), "dG", 16)
                k.inc(g.dma_start(dram["dbg_sm"][:, :], t["SM"][:, 0:512]), "dG", 16)
                k.inc(g.dma_start(dram["dbg_k"][:, :], dram["loc_k"][:, :]), "dG", 16)
                k.inc(g.dma_start(dram["dbg_v"][:, :], dram["loc_v"][:, :]), "dG", 16)
                k.inc(g.dma_start(dram["dbg_qT"][:, :], dram["loc_qT"][:, :]), "dG", 16)
                k.inc(g.dma_start(dram["dbg_e"][:, :], dram["cc_e_in"][:, :]), "dG", 16)
                k.inc(g.dma_start(dram["dbg_den"][:, :], dram["cc_den"][:, :]), "dG", 16)
                g.wait_ge(sems["dG"], k.c["dG"])

        @blk.sync
        def _(sy):
            k.w(sy, "ve", k.c["ve"])
            k.w(sy, "se", k.c["se"])
            k.inc(sy.dma_start(dram["y0"].ap().rearrange("(t p) d -> p t d", p=128),
                               t["xres"][:].rearrange("p (t d) -> p t d", d=1024)), "dS", 16)
            sy.wait_ge(sems["dS"], k.c["dS"])

    return nc


N_SUB = N_SUB_TOTAL


def kernel(**inputs):
    x = np.asarray(inputs["L0_attn_a_x"], dtype=np.float32)
    w = _prep_weights(inputs)
    consts = _prep_consts()
    nc = build(min(N_SUB, N_SUB_TOTAL))
    in_maps = []
    for c in range(NCORE):
        m = {"x0": np.ascontiguousarray(x[c]), "identf": consts["ident"],
             "mtrif": consts["mtri"], "ctrif": consts["ctri"],
             "rope_neox": consts["rope_neox"], "rope_il": consts["rope_il"],
             "cm_causal": consts["cm_causal"], "pe_T": consts["pe_T"],
             "cm_win": consts["cm_win"]}
        m.update(w)
        m.update(_prep_core_consts(c))
        in_maps.append(m)
    res = run_bass_kernel_spmd(nc, in_maps, core_ids=list(range(NCORE)))
    return np.stack([res.results[c]["y0"] for c in range(NCORE)], axis=0)
